# version 12
# speedup vs baseline: 1.5453x; 1.0120x over previous
import numpy as np
import ml_dtypes
from contextlib import ExitStack
import concourse.bass as bass
import concourse.mybir as mybir
from concourse.bass_utils import run_bass_kernel_spmd
from concourse.alu_op_type import AluOpType as OP

F32 = mybir.dt.float32
BF16 = mybir.dt.bfloat16
I32 = mybir.dt.int32
AF = mybir.ActivationFunctionType
AX = mybir.AxisListType

D = 1024
T = 2048
NSEQ = 2
DC = D // 128
TT = T // 128
NH = 16
HD = 64
WIN = 128
SLAB = 512
NSLAB = T // SLAB

ENGINES = ("sync", "tensor", "vector", "scalar", "gpsimd")

PHASES = dict(l0a=True, l0f=True, l1=True, moe=True)
DBG = []



class Sem:
    def __init__(self, h, name):
        self.h = h
        self.name = name
        self.val = 0


class Prog:
    def __init__(self, nc, ctx):
        self.nc = nc
        self.ctx = ctx
        self.q = {e: [] for e in ENGINES}
        self.bar_sem = None
        self.marks = []

    def mark(self, name):
        self.marks.append((name, {e: sum(1 for it in self.q[e] if it[0] == "emit")
                                  for e in ENGINES}))

    def sem(self, name):
        return Sem(self.ctx.enter_context(self.nc.semaphore(name)), name)

    def S(self, role):
        if not hasattr(self, "_pool"):
            self._pool = {}
        if role not in self._pool:
            self._pool[role] = self.sem(f"sp_{role}")
        return self._pool[role]

    def wait(self, eng, sem, val=None):
        v = sem.val if val is None else val
        if v > 0:
            self.q[eng].append(("wait", sem.h, v))

    def emit(self, eng, fn, inc=None, incn=1):
        if inc is not None:
            inc.val += incn
        self.q[eng].append(("emit", fn, inc, incn))

    def barrier(self):
        if self.bar_sem is None:
            self.bar_sem = self.sem("s_bar")
        s = self.bar_sem
        for e in ENGINES:
            self.emit(e, lambda eng: eng.drain(), inc=s, incn=1)
        tgt = s.val
        for e in ENGINES:
            self.wait(e, s, tgt)
        if getattr(self, "mark_tile", None) is not None:
            mt = self.mark_tile
            self.emit("scalar", lambda sc: sc.activation(
                out=mt[0:1, 0:1], in_=mt[0:1, 0:1], func=AF.Sign))

    def replay(self, block):
        def mk(engname):
            def body(eng):
                for item in self.q[engname]:
                    if item[0] == "wait":
                        _, h, v = item
                        eng.wait_ge(h, v)
                    else:
                        _, fn, inc, incn = item
                        instr = fn(eng)
                        if inc is not None:
                            if instr is None:
                                instr = eng.drain()
                            instr.then_inc(inc.h, incn)
            return body

        for engname in ENGINES:
            getattr(block, engname)(mk(engname))


def chunk_list(e, nc, name, cols, dtype, n=DC):
    return [e(nc.sbuf_tensor(f"{name}{c}", [128, cols], dtype)) for c in range(n)]


class Ctx:
    pass


def dbg_dump_chunks(g, name, tiles, cols, dtype=F32):
    if name not in DBG:
        return
    nc, p = g.nc, g.p
    dram = nc.dram_tensor(f"dbg_{name}", [len(tiles), 128, cols], dtype,
                          kind="ExternalOutput").ap()
    p.barrier()
    s = p.S("dbg")
    for i, t in enumerate(tiles):
        p.emit("gpsimd", lambda gp, i=i, t=t: gp.dma_start(
            out=dram[i], in_=t[:, 0:cols]), inc=s, incn=16)
    p.wait("gpsimd", s)
    p.barrier()



def build_program():
    nc = bass.Bass("TRN2", target_bir_lowering=False)
    g = Ctx()
    g.nc = nc

    g.x_in = nc.dram_tensor("x", [NSEQ, T, D], F32, kind="ExternalInput").ap()
    g.out_hbm = nc.dram_tensor("out", [NSEQ, T, D], F32, kind="ExternalOutput").ap()

    def inp(name, shape, dt=F32):
        return nc.dram_tensor(name, shape, dt, kind="ExternalInput").ap()

    g.wq = inp("wq", [D, D]); g.wk = inp("wk", [D, D])
    g.wv = inp("wv", [D, D]); g.wo = inp("wo", [D, D])
    g.bq_t = inp("bq_t", [128, DC]); g.bk_t = inp("bk_t", [128, DC])
    g.bv_t = inp("bv_t", [128, DC]); g.bo_t = inp("bo_t", [128, DC])
    g.bv_row = inp("bv_row", [1, D])
    g.ident_f = inp("ident_f", [128, 128])
    g.ident_b = inp("ident_b", [128, 128], BF16)
    g.ones_col_f = inp("ones_col_f", [128, 1])
    g.ones_col_b = inp("ones_col_b", [128, 1], BF16)
    g.ones_row_f = inp("ones_row_f", [1, 128])
    g.ones_row_b = inp("ones_row_b", [1, 128], BF16)
    g.maskL_in = inp("maskL", [128, 128], BF16)
    g.maskR_in = inp("maskR", [128, 128], BF16)

    g.stg_wq = nc.dram_tensor("stg_wq", [DC, 128, D], BF16, kind="Internal").ap()
    g.stg_wk = nc.dram_tensor("stg_wk", [DC, 128, D], BF16, kind="Internal").ap()
    g.stg_wv = nc.dram_tensor("stg_wv", [DC, 128, D], BF16, kind="Internal").ap()
    g.stg_wo = nc.dram_tensor("stg_wo", [DC, 128, D], BF16, kind="Internal").ap()
    g.wg_i = inp("wg_i", [3 * D, D], I32)
    g.wu_i = inp("wu_i", [3 * D, D], I32)
    g.wd_i = inp("wd_i", [D, 3 * D], I32)
    g.stg_wgT = nc.dram_tensor("stg_wgT", [DC, 128, 3 * D], BF16, kind="Internal").ap()
    g.stg_wuT = nc.dram_tensor("stg_wuT", [DC, 128, 3 * D], BF16, kind="Internal").ap()
    g.stg_wdT = nc.dram_tensor("stg_wdT", [3 * DC, 128, D], BF16, kind="Internal").ap()
    g.win_f = inp("win_f", [D, 2 * D])
    g.wout_f = inp("wout_f", [2 * D, D])
    g.A_t = inp("A_t", [128, 16 * 256])
    g.Bm_t = inp("Bm_t", [128, 256])
    g.Cm_t = inp("Cm_t", [128, 256])
    g.c4_t = inp("c4_t", [128, 64])
    g.bin_t = inp("bin_t", [128, 16])
    g.cb_t = inp("cb_t", [128, 16])
    g.bout_t = inp("bout_t", [128, DC])
    g.nw_t = inp("nw_t", [128, DC])
    g.stg_win = nc.dram_tensor("stg_win", [DC, 128, 2 * D], BF16, kind="Internal").ap()
    g.stg_wout = nc.dram_tensor("stg_wout", [16, 128, D], BF16, kind="Internal").ap()
    g.stg_Vt = nc.dram_tensor("stg_Vt", [128, 16 * 256], BF16, kind="Internal").ap()
    g.stg_Wt = nc.dram_tensor("stg_Wt", [128, 16 * 256], BF16, kind="Internal").ap()
    g.sW1_f = inp("sW1_f", [D, 512])
    g.sW2_f = inp("sW2_f", [512, D])
    g.rW1_f = inp("rW1_f", [8 * D, 512])
    g.rW2_f = inp("rW2_f", [8 * 512, D])
    g.rw_t = inp("rw_t", [128, 64])
    g.ebias_t = inp("ebias_t", [8, 1])
    g.sb1_t = inp("sb1_t", [128, 4])
    g.rb1_t = inp("rb1_t", [128, 32])
    g.rb2aug = inp("rb2aug", [9, D])
    g.stg_sW1 = nc.dram_tensor("stg_sW1", [DC, 128, 512], BF16, kind="Internal").ap()
    g.stg_sW2 = nc.dram_tensor("stg_sW2", [4, 128, D], BF16, kind="Internal").ap()
    g.stg_rW1 = nc.dram_tensor("stg_rW1", [64, 128, 512], BF16, kind="Internal").ap()
    g.stg_rW2 = nc.dram_tensor("stg_rW2", [32, 128, D], BF16, kind="Internal").ap()

    with ExitStack() as ctx:
        e = ctx.enter_context
        p = Prog(nc, ctx)
        g.p = p
        g.e = e

        g.idf = e(nc.sbuf_tensor("idf", [128, 128], F32))
        p.mark_tile = e(nc.sbuf_tensor("marker", [1, 1], F32))
        g.idb = e(nc.sbuf_tensor("idb", [128, 128], BF16))
        g.ocf = e(nc.sbuf_tensor("ocf", [128, 1], F32))
        g.ocb = e(nc.sbuf_tensor("ocb", [128, 1], BF16))
        g.orf = e(nc.sbuf_tensor("orf", [1, 128], F32))
        g.orb = e(nc.sbuf_tensor("orb", [1, 128], BF16))
        g.maskL = e(nc.sbuf_tensor("sb_maskL", [128, 128], BF16))
        g.maskR = e(nc.sbuf_tensor("sb_maskR", [128, 128], BF16))
        g.bias_q = e(nc.sbuf_tensor("bias_q", [128, DC], F32))
        g.bias_k = e(nc.sbuf_tensor("bias_k", [128, DC], F32))
        g.bias_v = e(nc.sbuf_tensor("bias_v", [128, DC], F32))
        g.bias_o = e(nc.sbuf_tensor("bias_o", [128, DC], F32))
        g.bv_row_b = e(nc.sbuf_tensor("bv_row_b", [1, D], BF16))
        g.kv = e(nc.sbuf_tensor("tkv", [128, 16 * 16], F32))
        g.AQT = e(nc.sbuf_tensor("tAQT", [128, 16 * 256], BF16))
        g.ceff = e(nc.sbuf_tensor("tceff", [128, 16], F32))
        g.corr3 = e(nc.sbuf_tensor("tcorr3", [128, 48], F32))
        g.cl = e(nc.sbuf_tensor("tcl", [128, 64], F32))
        g.bias_bout = e(nc.sbuf_tensor("bias_bout", [128, DC], F32))
        g.nw_sb = e(nc.sbuf_tensor("nw_sb", [128, DC], F32))
        g.rw_sb = e(nc.sbuf_tensor("rw_sb", [128, 64], BF16))
        g.ebias_sb = e(nc.sbuf_tensor("ebias_sb", [8, 1], F32))
        g.bias_sb1 = e(nc.sbuf_tensor("bias_sb1", [128, 4], F32))
        g.bias_rb1 = e(nc.sbuf_tensor("bias_rb1", [128, 32], F32))
        g.rb2aug_sb = e(nc.sbuf_tensor("rb2aug_sb", [9, D], BF16))
        g.czero = e(nc.sbuf_tensor("czero", [128, 1], F32))
        g.cn4 = e(nc.sbuf_tensor("cn4", [128, 1], F32))
        nc.const_aps.register(F32, 0.0, g.czero[:, 0:1])
        nc.const_aps.register(F32, -4.0, g.cn4[:, 0:1])

        s_c = p.sem("s_const")
        for dst, src in [(g.idf, g.ident_f), (g.idb, g.ident_b),
                         (g.ocf, g.ones_col_f), (g.ocb, g.ones_col_b),
                         (g.orf, g.ones_row_f), (g.orb, g.ones_row_b),
                         (g.maskL, g.maskL_in), (g.maskR, g.maskR_in),
                         (g.bias_q, g.bq_t), (g.bias_k, g.bk_t),
                         (g.bias_v, g.bv_t), (g.bias_o, g.bo_t)]:
            p.emit("sync", lambda s, d=dst, x=src: s.dma_start(out=d[:], in_=x),
                   inc=s_c, incn=16)
        bv_row_f = e(nc.sbuf_tensor("bv_row_f", [1, D], F32))
        p.emit("sync", lambda s: s.dma_start(out=bv_row_f[:], in_=g.bv_row),
               inc=s_c, incn=16)
        p.emit("sync", lambda s: s.dma_start(out=g.nw_sb[:], in_=g.nw_t),
               inc=s_c, incn=16)
        for eng in ("tensor", "vector", "scalar"):
            p.wait(eng, s_c)
        p.emit("vector", lambda v: v.memset(g.czero[:], 0.0))
        p.emit("vector", lambda v: v.memset(g.cn4[:], -4.0))
        p.emit("vector", lambda v: v.tensor_copy(out=g.bv_row_b[:], in_=bv_row_f[:]))
        p.emit("vector", lambda v: v.tensor_scalar(
            out=g.bias_q[:], in0=g.bias_q[:], scalar1=0.125, scalar2=None,
            op0=OP.mult))

        p.mark("start")
        with ExitStack() as pctx:
            pe = pctx.enter_context
            g.pwi = [pe(nc.sbuf_tensor(f"prep_i{j}", [128, 3 * D], I32)) for j in range(4)]
            g.pwb = [pe(nc.sbuf_tensor(f"prep_o{j}", [128, 3 * D], BF16)) for j in range(4)]
            g.ptb = [pe(nc.sbuf_tensor(f"prep_t{j}", [128, 128], BF16)) for j in range(4)]
            g.pps = [pe(nc.psum_tensor(f"prep_ps{j}", [128, 128], BF16)) for j in range(2)]
            if PHASES["l1"]:
                build_prep_l1_tables(g, pctx.enter_context)
            build_prep_staging(g)
            if PHASES["l1"]:
                build_prep_l1_ship(g)
            p.barrier()
        g.stream = chunk_list(e, nc, "stream", T, F32)
        g.xln = chunk_list(e, nc, "xln", T, BF16)
        g.rstd_bc = e(nc.sbuf_tensor("rstd_bc", [128, T], BF16))
        g.mrstd_bc = e(nc.sbuf_tensor("mrstd_bc", [128, T], BF16))

        g.s_out = p.sem("s_out")
        for seq in range(NSEQ):
            p.mark(f"prep_done{seq}")
            build_load_stream(g, seq)
            p.mark(f"load{seq}")
            p.barrier()
            if PHASES["l0a"]:
                build_ln(g, seq, tag=f"ln0a{seq}")
                p.mark(f"ln0a{seq}")
                p.barrier()
                if seq == 0:
                    dbg_dump_chunks(g, "xln0", g.xln, T, BF16)
                build_l0a(g, seq)
                p.mark(f"l0a{seq}")
                p.barrier()
            if PHASES["l0f"]:
                build_ln(g, seq, tag=f"ln0f{seq}")
                p.mark(f"ln0f{seq}")
                p.barrier()
                if seq == 0:
                    dbg_dump_chunks(g, "xln1", g.xln, T, BF16)
                build_l0f(g, seq)
                p.mark(f"l0f{seq}")
                p.barrier()
                if seq == 0:
                    dbg_dump_chunks(g, "post_ffn", g.stream, T, F32)
            build_store_stream(g, seq)
            p.barrier()

        p.wait("gpsimd", g.s_out, NSEQ * TT * 16)

        block = e(nc.Block())
        p.replay(block)

    global LAST_MARKS
    LAST_MARKS = p.marks
    return nc


LAST_MARKS = None


def build_l0f(g, seq):
    nc, p, e = g.nc, g.p, g.e
    HHALF = 1024
    NHC = HHALF // 128
    NTT = 4
    with ExitStack() as ctx:
        ee = ctx.enter_context
        wgT = chunk_list(ee, nc, f"f{seq}wg", HHALF, BF16)
        wuT = chunk_list(ee, nc, f"f{seq}wu", HHALF, BF16)
        wdT = [ee(nc.sbuf_tensor(f"f{seq}wd{i}", [128, D], BF16)) for i in range(NHC)]
        hsb = [ee(nc.sbuf_tensor(f"f{seq}h{i}", [128, 512], BF16)) for i in range(NHC)]
        sil = [ee(nc.sbuf_tensor(f"f{seq}sl{j}", [128, 512], BF16)) for j in range(2)]
        ps_g = ee(nc.psum_tensor(f"f{seq}psg", [128, 512], F32))
        ps_u = ee(nc.psum_tensor(f"f{seq}psu", [128, 512], F32))
        ps_dn = [ee(nc.psum_tensor(f"f{seq}psd{i}", [128, 512], F32)) for i in range(4)]
        s_wld = p.S("wld0")
        s_gu = p.S("gu")
        s_sil = p.S("sil")
        s_hm = p.S("hm")
        s_dn = p.S("dn")
        s_dnev = p.S("dnev")

        hm_rec = []
        dn_bank_free = [0, 0, 0, 0]
        for half in range(3):
            if half >= 1:
                p.wait("sync", s_dn, s_dn.val)
            wld_base = s_wld.val
            for dc in range(DC):
                p.emit("sync", lambda s, dc=dc, half=half: s.dma_start(
                    out=wgT[dc][:], in_=g.stg_wgT[dc][:, half * HHALF:(half + 1) * HHALF]),
                    inc=s_wld, incn=16)
                p.emit("sync", lambda s, dc=dc, half=half: s.dma_start(
                    out=wuT[dc][:], in_=g.stg_wuT[dc][:, half * HHALF:(half + 1) * HHALF]),
                    inc=s_wld, incn=16)
            for i in range(NHC):
                p.emit("sync", lambda s, i=i, half=half: s.dma_start(
                    out=wdT[i][:], in_=g.stg_wdT[half * NHC + i]),
                    inc=s_wld, incn=16)
            wld_val = s_wld.val
            p.wait("tensor", s_wld, wld_val)
            if seq == 0 and half == 0:
                for eng in ("gpsimd",):
                    p.wait(eng, s_wld, wld_val)
                dbg_dump_chunks(g, "wgT0", wgT, HHALF, BF16)
                dbg_dump_chunks(g, "wdT0", wdT, D, BF16)

            for tt in range(NTT):
                lo = tt * 512
                hm_base = s_hm.val
                for hc in range(NHC):
                    if not (half == 0 and tt == 0 and hc == 0):
                        p.wait("tensor", s_sil, s_sil.val)
                        p.wait("tensor", s_hm, s_hm.val)
                    for dc in range(DC):
                        p.emit("tensor", lambda t, dc=dc, hc=hc, lo=lo: t.matmul(
                            out=ps_g[:], lhsT=wgT[dc][:, hc * 128:(hc + 1) * 128],
                            rhs=g.xln[dc][:, lo:lo + 512],
                            start=(dc == 0), stop=(dc == DC - 1)),
                            inc=(s_gu if dc == DC - 1 else None))
                    guv_g = s_gu.val
                    for dc in range(DC):
                        p.emit("tensor", lambda t, dc=dc, hc=hc, lo=lo: t.matmul(
                            out=ps_u[:], lhsT=wuT[dc][:, hc * 128:(hc + 1) * 128],
                            rhs=g.xln[dc][:, lo:lo + 512],
                            start=(dc == 0), stop=(dc == DC - 1)),
                            inc=(s_gu if dc == DC - 1 else None))
                    guv_u = s_gu.val
                    p.wait("scalar", s_gu, guv_g)
                    nop_ = half * NTT * NHC + tt * NHC + hc
                    sj = nop_ % 2
                    if nop_ >= 2:
                        p.wait("scalar", s_hm, hm_rec[nop_ - 2])
                    p.emit("scalar", lambda sc, sj=sj: sc.activation(
                        out=sil[sj][:], in_=ps_g[:], func=AF.Silu), inc=s_sil)
                    p.wait("vector", s_gu, guv_u)
                    p.wait("vector", s_sil, s_sil.val)
                    if half > 0 or tt > 0:
                        p.wait("vector", s_dn, dn_prev_tt)
                    p.emit("vector", lambda v, sj=sj, hc=hc: v.tensor_tensor(
                        out=hsb[hc][:], in0=sil[sj][:], in1=ps_u[:], op=OP.mult),
                        inc=s_hm)
                    hm_rec.append(s_hm.val)
                hm_val = s_hm.val
                p.wait("tensor", s_hm, hm_val)
                for dh in range(2):
                    for dc4 in range(4):
                        dcg = dh * 4 + dc4
                        if dn_bank_free[dc4] > 0:
                            p.wait("tensor", s_dnev, dn_bank_free[dc4])
                        for hc in range(NHC):
                            p.emit("tensor",
                                   lambda t, hc=hc, dcg=dcg, dc4=dc4: t.matmul(
                                       out=ps_dn[dc4][:], lhsT=wdT[hc][:, dcg * 128:(dcg + 1) * 128],
                                       rhs=hsb[hc][:],
                                       start=(hc == 0), stop=(hc == NHC - 1)),
                                   inc=(s_dn if hc == NHC - 1 else None))
                        dnv = s_dn.val
                        p.wait("vector", s_dn, dnv)
                        p.emit("vector", lambda v, dc4=dc4, dcg=dcg, lo=lo: v.tensor_tensor(
                            out=g.stream[dcg][:, lo:lo + 512], in0=ps_dn[dc4][:],
                            in1=g.stream[dcg][:, lo:lo + 512], op=OP.add), inc=s_dnev)
                        dn_bank_free[dc4] = s_dnev.val
                dn_prev_tt = s_dn.val
                if seq == 0 and half == 0 and tt == 0:
                    dbg_dump_chunks(g, "hsb0", hsb, 512, BF16)
        p.barrier()


def build_prep_l1_tables(g, pe):
    nc, p = g.nc, g.p
    s_tl = p.S("tl")
    A1 = pe(nc.sbuf_tensor("plA1", [128, 4096], BF16))
    A1T = pe(nc.sbuf_tensor("plA1T", [128, 4096], BF16))
    A2 = pe(nc.sbuf_tensor("plA2", [128, 4096], BF16))
    A2T = pe(nc.sbuf_tensor("plA2T", [128, 4096], BF16))
    PRD = pe(nc.sbuf_tensor("plPRD", [128, 4096], BF16))
    Vrev = pe(nc.sbuf_tensor("plVrev", [128, 16 * 256], BF16))
    VtS = pe(nc.sbuf_tensor("plVtS", [128, 16 * 256], BF16))
    WtS = pe(nc.sbuf_tensor("plWtS", [128, 16 * 256], BF16))
    Bmb = pe(nc.sbuf_tensor("plBmb", [128, 256], BF16))
    Cmb = pe(nc.sbuf_tensor("plCmb", [128, 256], BF16))
    smallf = pe(nc.sbuf_tensor("plsm", [128, 544], F32))
    b_tl = s_tl.val
    s_tl2 = p.S("tl2")
    with nc.sbuf_tensor("plAf", [128, 1024], F32) as Af:
        for qt in range(4):
            if qt > 0:
                p.wait("sync", s_tl2, s_tl2.val)
            p.emit("sync", lambda s, qt=qt, Af=Af: s.dma_start(
                out=Af[:], in_=g.A_t[:, qt * 1024:(qt + 1) * 1024]),
                inc=s_tl, incn=16)
            p.wait("vector", s_tl, b_tl + (qt + 1) * 16)
            p.emit("vector", lambda v, qt=qt, Af=Af: v.tensor_copy(
                out=A1[:, qt * 1024:(qt + 1) * 1024], in_=Af[:]),
                inc=s_tl2)
    b_tl = s_tl.val
    p.emit("sync", lambda s: s.dma_start(out=smallf[:, 0:256], in_=g.Bm_t),
           inc=s_tl, incn=16)
    p.emit("sync", lambda s: s.dma_start(out=smallf[:, 256:512], in_=g.Cm_t),
           inc=s_tl, incn=16)
    p.emit("sync", lambda s: s.dma_start(out=g.cl[:, 0:64], in_=g.c4_t),
           inc=s_tl, incn=16)
    p.emit("sync", lambda s: s.dma_start(out=smallf[:, 512:528], in_=g.bin_t),
           inc=s_tl, incn=16)
    p.emit("sync", lambda s: s.dma_start(out=smallf[:, 528:544], in_=g.cb_t),
           inc=s_tl, incn=16)
    p.emit("sync", lambda s: s.dma_start(out=g.bias_bout[:], in_=g.bout_t),
           inc=s_tl, incn=16)
    p.wait("vector", s_tl, b_tl + 16 * 6)

    def ve(fn):
        p.emit("vector", fn)

    ve(lambda v: v.tensor_copy(out=Bmb[:], in_=smallf[:, 0:256]))
    ve(lambda v: v.tensor_copy(out=Cmb[:], in_=smallf[:, 256:512]))
    cl4 = g.cl[:].rearrange("p (d l) -> p d l", l=4)
    ve(lambda v: v.tensor_reduce(
        out=g.ceff[:].rearrange("p (d o) -> p d o", o=1), in_=cl4,
        axis=AX.X, op=OP.add))
    ve(lambda v: v.tensor_tensor(
        out=g.ceff[:], in0=g.ceff[:], in1=smallf[:, 512:528], op=OP.mult))
    ve(lambda v: v.tensor_tensor(
        out=g.ceff[:], in0=g.ceff[:], in1=smallf[:, 528:544], op=OP.add))
    c3 = g.corr3[:].rearrange("p (d t) -> p d t", t=3)
    for t in range(3):
        ve(lambda v, t=t: v.tensor_reduce(
            out=c3[:, :, t:t + 1], in_=cl4[:, :, t + 1:4],
            axis=AX.X, op=OP.add))
    ve(lambda v: v.tensor_tensor(
        out=c3, in0=c3,
        in1=smallf[:, 512:528].rearrange("p (d o) -> p d o", o=1)
        .broadcast_to([128, 16, 3]), op=OP.mult))

    def a_rs(buf):
        return buf[:].rearrange("p (d r s) -> p d r s", r=16, s=16)

    def tree4(pv):
        for lev in (8, 4, 2, 1):
            ve(lambda v, lev=lev, pv=pv: v.tensor_tensor(
                out=pv[:, :, :, 0:lev], in0=pv[:, :, :, 0:lev],
                in1=pv[:, :, :, lev:2 * lev], op=OP.add))

    def transpose_A(dst, srcb):
        ve(lambda v: v.tensor_copy(
            out=a_rs(dst), in_=a_rs(srcb).transpose([0, 1, 3, 2])))

    def square_A(dst, srcN, srcT):
        for dc in range(16):
            pv = PRD[:].rearrange("p (r s gt) -> p r s gt", r=16, s=16)
            srn = a_rs(srcN)[:, dc].unsqueeze(2).broadcast_to([128, 16, 16, 16])
            srt = a_rs(srcT)[:, dc].unsqueeze(1).broadcast_to([128, 16, 16, 16])
            ve(lambda v, srn=srn, srt=srt, pv=pv: v.tensor_tensor(
                out=pv, in0=srn, in1=srt, op=OP.mult))
            tree4(pv)
            ve(lambda v, dc=dc, pv=pv: v.tensor_copy(
                out=a_rs(dst)[:, dc].unsqueeze(3), in_=pv[:, :, :, 0:1]))

    vr = Vrev[:].rearrange("p (d j s) -> p d j s", j=16, s=16)
    wt = WtS[:].rearrange("p (d i s) -> p d i s", i=16, s=16)
    ve(lambda v: v.tensor_copy(
        out=vr[:, :, 0, :], in_=Bmb[:].rearrange("p (d s) -> p d s", s=16)))
    pv0 = PRD[:].rearrange("p (d r st) -> p d r st", r=16, st=16)[:, 0:16]
    cb = Cmb[:].rearrange("p (d s) -> p d s", s=16).unsqueeze(2)\
        .broadcast_to([128, 16, 16, 16])
    ve(lambda v: v.tensor_tensor(out=pv0, in0=a_rs(A1), in1=cb, op=OP.mult))
    tree4(pv0)
    ve(lambda v: v.tensor_copy(
        out=wt[:, :, 0, :].unsqueeze(3), in_=pv0[:, :, :, 0:1]))

    transpose_A(A1T, A1)
    curN, curT, nxtN, nxtT = A1, A1T, A2, A2T
    m = 1
    while True:
        for dc in range(16):
            pj = PRD[:].rearrange("p (j s rt) -> p j s rt", s=16, rt=16)[:, 0:m]
            i0 = vr[:, dc, 0:m, :].unsqueeze(2).broadcast_to([128, m, 16, 16])
            i1 = a_rs(curT)[:, dc].unsqueeze(1).broadcast_to([128, m, 16, 16])
            ve(lambda v, i0=i0, i1=i1, pj=pj: v.tensor_tensor(
                out=pj, in0=i0, in1=i1, op=OP.mult))
            tree4(pj)
            ve(lambda v, dc=dc, m=m, pj=pj: v.tensor_copy(
                out=vr[:, dc, m:2 * m, :].unsqueeze(3), in_=pj[:, :, :, 0:1]))
            pj2 = PRD[:].rearrange("p (j r st) -> p j r st", r=16, st=16)[:, 0:m]
            k0 = wt[:, dc, 0:m, :].unsqueeze(2).broadcast_to([128, m, 16, 16])
            k1 = a_rs(curN)[:, dc].unsqueeze(1).broadcast_to([128, m, 16, 16])
            ve(lambda v, k0=k0, k1=k1, pj2=pj2: v.tensor_tensor(
                out=pj2, in0=k0, in1=k1, op=OP.mult))
            tree4(pj2)
            ve(lambda v, dc=dc, m=m, pj2=pj2: v.tensor_copy(
                out=wt[:, dc, m:2 * m, :].unsqueeze(3), in_=pj2[:, :, :, 0:1]))
        if m == 8:
            break
        square_A(nxtN, curN, curT)
        transpose_A(nxtT, nxtN)
        curN, curT, nxtN, nxtT = nxtN, nxtT, curN, curT
        m *= 2
    square_A(nxtN, curN, curT)
    ve(lambda v: v.tensor_copy(
        out=g.AQT[:].rearrange("p (d s r) -> p d s r", s=16, r=16),
        in_=a_rs(nxtN).transpose([0, 1, 3, 2])))
    kvv = g.kv[:].rearrange("p (d t) -> p d t", t=16)
    for dc in range(16):
        pk = PRD[:].rearrange("p (t st) -> p t st", st=16)[:, 0:16]
        i1 = Cmb[:].rearrange("p (d s) -> p d s", s=16)[:, dc].unsqueeze(1)\
            .broadcast_to([128, 16, 16])
        ve(lambda v, dc=dc, pk=pk, i1=i1: v.tensor_tensor(
            out=pk, in0=vr[:, dc], in1=i1, op=OP.mult))
        for lev in (8, 4, 2, 1):
            ve(lambda v, lev=lev, pk=pk: v.tensor_tensor(
                out=pk[:, :, 0:lev], in0=pk[:, :, 0:lev],
                in1=pk[:, :, lev:2 * lev], op=OP.add))
        ve(lambda v, dc=dc, pk=pk: v.tensor_copy(
            out=kvv[:, dc].unsqueeze(2), in_=pk[:, :, 0:1]))
    vt = VtS[:].rearrange("p (d s i) -> p d s i", s=16, i=16)
    vr_svi = vr.transpose([0, 1, 3, 2])
    for i in range(16):
        ve(lambda v, i=i: v.tensor_copy(
            out=vt[:, :, :, i:i + 1], in_=vr_svi[:, :, :, 15 - i:16 - i]))
    g._VtS = VtS
    g._WtS = WtS


def build_prep_staging(g):
    nc, p = g.nc, g.p
    s_ld = [p.S("ld0"), p.S("ld1"), p.S("ld2"), p.S("ld3")]
    s_cast = p.S("cast")
    s_tr = p.S("tr")
    s_ev = p.S("ev")
    s_st = p.S("st")
    s_tl = p.S("tl")
    b_ld = [s.val for s in s_ld]
    b_cast = s_cast.val
    b_tr = s_tr.val
    b_ev = s_ev.val
    b_st = s_st.val
    b_tl = s_tl.val

    with ExitStack() as c2:
        smf = c2.enter_context(nc.sbuf_tensor("pm_sm", [128, 64], F32))
        r9f = c2.enter_context(nc.sbuf_tensor("pm_r9", [9, D], F32))
        p.wait("sync", p.S("tl2"), p.S("tl2").val)
        p.emit("sync", lambda s: s.dma_start(out=smf[:], in_=g.rw_t), inc=s_tl, incn=16)
        p.emit("sync", lambda s: s.dma_start(out=r9f[:], in_=g.rb2aug), inc=s_tl, incn=16)
        p.emit("sync", lambda s: s.dma_start(out=g.ebias_sb[:], in_=g.ebias_t), inc=s_tl, incn=16)
        p.emit("sync", lambda s: s.dma_start(out=g.bias_sb1[:], in_=g.sb1_t), inc=s_tl, incn=16)
        p.emit("sync", lambda s: s.dma_start(out=g.bias_rb1[:], in_=g.rb1_t), inc=s_tl, incn=16)
        p.wait("scalar", s_tl, b_tl + 16 * 5)
        p.emit("scalar", lambda sc: sc.copy(out=g.rw_sb[:], in_=smf[:]))
        p.emit("scalar", lambda sc: sc.copy(out=g.rb2aug_sb[:], in_=r9f[:]))

        jobs = []
        for (w_in, w_stg, scl) in ((g.wq, g.stg_wq, 0.125), (g.wk, g.stg_wk, None),
                                   (g.wv, g.stg_wv, None), (g.wo, g.stg_wo, None)):
            for c in range(DC):
                jobs.append((w_in[c * 128:(c + 1) * 128, :], D, False, scl,
                             "direct", (w_stg, c)))
        for c in range(DC):
            jobs.append((g.win_f[c * 128:(c + 1) * 128, :], 2 * D, False, ("nw", c),
                         "direct", (g.stg_win, c)))
        for c in range(16):
            jobs.append((g.wout_f[c * 128:(c + 1) * 128, :], D, False, None,
                         "direct", (g.stg_wout, c)))
        for c in range(DC):
            jobs.append((g.sW1_f[c * 128:(c + 1) * 128, :], 512, False, None,
                         "direct", (g.stg_sW1, c)))
        for c in range(4):
            jobs.append((g.sW2_f[c * 128:(c + 1) * 128, :], D, False, None,
                         "direct", (g.stg_sW2, c)))
        for i in range(64):
            jobs.append((g.rW1_f[i * 128:(i + 1) * 128, :], 512, False, None,
                         "direct", (g.stg_rW1, i)))
        for i in range(32):
            jobs.append((g.rW2_f[i * 128:(i + 1) * 128, :], D, False, None,
                         "direct", (g.stg_rW2, i)))
        for (w_in, w_stg) in ((g.wg_i, g.stg_wgT), (g.wu_i, g.stg_wuT)):
            for hc in range(3 * DC):
                jobs.append((w_in[hc * 128:(hc + 1) * 128, :], D, True, 1.0 / 127.0,
                             "transpose", (w_stg, DC, hc)))
        for dc in range(DC):
            jobs.append((g.wd_i[dc * 128:(dc + 1) * 128, :], 3 * D, True, 1.0 / 127.0,
                         "transpose", (g.stg_wdT, 3 * DC, dc)))

        cast_after = []
        tr_after = []
        st_after = []
        nt = 0
        nst = 0
        for jn, (src_ap, ncols, is_int, scl, kind, args) in enumerate(jobs):
            j = jn % 4
            if jn >= 4:
                p.wait("sync", s_cast, b_cast + cast_after[jn - 4] + 1)
            inbuf = g.pwi[j][:, 0:ncols] if is_int else \
                g.pwi[j][:].bitcast(F32)[:, 0:ncols]
            p.emit("sync", lambda s, inbuf=inbuf, src_ap=src_ap: s.dma_start(
                out=inbuf, in_=src_ap), inc=s_ld[j], incn=16)
            p.wait("scalar", s_ld[j], b_ld[j] + (jn // 4 + 1) * 16)
            if jn >= 4:
                p.wait("scalar", s_st, b_st + st_after[jn - 4] * 16)
                if tr_after[jn - 4] > 0:
                    p.wait("scalar", s_tr, b_tr + tr_after[jn - 4])
            ob = g.pwb[j][:, 0:ncols]
            if isinstance(scl, tuple):
                cslice = g.nw_sb[:, scl[1]:scl[1] + 1]
                p.emit("scalar", lambda sc, inbuf=inbuf, ob=ob, cs=cslice:
                       sc.activation(out=ob, in_=inbuf, func=AF.Identity,
                                     scale=cs), inc=s_cast)
            elif scl is not None:
                p.emit("scalar", lambda sc, inbuf=inbuf, ob=ob, scl=scl: sc.mul(
                    out=ob, in_=inbuf, mul=scl), inc=s_cast)
            else:
                p.emit("scalar", lambda sc, inbuf=inbuf, ob=ob: sc.copy(
                    out=ob, in_=inbuf), inc=s_cast)
            castv = s_cast.val - b_cast - 1
            if kind == "direct":
                w_stg, oc = args
                p.wait("gpsimd", s_cast, b_cast + castv + 1)
                p.emit("gpsimd", lambda gp, w_stg=w_stg, oc=oc, j=j, ncols=ncols:
                       gp.dma_start(out=w_stg[oc], in_=g.pwb[j][:, 0:ncols]),
                       inc=s_st, incn=16)
                nst += 1
            else:
                w_stg, n_inner, rc = args
                for ic in range(n_inner):
                    bank = nt % 2
                    tbuf = nt % 4
                    p.wait("tensor", s_cast, b_cast + castv + 1)
                    if nt >= 2:
                        p.wait("tensor", s_ev, b_ev + nt - 1)
                    p.emit("tensor", lambda t, j=j, ic=ic, bank=bank: t.transpose(
                        out=g.pps[bank][:], in_=g.pwb[j][:, ic * 128:(ic + 1) * 128],
                        identity=g.idb[:]), inc=s_tr)
                    p.wait("scalar", s_tr, s_tr.val)
                    if nst >= 4:
                        p.wait("scalar", s_st, b_st + (nst - 3) * 16)
                    p.emit("scalar", lambda sc, bank=bank, tbuf=tbuf: sc.copy(
                        out=g.ptb[tbuf][:], in_=g.pps[bank][:]), inc=s_ev)
                    p.wait("gpsimd", s_ev, b_ev + nt + 1)
                    p.emit("gpsimd",
                           lambda gp, w_stg=w_stg, ic=ic, rc=rc, tbuf=tbuf:
                           gp.dma_start(out=w_stg[ic][:, rc * 128:(rc + 1) * 128],
                                        in_=g.ptb[tbuf][:]), inc=s_st, incn=16)
                    nt += 1
                    nst += 1
            cast_after.append(castv)
            tr_after.append(nt)
            st_after.append(nst)
    g.prep_done = (s_st, s_st.val)


def build_prep_l1_ship(g):
    nc, p = g.nc, g.p
    s_tf = p.S("tl2")
    p.emit("vector", lambda v: v.drain(), inc=s_tf)
    p.wait("gpsimd", s_tf, s_tf.val)
    p.emit("gpsimd", lambda gp: gp.dma_start(out=g.stg_Vt, in_=g._VtS[:]),
           inc=p.S("st"), incn=16)
    p.emit("gpsimd", lambda gp: gp.dma_start(out=g.stg_Wt, in_=g._WtS[:]),
           inc=p.S("st"), incn=16)
    p.wait("gpsimd", p.S("st"), p.S("st").val)


def build_load_stream(g, seq):
    nc, p, e = g.nc, g.p, g.e
    with ExitStack() as ctx:
        ee = ctx.enter_context
        tok = [ee(nc.sbuf_tensor(f"ld_tok{seq}_{j}", [128, D], F32)) for j in range(2)]
        ps = [ee(nc.psum_tensor(f"ld_ps{seq}_{j}", [128, 128], F32)) for j in range(2)]
        s_ld = [p.S("ld0"), p.S("ld1")]
        s_tr = p.S("tr")
        s_ev = p.S("ev")
        b_ld = [s_ld[0].val, s_ld[1].val]
        b_tr = s_tr.val
        b_ev = s_ev.val

        for i in range(TT):
            j = i % 2
            if i >= 2:
                p.wait("sync", s_tr, b_tr + (i - 1) * DC)
            p.emit("sync", lambda s, j=j, i=i: s.dma_start(
                out=tok[j][:], in_=g.x_in[seq, i * 128:(i + 1) * 128, :]),
                inc=s_ld[j], incn=16)
            for c in range(DC):
                n = i * DC + c
                bank = n % 2
                p.wait("tensor", s_ld[j], b_ld[j] + (i // 2 + 1) * 16)
                if n >= 2:
                    p.wait("tensor", s_ev, b_ev + n - 1)
                p.emit("tensor", lambda t, j=j, c=c, bank=bank: t.transpose(
                    out=ps[bank][:], in_=tok[j][:, c * 128:(c + 1) * 128],
                    identity=g.idf[:]), inc=s_tr)
                p.wait("vector", s_tr, b_tr + n + 1)
                p.emit("vector", lambda v, c=c, i=i, bank=bank: v.tensor_copy(
                    out=g.stream[c][:, i * 128:(i + 1) * 128], in_=ps[bank][:]),
                    inc=s_ev)
        p.barrier()


def build_store_stream(g, seq):
    nc, p, e = g.nc, g.p, g.e
    with ExitStack() as ctx:
        ee = ctx.enter_context
        otok = [ee(nc.sbuf_tensor(f"st_tok{seq}_{j}", [128, D], F32)) for j in range(2)]
        ps = [ee(nc.psum_tensor(f"st_ps{seq}_{j}", [128, 128], F32)) for j in range(2)]
        s_tr = p.S("tr")
        s_ev = p.S("ev")
        out_base = g.s_out.val
        b_tr = s_tr.val
        b_ev = s_ev.val

        for i in range(TT):
            j = i % 2
            for c in range(DC):
                n = i * DC + c
                bank = n % 2
                if n >= 2:
                    p.wait("tensor", s_ev, b_ev + n - 1)
                p.emit("tensor", lambda t, c=c, i=i, bank=bank: t.transpose(
                    out=ps[bank][:], in_=g.stream[c][:, i * 128:(i + 1) * 128],
                    identity=g.idf[:]), inc=s_tr)
                p.wait("vector", s_tr, b_tr + n + 1)
                if i >= 2 and c == 0:
                    p.wait("vector", g.s_out, out_base + (i - 1) * 16)
                p.emit("vector", lambda v, c=c, bank=bank, j=j: v.tensor_copy(
                    out=otok[j][:, c * 128:(c + 1) * 128], in_=ps[bank][:]),
                    inc=s_ev)
            p.wait("gpsimd", s_ev, b_ev + (i + 1) * DC)
            p.emit("gpsimd", lambda gp, i=i, j=j: gp.dma_start(
                out=g.out_hbm[seq, i * 128:(i + 1) * 128, :], in_=otok[j][:]),
                inc=g.s_out, incn=16)
        p.barrier()


def build_ln(g, seq, tag):
    nc, p, e = g.nc, g.p, g.e
    NP = 4
    with ExitStack() as ctx:
        ee = ctx.enter_context
        xsq = [ee(nc.sbuf_tensor(f"{tag}_xsq{j}", [128, 512], BF16)) for j in range(2)]
        mean = [ee(nc.sbuf_tensor(f"{tag}_mean{j}", [1, 512], F32)) for j in range(2)]
        vtmp = [ee(nc.sbuf_tensor(f"{tag}_vtmp{j}", [1, 512], F32)) for j in range(2)]
        m2 = ee(nc.sbuf_tensor(f"{tag}_m2", [1, 512], F32))
        rstd_pc = [ee(nc.sbuf_tensor(f"{tag}_rstd{j}", [1, 512], F32)) for j in range(2)]
        mrstd_pc = [ee(nc.sbuf_tensor(f"{tag}_mrstd{j}", [1, 512], F32)) for j in range(2)]
        ps_m = [ee(nc.psum_tensor(f"{tag}_psm{j}", [1, 512], F32)) for j in range(2)]
        ps_s = [ee(nc.psum_tensor(f"{tag}_pss{j}", [1, 512], F32)) for j in range(2)]
        ps_b1 = [ee(nc.psum_tensor(f"{tag}_psb1{j}", [128, 512], F32)) for j in range(2)]
        ps_b2 = [ee(nc.psum_tensor(f"{tag}_psb2{j}", [128, 512], F32)) for j in range(2)]
        s_sq = p.S("ln_sq")
        s_x2 = p.S("ln_x2")
        s_mm = p.S("ln_mm")
        s_cmb = p.S("ln_cmb")
        s_sqrt = p.S("ln_sqrt")
        s_fin = p.S("ln_fin")
        s_bc = p.S("ln_bc")
        s_ev = p.S("ln_ev")
        b_sq = s_sq.val; b_x2 = s_x2.val; b_mm = s_mm.val; b_cmb = s_cmb.val
        b_sqrt = s_sqrt.val; b_fin = s_fin.val; b_bc = s_bc.val; b_ev = s_ev.val

        def head(piece):
            lo = piece * 512
            j = piece % 2
            for c in range(DC):
                n = piece * DC + c
                if n >= 2:
                    p.wait("scalar", s_x2, b_x2 + n - 1)
                if c == 0 and piece >= 2:
                    p.wait("scalar", s_cmb, b_cmb + piece - 1)
                p.emit("scalar", lambda sc, c=c, lo=lo, jx=n % 2: sc.activation(
                    out=xsq[jx][:], in_=g.stream[c][:, lo:lo + 512], func=AF.Square),
                    inc=s_sq)
                p.wait("tensor", s_sq, b_sq + n + 1)
                if c == 0 and piece >= 2:
                    p.wait("tensor", s_cmb, b_cmb + piece - 1)
                p.emit("tensor", lambda t, c=c, lo=lo, j=j, first=(c == 0): t.matmul(
                    out=ps_m[j][0:1, :], lhsT=g.ocf[:, 0:1],
                    rhs=g.stream[c][:, lo:lo + 512],
                    start=first, stop=(c == DC - 1)))
                p.emit("tensor", lambda t, jx=n % 2, j=j, first=(c == 0),
                       last=(c == DC - 1): t.matmul(
                           out=ps_s[j][0:1, :], lhsT=g.ocb[:, 0:1], rhs=xsq[jx][:],
                           start=first, stop=last), inc=s_x2)
                if c == DC - 1:
                    p.emit("tensor", lambda t: None, inc=s_mm)

        def tail(piece):
            lo = piece * 512
            j = piece % 2
            p.wait("vector", s_mm, b_mm + piece + 1)
            p.emit("vector", lambda v, j=j: v.tensor_scalar(
                out=mean[j][:], in0=ps_m[j][0:1, :], scalar1=1.0 / D, scalar2=None,
                op0=OP.mult))
            p.emit("vector", lambda v, j=j: v.tensor_scalar(
                out=vtmp[j][:], in0=ps_s[j][0:1, :], scalar1=1.0 / D, scalar2=1e-5,
                op0=OP.mult, op1=OP.add))
            p.emit("vector", lambda v, j=j: v.tensor_tensor(
                out=m2[:], in0=mean[j][:], in1=mean[j][:], op=OP.mult))
            p.emit("vector", lambda v, j=j: v.tensor_tensor(
                out=vtmp[j][:], in0=vtmp[j][:], in1=m2[:], op=OP.subtract),
                inc=s_cmb)
            p.wait("scalar", s_cmb, b_cmb + piece + 1)
            p.emit("scalar", lambda sc, j=j: sc.activation(
                out=rstd_pc[j][:], in_=vtmp[j][:], func=AF.Ln))
            p.emit("scalar", lambda sc, j=j: sc.activation(
                out=rstd_pc[j][:], in_=rstd_pc[j][:], func=AF.Exp, scale=-0.5),
                inc=s_sqrt)
            p.wait("vector", s_sqrt, b_sqrt + piece + 1)
            p.emit("vector", lambda v, j=j: v.scalar_tensor_tensor(
                out=mrstd_pc[j][:], in0=mean[j][:], scalar=-1.0, in1=rstd_pc[j][:],
                op0=OP.mult, op1=OP.mult), inc=s_fin)
            p.wait("tensor", s_fin, b_fin + piece + 1)
            if piece >= 2:
                p.wait("tensor", s_ev, b_ev + (piece - 1) * 2)
            p.emit("tensor", lambda t, j=j: t.matmul(
                out=ps_b1[j][:], lhsT=g.orf[0:1, :], rhs=rstd_pc[j][:],
                start=True, stop=True))
            p.emit("tensor", lambda t, j=j: t.matmul(
                out=ps_b2[j][:], lhsT=g.orf[0:1, :], rhs=mrstd_pc[j][:],
                start=True, stop=True), inc=s_bc)
            p.wait("vector", s_bc, b_bc + piece + 1)
            p.emit("vector", lambda v, lo=lo, j=j: v.tensor_copy(
                out=g.rstd_bc[:, lo:lo + 512], in_=ps_b1[j][:]), inc=s_ev)
            p.emit("vector", lambda v, lo=lo, j=j: v.tensor_copy(
                out=g.mrstd_bc[:, lo:lo + 512], in_=ps_b2[j][:]), inc=s_ev)

        head(0)
        head(1)
        for piece in range(NP):
            if piece + 2 < NP:
                head(piece + 2)
            tail(piece)
        for c in range(DC):
            p.emit("vector", lambda v, c=c: v.tensor_tensor(
                out=g.xln[c][:], in0=g.stream[c][:], in1=g.rstd_bc[:],
                op=OP.mult))
            p.emit("vector", lambda v, c=c: v.tensor_tensor(
                out=g.xln[c][:], in0=g.xln[c][:], in1=g.mrstd_bc[:],
                op=OP.add))
        p.barrier()


def build_l0a(g, seq):
    nc, p, e = g.nc, g.p, g.e
    with ExitStack() as ctx:
        ee = ctx.enter_context
        q_T = chunk_list(ee, nc, f"a{seq}q", SLAB, BF16)
        k_T = chunk_list(ee, nc, f"a{seq}k", SLAB + 256, BF16)
        v_tok = [ee(nc.sbuf_tensor(f"a{seq}v{t}", [128, D], BF16)) for t in range(6)]
        attn_T = chunk_list(ee, nc, f"a{seq}o", SLAB, BF16)
        probs = [ee(nc.sbuf_tensor(f"a{seq}pr{j}", [128, 768], BF16)) for j in range(4)]
        rz = [ee(nc.sbuf_tensor(f"a{seq}rz{j}", [1, 256], F32)) for j in range(2)]
        zsb = [ee(nc.sbuf_tensor(f"a{seq}zs{j}", [1, 256], F32)) for j in range(2)]
        rzb16 = [ee(nc.sbuf_tensor(f"a{seq}rzb16{j}", [1, 256], BF16)) for j in range(2)]
        rzb = [ee(nc.sbuf_tensor(f"a{seq}rzb{j}", [128, 128], F32)) for j in range(2)]
        wslc = [ee(nc.sbuf_tensor(f"a{seq}ws{j}", [128, D], BF16)) for j in range(2)]
        wvf = chunk_list(ee, nc, f"a{seq}wv", D, BF16)
        ps_proj = [ee(nc.psum_tensor(f"a{seq}pp{j}", [128, 512], F32)) for j in range(2)]
        ps_sc2 = [ee(nc.psum_tensor(f"a{seq}psc{j}", [128, 384], F32)) for j in range(2)]
        ps_zb = [ee(nc.psum_tensor(f"a{seq}pz{j}", [128, 256], F32)) for j in range(2)]
        ps_av = [ee(nc.psum_tensor(f"a{seq}pa{j}", [128, 128], F32)) for j in range(2)]
        dbg_pr = ee(nc.sbuf_tensor(f"a{seq}dbgpr", [128, 768], BF16))
        dbg_rzb = ee(nc.sbuf_tensor(f"a{seq}dbgrzb", [128, 128], F32))
        dbg_av = ee(nc.sbuf_tensor(f"a{seq}dbgav", [128, 128], F32))
        dbg_z = ee(nc.sbuf_tensor(f"a{seq}dbgz", [128, 256], F32))

        ps_sc = [ps_sc2[0], ps_sc2[1], None, None]
        s_wld = [p.S("wld0"), p.S("wld1")]
        s_pmm = p.S("pmm")
        s_pev = [p.S("pev0"), p.S("pev1")]
        s_qk = p.S("qk")
        s_exp = p.S("exp")
        s_msk = p.S("msk")
        s_msz = p.S("msz")
        s_zmm = p.S("zmm")
        s_rz = p.S("rz")
        s_bcm = p.S("bcm")
        s_rzb = p.S("rzb")
        s_av = p.S("av")
        s_fin = p.S("afin")

        wseq = {0: g.stg_wq, 1: g.stg_wk, 2: g.stg_wv, 3: g.stg_wo}
        wslc_free = [0, 0]
        wvf_free = [0]
        nslc = [0]

        def load_slices(widx, m):
            j = nslc[0] % 2
            nslc[0] += 1
            if wslc_free[j] > 0:
                p.wait("sync", s_pmm, wslc_free[j])
            for c in range(DC):
                p.emit("sync", lambda s, j=j, c=c, w=wseq[widx], m=m: s.dma_start(
                    out=wslc[j][:, c * 128:(c + 1) * 128],
                    in_=w[c][:, m * 128:(m + 1) * 128]), inc=s_wld[0], incn=16)
            return j, s_wld[0].val

        pfill = [0]
        bank_ev = [s_pev[0].val, s_pev[1].val]

        def proj_guard():
            b = pfill[0] % 2
            if bank_ev[b] > 0:
                p.wait("tensor", s_pev[b], bank_ev[b])
            return b

        def proj(widx, m, mov_cols, evac):
            wj, wldv = load_slices(widx, m)
            b = proj_guard()
            p.wait("tensor", s_wld[0], wldv)
            lo, n = mov_cols
            for kc in range(DC):
                p.emit("tensor",
                       lambda t, kc=kc, b=b, lo=lo, n=n, wj=wj: t.matmul(
                           out=ps_proj[b][:, 0:n],
                           lhsT=wslc[wj][:, kc * 128:(kc + 1) * 128],
                           rhs=g.xln[kc][:, lo:lo + n],
                           start=(kc == 0), stop=(kc == DC - 1)),
                       inc=(s_pmm if kc == DC - 1 else None))
            wslc_free[wj] = s_pmm.val
            evac(b, s_pmm.val)
            bank_ev[b] = s_pev[b].val
            pfill[0] += 1

        def mk_evac_s(dst, dcols, bias, m):
            def ev(b, mmv):
                p.wait("scalar", s_pmm, mmv)
                lo, n = dcols
                p.emit("scalar", lambda sc, b=b, lo=lo, n=n, m=m: sc.activation(
                    out=dst[:, lo:lo + n], in_=ps_proj[b][:, 0:n], func=AF.Identity,
                    bias=bias[:, m:m + 1], scale=1.0), inc=s_pev[b])
            return ev

        def proj_tok_t6(tile_tok_lo, tt_ix, wv_ldv):
            for b in (0, 1):
                if bank_ev[b] > 0:
                    p.wait("tensor", s_pev[b], bank_ev[b])
            p.wait("tensor", s_wld[1], wv_ldv)
            for kc in range(DC):
                for half in (0, 1):
                    p.emit("tensor",
                           lambda t, kc=kc, half=half, lo=tile_tok_lo: t.matmul(
                               out=ps_proj[half][:, 0:512],
                               lhsT=g.xln[kc][:, lo:lo + 128],
                               rhs=wvf[kc][:, half * 512:(half + 1) * 512],
                               start=(kc == 0), stop=False),
                           inc=(s_pmm if half == 1 else None))
            for half in (0, 1):
                p.emit("tensor", lambda t, half=half: t.matmul(
                    out=ps_proj[half][:, 0:512], lhsT=g.orb[0:1, 0:128],
                    rhs=g.bv_row_b[0:1, half * 512:half * 512 + 512],
                    start=False, stop=True), inc=s_pmm)
                mmv = s_pmm.val
                p.wait("vector", s_pmm, mmv)
                p.emit("vector", lambda v, half=half, tt=tt_ix: v.tensor_copy(
                    out=v_tok[tt][:, half * 512:(half + 1) * 512],
                    in_=ps_proj[half][:, 0:512]), inc=s_pev[half])
                bank_ev[half] = s_pev[half].val
            pfill[0] += 2

        ps_sc[2] = ps_proj[0][:, 0:384] if False else None
        pair_rec = {}
        gp = 0

        def sc_tile(u):
            j = u % 4
            if j < 2:
                return ps_sc[j][:]
            return ps_proj[j - 2][:, 0:384]

        for s in range(NSLAB):
            body_lo = s * SLAB
            wlo = body_lo - 128
            if wvf_free[0] > 0:
                p.wait("sync", s_pmm, wvf_free[0])
            for kc in range(DC):
                p.emit("sync", lambda sy, kc=kc: sy.dma_start(
                    out=wvf[kc][:], in_=g.stg_wv[kc]), inc=s_wld[1], incn=16)
            wv_ldv = s_wld[1].val
            for m in range(DC):
                proj(0, m, (body_lo, SLAB), mk_evac_s(q_T[m], (0, SLAB), g.bias_q, m))
            vlo = max(0, -wlo)
            vhi = 768 - max(0, (wlo + 768) - T)
            nmsz = 0
            if vlo > 0:
                for m in range(DC):
                    p.emit("vector", lambda v, m=m, vlo=vlo: v.memset(
                        k_T[m][:, 0:vlo], 0.0), inc=s_msz)
                for t6 in range(vlo // 128):
                    p.emit("vector", lambda v, t6=t6: v.memset(v_tok[t6][:], 0.0),
                           inc=s_msz)
            if vhi < 768:
                for m in range(DC):
                    p.emit("vector", lambda v, m=m, vhi=vhi: v.memset(
                        k_T[m][:, vhi:768], 0.0), inc=s_msz)
                for t6 in range(vhi // 128, 6):
                    p.emit("vector", lambda v, t6=t6: v.memset(v_tok[t6][:], 0.0),
                           inc=s_msz)
            msz_val = s_msz.val
            for m in range(DC):
                wj, wldv = load_slices(1, m)
                p.wait("tensor", s_wld[0], wldv)
                for ti in range(2):
                    blo = max(vlo, ti * 384)
                    bhi = min(vhi, (ti + 1) * 384)
                    if bhi <= blo:
                        continue
                    b = proj_guard()
                    n = bhi - blo
                    lo = wlo + blo
                    for kc in range(DC):
                        p.emit("tensor",
                               lambda t, kc=kc, b=b, lo=lo, n=n, wj=wj: t.matmul(
                                   out=ps_proj[b][:, 0:n],
                                   lhsT=wslc[wj][:, kc * 128:(kc + 1) * 128],
                                   rhs=g.xln[kc][:, lo:lo + n],
                                   start=(kc == 0), stop=(kc == DC - 1)),
                               inc=(s_pmm if kc == DC - 1 else None))
                    mk_evac_s(k_T[m], (blo, n), g.bias_k, m)(b, s_pmm.val)
                    bank_ev[b] = s_pev[b].val
                    pfill[0] += 1
                wslc_free[wj] = s_pmm.val
            for t6 in range(6):
                tok_lo = wlo + t6 * 128
                if tok_lo < 0 or tok_lo >= T:
                    continue
                proj_tok_t6(tok_lo, t6, wv_ldv)
            wvf_free[0] = s_pmm.val

            kqv_ev0, kqv_ev1 = s_pev[0].val, s_pev[1].val
            if seq == 0 and s == 0:
                dbg_dump_chunks(g, "q0", q_T, SLAB, BF16)
                dbg_dump_chunks(g, "k0", k_T, SLAB + 256, BF16)
                dbg_dump_chunks(g, "v0", v_tok, D, BF16)

            pairs = [(bb, hp) for bb in range(4) for hp in range(DC)]
            npairs = len(pairs)
            base_u = gp

            def emit_qk(u):
                bb, hp = pairs[u - base_u]
                if u >= 4:
                    r = pair_rec[u - 4]
                    p.wait("tensor", s_exp, r["exp"])
                if u - base_u <= 1:
                    p.wait("tensor", s_pev[0], kqv_ev0)
                    p.wait("tensor", s_pev[1], kqv_ev1)
                    p.wait("tensor", s_msz, msz_val)
                rec = pair_rec.setdefault(u, {})
                sct = sc_tile(u)
                for half in range(2):
                    for kc in range(3):
                        p.emit("tensor",
                               lambda t, half=half, kc=kc, bb=bb, hp=hp, sct=sct:
                               t.matmul(
                                   out=sct[:, kc * 128:(kc + 1) * 128],
                                   lhsT=k_T[hp][half * 64:(half + 1) * 64,
                                                (bb + kc) * 128:(bb + kc + 1) * 128],
                                   rhs=q_T[hp][half * 64:(half + 1) * 64,
                                               bb * 128:(bb + 1) * 128],
                                   start=(kc == 0), stop=(kc == 2),
                                   skip_group_check=True),
                               inc=(s_qk if kc == 2 else None))
                    rec[f"qk{half}"] = s_qk.val
                    if half == 0:
                        p.wait("tensor", s_exp, s_exp.val + 1)

            def emit_expmask(u):
                pp = u % 4
                rec = pair_rec[u]
                sct = sc_tile(u)
                for half in range(2):
                    if u >= 4:
                        p.wait("scalar", s_av, pair_rec[u - 4]["av"])
                    p.wait("scalar", s_qk, rec[f"qk{half}"])
                    p.emit("scalar", lambda sc, pp=pp, half=half, sct=sct:
                           sc.activation(
                               out=probs[pp][:, half * 384:(half + 1) * 384],
                               in_=sct, func=AF.Exp, bias=g.cn4[:, 0:1],
                               scale=1.0), inc=s_exp)
                rec["exp"] = s_exp.val
                p.wait("vector", s_exp, s_exp.val)
                for half in range(2):
                    p.emit("vector", lambda v, pp=pp, half=half: v.tensor_tensor(
                        out=probs[pp][:, half * 384:half * 384 + 128],
                        in0=probs[pp][:, half * 384:half * 384 + 128],
                        in1=g.maskL[:], op=OP.mult))
                    p.emit("vector", lambda v, pp=pp, half=half: v.tensor_tensor(
                        out=probs[pp][:, half * 384 + 256:half * 384 + 384],
                        in0=probs[pp][:, half * 384 + 256:half * 384 + 384],
                        in1=g.maskR[:], op=OP.mult),
                        inc=(s_msk if half == 1 else None))
                rec["msk"] = s_msk.val

            def emit_pe_z(u):
                pp = u % 2
                rec = pair_rec[u]
                p.wait("tensor", s_msk, rec["msk"])
                if u >= 2:
                    p.wait("tensor", s_rzb, pair_rec[u - 2]["rzb"])
                pr2 = probs[u % 4][:].rearrange("p (h x) -> p h x", h=2)
                for kc in range(3):
                    p.emit("tensor", lambda t, pp=pp, kc=kc, pr2=pr2: t.matmul(
                        out=ps_zb[pp][0:1, :], lhsT=g.ocb[:, 0:1],
                        rhs=pr2[:, :, kc * 128:(kc + 1) * 128],
                        start=(kc == 0), stop=(kc == 2), skip_group_check=True),
                        inc=(s_zmm if kc == 2 else None))
                rec["zmm"] = s_zmm.val

            def emit_pe_av(u):
                pp = u % 2
                p4 = u % 4
                rec = pair_rec[u]
                if u >= 2:
                    p.wait("tensor", s_fin, pair_rec[u - 2]["fin"])
                bb, hp = pairs[u - base_u]
                for half in range(2):
                    for kc in range(3):
                        p.emit("tensor",
                               lambda t, pp=pp, p4=p4, half=half, kc=kc, bb=bb, hp=hp:
                               t.matmul(
                                   out=ps_av[pp][half * 64:(half + 1) * 64, :],
                                   lhsT=v_tok[bb + kc][:, hp * 128 + half * 64:
                                                       hp * 128 + (half + 1) * 64],
                                   rhs=probs[p4][:, half * 384 + kc * 128:
                                                 half * 384 + (kc + 1) * 128],
                                   start=(kc == 0), stop=(kc == 2),
                                   tile_position=(0, half * 64),
                                   skip_group_check=True),
                               inc=(s_av if (half == 1 and kc == 2) else None))
                rec["av"] = s_av.val

            def emit_pe_bc(u):
                pp = u % 2
                rec = pair_rec[u]
                p.wait("tensor", s_rz, rec["rz"])
                p.emit("tensor", lambda t, pp=pp: t.matmul(
                    out=ps_zb[pp][64:128, 0:128], lhsT=g.orb[0:1, 0:64],
                    rhs=rzb16[pp][0:1, 128:256], start=True, stop=True,
                    tile_position=(0, 64), skip_group_check=True))
                p.emit("tensor", lambda t, pp=pp: t.matmul(
                    out=ps_zb[pp][0:64, 0:128], lhsT=g.orb[0:1, 0:64],
                    rhs=rzb16[pp][0:1, 0:128], start=True, stop=True,
                    skip_group_check=True), inc=s_bcm)
                rec["bcm"] = s_bcm.val

            def emit_v_rz(u):
                pp = u % 2
                rec = pair_rec[u]
                p.wait("scalar", s_zmm, rec["zmm"])
                if u >= 2:
                    p.wait("scalar", s_bcm, pair_rec[u - 2]["bcm"])
                p.emit("scalar", lambda sc, pp=pp: sc.activation(
                    out=rz[pp][:], in_=ps_zb[pp][0:1, :], func=AF.Ln))
                p.emit("scalar", lambda sc, pp=pp: sc.activation(
                    out=rzb16[pp][:], in_=rz[pp][:], func=AF.Exp, scale=-1.0),
                    inc=s_rz)
                rec["rz"] = s_rz.val

            def emit_v_fin(u):
                pp = u % 2
                rec = pair_rec[u]
                bb, hp = pairs[u - base_u]
                p.wait("vector", s_bcm, rec["bcm"])
                p.emit("vector", lambda v, pp=pp: v.tensor_copy(
                    out=rzb[pp][:], in_=ps_zb[pp][:, 0:128]), inc=s_rzb)
                rec["rzb"] = s_rzb.val
                p.wait("vector", s_av, rec["av"])
                p.emit("vector", lambda v, pp=pp, hp=hp, bb=bb: v.tensor_tensor(
                    out=attn_T[hp][:, bb * 128:(bb + 1) * 128],
                    in0=ps_av[pp][:], in1=rzb[pp][:], op=OP.mult), inc=s_fin)
                rec["fin"] = s_fin.val

            emit_qk(base_u)
            emit_expmask(base_u)
            if npairs > 1:
                emit_qk(base_u + 1)
                emit_expmask(base_u + 1)
            for u in range(base_u, base_u + npairs):
                emit_pe_z(u)
                emit_v_rz(u)
                if u + 2 < base_u + npairs:
                    emit_qk(u + 2)
                emit_pe_av(u)
                emit_pe_bc(u)
                emit_v_fin(u)
                if u + 2 < base_u + npairs:
                    emit_expmask(u + 2)
            gp += npairs
            p.wait("tensor", s_exp, s_exp.val)

            attn_done = s_fin.val
            for m in range(DC):
                wj, wldv = load_slices(3, m)
                b_ = proj_guard()
                p.wait("tensor", s_wld[0], wldv)
                if m == 0:
                    p.wait("tensor", s_fin, attn_done)
                for kc in range(DC):
                    p.emit("tensor", lambda t, kc=kc, b_=b_, wj=wj: t.matmul(
                        out=ps_proj[b_][:, 0:SLAB],
                        lhsT=wslc[wj][:, kc * 128:(kc + 1) * 128],
                        rhs=attn_T[kc][:, 0:SLAB],
                        start=(kc == 0), stop=(kc == DC - 1)),
                        inc=(s_pmm if kc == DC - 1 else None))
                wslc_free[wj] = s_pmm.val
                mmv = s_pmm.val
                p.wait("vector", s_pmm, mmv)
                p.emit("vector", lambda v, b_=b_, m=m, lo=body_lo: v.scalar_tensor_tensor(
                    out=g.stream[m][:, lo:lo + SLAB], in0=ps_proj[b_][:, 0:SLAB],
                    scalar=g.bias_o[:, m:m + 1], in1=g.stream[m][:, lo:lo + SLAB],
                    op0=OP.add, op1=OP.add), inc=s_pev[b_])
                bank_ev[b_] = s_pev[b_].val
                pfill[0] += 1
        p.barrier()




def build_l1(g, seq):
    nc, p, e = g.nc, g.p, g.e
    Q = 16
    TQ = 512
    NCH = TQ // Q
    NQ = T // TQ
    NDC = 16
    with ExitStack() as ctx:
        ee = ctx.enter_context
        vbuf = ee(nc.sbuf_tensor(f"l1v{seq}", [128, 520], BF16))
        vtails = ee(nc.sbuf_tensor(f"l1vt{seq}", [128, NDC * 4], BF16))
        uc = ee(nc.sbuf_tensor(f"l1u{seq}", [128, TQ], BF16))
        y_all = ee(nc.sbuf_tensor(f"l1y{seq}", [128, NDC * TQ], BF16))
        g_all = ee(nc.sbuf_tensor(f"l1g{seq}", [128, NDC * NCH * 16], BF16))
        h_st = ee(nc.sbuf_tensor(f"l1h{seq}", [128, (NCH + 1) * 256], BF16))
        gp = ee(nc.sbuf_tensor(f"l1p{seq}", [128, 4096], BF16))
        wstat = [ee(nc.sbuf_tensor(f"l1w{seq}_{j}", [128, 2048], BF16)) for j in range(2)]
        vtb = [ee(nc.sbuf_tensor(f"l1vtb{seq}_{j}", [128, 256], BF16)) for j in range(2)]
        wtb = [ee(nc.sbuf_tensor(f"l1wtb{seq}_{j}", [128, 256], BF16)) for j in range(2)]
        xsq = [ee(nc.sbuf_tensor(f"l1xq{seq}_{j}", [128, 512], BF16)) for j in range(2)]
        rpc = ee(nc.sbuf_tensor(f"l1rp{seq}", [1, 512], F32))
        otmp = ee(nc.sbuf_tensor(f"l1ot{seq}", [128, 512], F32))
        ps_w = [ee(nc.psum_tensor(f"l1pw{seq}_{j}", [128, 512], F32)) for j in range(2)]
        ps_ss = ee(nc.psum_tensor(f"l1ps{seq}", [1, 512], F32))
        ps_bc = ee(nc.psum_tensor(f"l1pb{seq}", [128, 512], F32))

        s_pmm = p.S("pmm")
        s_pev = [p.S("pev0"), p.S("pev1")]
        s_wld = [p.S("wld0"), p.S("wld1")]
        s_sq = p.S("ln_sq")
        s_mm = p.S("ln_mm")
        s_cmb = p.S("ln_cmb")
        s_sqrt = p.S("ln_sqrt")
        s_bc = p.S("ln_bc")
        s_ev = p.S("ln_ev")
        s_scy = p.S("scy")
        s_scv = p.S("scv")

        s_x2 = p.S("ln_x2")
        b_sq = s_sq.val; b_mm = s_mm.val; b_cmb = s_cmb.val
        b_sqrt = s_sqrt.val; b_bc = s_bc.val; b_ev = s_ev.val
        b_x2 = s_x2.val
        for piece in range(4):
            lo = piece * 512
            for c in range(DC):
                n = piece * DC + c
                if n >= 2:
                    p.wait("scalar", s_x2, b_x2 + n - 1)
                p.emit("scalar", lambda sc, c=c, lo=lo, j=n % 2: sc.activation(
                    out=xsq[j][:], in_=g.xln[c][:, lo:lo + 512], func=AF.Square),
                    inc=s_sq)
                p.wait("tensor", s_sq, b_sq + n + 1)
                if piece > 0 and c == 0:
                    p.wait("tensor", s_cmb, b_cmb + piece)
                p.emit("tensor", lambda t, j=n % 2, first=(c == 0), last=(c == DC - 1):
                       t.matmul(out=ps_ss[0:1, :], lhsT=g.ocb[:, 0:1], rhs=xsq[j][:],
                                start=first, stop=last),
                       inc=s_x2)
                if c == DC - 1:
                    p.emit("tensor", lambda t: None, inc=s_mm)
            p.wait("vector", s_mm, b_mm + piece + 1)
            p.emit("vector", lambda v: v.tensor_scalar(
                out=rpc[:], in0=ps_ss[0:1, :], scalar1=1.0 / (2 * D), scalar2=1e-6,
                op0=OP.mult, op1=OP.add), inc=s_cmb)
            p.wait("scalar", s_cmb, b_cmb + piece + 1)
            p.emit("scalar", lambda sc: sc.activation(
                out=rpc[:], in_=rpc[:], func=AF.Ln))
            p.emit("scalar", lambda sc: sc.activation(
                out=rpc[:], in_=rpc[:], func=AF.Exp, scale=-0.5), inc=s_sqrt)
            p.wait("tensor", s_sqrt, b_sqrt + piece + 1)
            if piece > 0:
                p.wait("tensor", s_ev, b_ev + piece)
            p.emit("tensor", lambda t: t.matmul(
                out=ps_bc[:], lhsT=g.orf[0:1, :], rhs=rpc[:],
                start=True, stop=True), inc=s_bc)
            p.wait("vector", s_bc, b_bc + piece + 1)
            p.emit("vector", lambda v, lo=lo: v.tensor_copy(
                out=g.rstd_bc[:, lo:lo + 512], in_=ps_bc[:]), inc=s_ev)
        p.barrier()

        kvv = g.kv[:].rearrange("p (d t) -> p d t", t=16)
        hsv = h_st[:].rearrange("p (c d s) -> p c d s", d=NDC, s=16)
        gav = g_all[:].rearrange("p (d c s) -> p d c s", c=NCH, s=16)
        yav = y_all[:].rearrange("p (d t) -> p d t", t=TQ)
        aqv = g.AQT[:].rearrange("p (d s r) -> p d s r", s=16, r=16)

        wst_free = [0, 0]
        vt_free = [0, 0]
        wt_free = [0, 0]
        bank_free = [0, 0]

        p.emit("vector", lambda v: v.memset(h_st[:, 0:256], 0.0))
        p.emit("vector", lambda v: v.memset(vtails[:], 0.0))

        for q in range(NQ):
            lo = q * TQ
            for dc in range(NDC):
                jj = dc % 2
                if wst_free[jj] > 0:
                    p.wait("sync", s_pmm, wst_free[jj])
                for c in range(DC):
                    p.emit("sync", lambda s, jj=jj, c=c, dc=dc: s.dma_start(
                        out=wstat[jj][:, c * 128:(c + 1) * 128],
                        in_=g.stg_win[c][:, dc * 128:(dc + 1) * 128]),
                        inc=s_wld[0], incn=16)
                wldv = s_wld[0].val
                if vt_free[jj] > 0:
                    p.wait("sync", s_scv, vt_free[jj])
                p.emit("sync", lambda s, jj=jj, dc=dc: s.dma_start(
                    out=vtb[jj][:], in_=g.stg_Vt[:, dc * 256:(dc + 1) * 256]),
                    inc=s_wld[1], incn=16)
                vtldv = s_wld[1].val
                b = dc % 2
                if bank_free[b] > 0:
                    p.wait("tensor", s_pev[b], bank_free[b])
                p.wait("tensor", s_wld[0], wldv)
                for c in range(DC):
                    p.emit("tensor", lambda t, c=c, b=b, jj=jj, lo=lo: t.matmul(
                        out=ps_w[b][:], lhsT=wstat[jj][:, c * 128:(c + 1) * 128],
                        rhs=g.xln[c][:, lo:lo + 512],
                        start=(c == 0), stop=(c == DC - 1)),
                        inc=(s_pmm if c == DC - 1 else None))
                wst_free[jj] = s_pmm.val
                mmv = s_pmm.val
                p.wait("vector", s_pmm, mmv)
                if q == 0:
                    p.emit("vector", lambda v: v.memset(vbuf[:, 0:4], 0.0))
                else:
                    p.emit("vector", lambda v, dc=dc: v.tensor_copy(
                        out=vbuf[:, 0:4], in_=vtails[:, dc * 4:(dc + 1) * 4]))
                p.emit("vector", lambda v, b=b, lo=lo: v.tensor_tensor(
                    out=vbuf[:, 4:516], in0=ps_w[b][:],
                    in1=g.rstd_bc[:, lo:lo + 512], op=OP.mult), inc=s_pev[b])
                bank_free[b] = s_pev[b].val
                p.emit("vector", lambda v, dc=dc: v.tensor_copy(
                    out=vtails[:, dc * 4:(dc + 1) * 4], in_=vbuf[:, 512:516]))
                p.emit("vector", lambda v, dc=dc: v.tensor_scalar(
                    out=uc[:], in0=vbuf[:, 1:513],
                    scalar1=g.cl[:, dc * 4 + 3:dc * 4 + 4],
                    scalar2=g.ceff[:, dc:dc + 1], op0=OP.mult, op1=OP.add))
                for lam in (2, 1, 0):
                    p.emit("vector", lambda v, dc=dc, lam=lam: v.scalar_tensor_tensor(
                        out=uc[:], in0=vbuf[:, 4 - lam:516 - lam],
                        scalar=g.cl[:, dc * 4 + lam:dc * 4 + lam + 1], in1=uc[:],
                        op0=OP.mult, op1=OP.add))
                if q == 0:
                    p.emit("vector", lambda v, dc=dc: v.tensor_tensor(
                        out=uc[:, 0:3], in0=uc[:, 0:3],
                        in1=g.corr3[:, dc * 3:dc * 3 + 3], op=OP.subtract))
                ucv = uc[:].rearrange("p (c i) -> p c i", i=Q)
                p.emit("vector", lambda v, dc=dc, ucv=ucv: v.tensor_scalar(
                    out=yav[:, dc].rearrange("p (c i) -> p c i", i=Q),
                    in0=ucv, scalar1=kvv[:, dc, 0:1], scalar2=None, op0=OP.mult))
                for tau in range(1, Q):
                    p.emit("vector", lambda v, dc=dc, tau=tau, ucv=ucv:
                           v.scalar_tensor_tensor(
                               out=yav[:, dc].rearrange("p (c i) -> p c i", i=Q)[:, :, tau:Q],
                               in0=ucv[:, :, 0:Q - tau],
                               scalar=kvv[:, dc, tau:tau + 1],
                               in1=yav[:, dc].rearrange("p (c i) -> p c i", i=Q)[:, :, tau:Q],
                               op0=OP.mult, op1=OP.add))
                p.wait("vector", s_wld[1], vtldv)
                vtv = vtb[jj][:].rearrange("p (s i) -> p s i", i=Q)
                for cg in range(2):
                    pg = gp[:].rearrange("p (c s i) -> p c s i", c=16, s=16)
                    i0 = ucv[:, cg * 16:(cg + 1) * 16].unsqueeze(2)\
                        .broadcast_to([128, 16, 16, Q])
                    i1 = vtv.unsqueeze(1).broadcast_to([128, 16, 16, Q])
                    p.emit("vector", lambda v, pg=pg, i0=i0, i1=i1: v.tensor_tensor(
                        out=pg, in0=i0, in1=i1, op=OP.mult))
                    for lev in (8, 4, 2):
                        p.emit("vector", lambda v, pg=pg, lev=lev: v.tensor_tensor(
                            out=pg[:, :, :, 0:lev], in0=pg[:, :, :, 0:lev],
                            in1=pg[:, :, :, lev:2 * lev], op=OP.add))
                    p.emit("vector", lambda v, pg=pg, dc=dc, cg=cg: v.tensor_tensor(
                        out=gav[:, dc, cg * 16:(cg + 1) * 16].unsqueeze(3),
                        in0=pg[:, :, :, 0:1], in1=pg[:, :, :, 1:2], op=OP.add))
                p.emit("vector", lambda v: v.drain(), inc=s_scv)
                vt_free[jj] = s_scv.val

            for c in range(NCH):
                pg = gp[:].rearrange("p (d s r) -> p d s r", d=NDC, s=16)[:, :, :, 0:16]
                i0 = hsv[:, c].unsqueeze(2).broadcast_to([128, NDC, 16, 16])
                p.emit("vector", lambda v, pg=pg, i0=i0: v.tensor_tensor(
                    out=pg, in0=i0, in1=aqv.unsqueeze(0)[0] if False else aqv,
                    op=OP.mult))
                for lev in (8, 4, 2):
                    p.emit("vector", lambda v, pg=pg, lev=lev: v.tensor_tensor(
                        out=pg[:, :, :, 0:lev], in0=pg[:, :, :, 0:lev],
                        in1=pg[:, :, :, lev:2 * lev], op=OP.add))
                p.emit("vector", lambda v, pg=pg: v.tensor_tensor(
                    out=pg[:, :, :, 0:1], in0=pg[:, :, :, 0:1],
                    in1=pg[:, :, :, 1:2], op=OP.add))
                p.emit("vector", lambda v, c=c, pg=pg: v.tensor_tensor(
                    out=hsv[:, c + 1].unsqueeze(3), in0=pg[:, :, :, 0:1],
                    in1=gav[:, :, c, :].unsqueeze(3), op=OP.add))

            for dc in range(NDC):
                jj = dc % 2
                if wt_free[jj] > 0:
                    p.wait("sync", s_scv, wt_free[jj])
                p.emit("sync", lambda s, jj=jj, dc=dc: s.dma_start(
                    out=wtb[jj][:], in_=g.stg_Wt[:, dc * 256:(dc + 1) * 256]),
                    inc=s_wld[1], incn=16)
                p.wait("vector", s_wld[1], s_wld[1].val)
                wtv = wtb[jj][:].rearrange("p (i s) -> p i s", s=16)
                for cg in range(2):
                    pg = gp[:].rearrange("p (c i s) -> p c i s", c=16, i=Q)
                    i0 = hsv[:, cg * 16:(cg + 1) * 16, dc, :].unsqueeze(2)\
                        .broadcast_to([128, 16, Q, 16])
                    i1 = wtv.unsqueeze(1).broadcast_to([128, 16, Q, 16])
                    p.emit("vector", lambda v, pg=pg, i0=i0, i1=i1: v.tensor_tensor(
                        out=pg, in0=i0, in1=i1, op=OP.mult))
                    for lev in (8, 4, 2):
                        p.emit("vector", lambda v, pg=pg, lev=lev: v.tensor_tensor(
                            out=pg[:, :, :, 0:lev], in0=pg[:, :, :, 0:lev],
                            in1=pg[:, :, :, lev:2 * lev], op=OP.add))
                    p.emit("vector", lambda v, pg=pg: v.tensor_tensor(
                        out=pg[:, :, :, 0:1], in0=pg[:, :, :, 0:1],
                        in1=pg[:, :, :, 1:2], op=OP.add))
                    yslice = yav[:, dc].rearrange("p (c i) -> p c i", i=Q)
                    p.emit("vector", lambda v, pg=pg, ys=yslice, cg=cg: v.tensor_tensor(
                        out=ys[:, cg * 16:(cg + 1) * 16].unsqueeze(3),
                        in0=ys[:, cg * 16:(cg + 1) * 16].unsqueeze(3),
                        in1=pg[:, :, :, 0:1], op=OP.add))
                p.emit("vector", lambda v: v.drain(), inc=s_scv)
                wt_free[jj] = s_scv.val
            p.emit("vector", lambda v: v.tensor_copy(
                out=h_st[:, 0:256], in_=h_st[:, NCH * 256:(NCH + 1) * 256]))
            p.emit("vector", lambda v: v.drain(), inc=s_scy)
            y_ready = s_scy.val

            for dout in range(DC):
                jj = dout % 2
                if wst_free[jj] > 0:
                    p.wait("sync", s_pmm, wst_free[jj])
                for ci in range(16):
                    p.emit("sync", lambda s, jj=jj, ci=ci, dout=dout: s.dma_start(
                        out=wstat[jj][:, ci * 128:(ci + 1) * 128],
                        in_=g.stg_wout[ci][:, dout * 128:(dout + 1) * 128]),
                        inc=s_wld[0], incn=16)
                wldv = s_wld[0].val
                b = dout % 2
                if bank_free[b] > 0:
                    p.wait("tensor", s_pev[b], bank_free[b])
                p.wait("tensor", s_wld[0], wldv)
                p.wait("tensor", s_scy, y_ready)
                for ci in range(16):
                    p.emit("tensor", lambda t, ci=ci, b=b, jj=jj: t.matmul(
                        out=ps_w[b][:], lhsT=wstat[jj][:, ci * 128:(ci + 1) * 128],
                        rhs=y_all[:, ci * 512:(ci + 1) * 512],
                        start=(ci == 0), stop=(ci == 15)),
                        inc=(s_pmm if ci == 15 else None))
                wst_free[jj] = s_pmm.val
                mmv = s_pmm.val
                p.wait("vector", s_pmm, mmv)
                p.emit("vector", lambda v, b=b, dout=dout, lo=lo: v.scalar_tensor_tensor(
                    out=otmp[:], in0=ps_w[b][:], scalar=g.bias_bout[:, dout:dout + 1],
                    in1=g.xln[dout][:, lo:lo + 512], op0=OP.add, op1=OP.add))
                p.emit("vector", lambda v, dout=dout, lo=lo: v.tensor_tensor(
                    out=g.stream[dout][:, lo:lo + 512],
                    in0=g.stream[dout][:, lo:lo + 512], in1=otmp[:], op=OP.add),
                    inc=s_pev[b])
                bank_free[b] = s_pev[b].val
        p.barrier()


def build_moe(g, seq):
    nc, p, e = g.nc, g.p, g.e
    NTT = 4
    with ExitStack() as ctx:
        ee = ctx.enter_context
        logits = ee(nc.sbuf_tensor(f"m{seq}lg", [8, T], F32))
        cfT = ee(nc.sbuf_tensor(f"m{seq}cf", [9, T], BF16))
        cftok = ee(nc.sbuf_tensor(f"m{seq}ct", [128, 8], F32))
        ttmp = ee(nc.sbuf_tensor(f"m{seq}tt", [128, 8], F32))
        m1 = ee(nc.sbuf_tensor(f"m{seq}m1", [128, 1], F32))
        m2 = ee(nc.sbuf_tensor(f"m{seq}m2", [128, 1], F32))
        cfb = [ee(nc.sbuf_tensor(f"m{seq}cb{j}", [128, 512], BF16)) for j in range(2)]
        hsil = [ee(nc.sbuf_tensor(f"m{seq}hs{j}", [128, 512], BF16)) for j in range(2)]
        h_all = [ee(nc.sbuf_tensor(f"m{seq}h{i}", [128, 512], BF16)) for i in range(36)]
        wst = [ee(nc.sbuf_tensor(f"m{seq}ws{j}", [128, 1024], BF16)) for j in range(2)]
        wdd = [ee(nc.sbuf_tensor(f"m{seq}wd{j}", [128, 36 * 128], BF16)) for j in range(2)]
        ps_lg = ee(nc.psum_tensor(f"m{seq}plg", [8, 512], F32))
        ps_tr = ee(nc.psum_tensor(f"m{seq}ptr", [128, 256], F32))
        ps_cfb = ee(nc.psum_tensor(f"m{seq}pcb", [128, 512], F32))
        ps_up = [ee(nc.psum_tensor(f"m{seq}pu{j}", [128, 512], F32)) for j in range(2)]
        ps_dn = [ee(nc.psum_tensor(f"m{seq}pd{i}", [128, 512], F32)) for i in range(4)]

        s_pmm = p.S("pmm")
        s_pev = [p.S("pev0"), p.S("pev1")]
        s_wld = [p.S("wld0"), p.S("wld1")]
        s_lg = p.S("qk")
        s_lge = p.S("exp")
        s_act = p.S("msk")
        s_tr = p.S("tr")
        s_cf = p.S("rz")
        s_cfT = p.S("rzb")
        s_bcm = p.S("bcm")
        s_cfbe = p.S("msz")
        s_sil = p.S("sil")
        s_hm = p.S("hm")
        s_dn = p.S("dn")
        s_dnev = p.S("dnev")

        b_lg = s_lg.val; b_lge = s_lge.val; b_act = s_act.val; b_tr = s_tr.val
        b_cf = s_cf.val; b_cfT = s_cfT.val
        for tile in range(NTT):
            lo = tile * 512
            if tile > 0:
                p.wait("tensor", s_lge, b_lge + tile)
            for c in range(DC):
                p.emit("tensor", lambda t, c=c, lo=lo: t.matmul(
                    out=ps_lg[0:8, :], lhsT=g.rw_sb[:, c * 8:(c + 1) * 8],
                    rhs=g.xln[c][:, lo:lo + 512],
                    start=(c == 0), stop=(c == DC - 1)),
                    inc=(s_lg if c == DC - 1 else None))
            p.wait("scalar", s_lg, b_lg + tile + 1)
            p.emit("scalar", lambda sc, lo=lo: sc.activation(
                out=logits[:, lo:lo + 512], in_=ps_lg[0:8, :], func=AF.Identity,
                bias=g.ebias_sb[0:8, 0:1], scale=1.0), inc=s_lge)
        p.wait("scalar", s_lge, b_lge + NTT)
        p.emit("scalar", lambda sc: sc.activation(
            out=logits[:], in_=logits[:], func=AF.Softplus))
        p.emit("scalar", lambda sc: sc.activation(
            out=logits[:], in_=logits[:], func=AF.Sqrt), inc=s_act)
        p.emit("vector", lambda v: v.memset(cfT[8:9, :], 1.0))
        p.wait("tensor", s_act, b_act + 1)
        for i in range(TT):
            p.emit("tensor", lambda t, i=i: t.transpose(
                out=ps_tr[:, 0:8], in_=logits[0:8, i * 128:(i + 1) * 128],
                identity=g.idf[:]), inc=s_tr)
            p.wait("vector", s_tr, b_tr + 2 * i + 1)
            p.emit("vector", lambda v: v.tensor_copy(out=cftok[:], in_=ps_tr[:, 0:8]))
            p.emit("vector", lambda v: v.tensor_reduce(
                out=m1[:], in_=cftok[:], axis=AX.X, op=OP.max))
            p.emit("vector", lambda v: v.tensor_scalar(
                out=ttmp[:], in0=cftok[:], scalar1=m1[:, 0:1], scalar2=None,
                op0=OP.is_equal))
            p.emit("vector", lambda v: v.scalar_tensor_tensor(
                out=ttmp[:], in0=ttmp[:], scalar=-1e30, in1=cftok[:],
                op0=OP.mult, op1=OP.add))
            p.emit("vector", lambda v: v.tensor_reduce(
                out=m2[:], in_=ttmp[:], axis=AX.X, op=OP.max))
            p.emit("vector", lambda v: v.tensor_scalar(
                out=ttmp[:], in0=cftok[:], scalar1=m2[:, 0:1], scalar2=None,
                op0=OP.is_ge))
            p.emit("vector", lambda v: v.tensor_tensor(
                out=cftok[:], in0=cftok[:], in1=ttmp[:], op=OP.mult),
                inc=s_cf)
            p.wait("tensor", s_cf, b_cf + i + 1)
            if i > 0:
                p.wait("tensor", s_cfT, b_cfT + i)
            p.emit("tensor", lambda t, i=i: t.transpose(
                out=ps_tr[0:8, 128:256], in_=cftok[:], identity=g.idf[:]),
                inc=s_tr)
            p.wait("vector", s_tr, b_tr + 2 * i + 2)
            p.emit("vector", lambda v, i=i: v.tensor_copy(
                out=cfT[0:8, i * 128:(i + 1) * 128], in_=ps_tr[0:8, 128:256]),
                inc=s_cfT)
        p.barrier()

        wst_free = [0, 0]
        wdd_free = [0, 0]
        upb_free = [0, 0]
        dnb_free = [0, 0, 0, 0]
        b_bcm = s_bcm.val
        b_cfbe = s_cfbe.val
        nbc = [0]
        hm_rec = []
        sil_rec = []
        nop = [0]

        for tile in range(NTT):
            lo = tile * 512
            for ei in range(9):
                if ei < 8:
                    jb = nbc[0] % 2
                    if nbc[0] >= 2:
                        p.wait("tensor", s_cfbe, b_cfbe + nbc[0] - 1)
                    p.emit("tensor", lambda t, ei=ei, lo=lo: t.matmul(
                        out=ps_cfb[:], lhsT=g.orb[0:1, :],
                        rhs=cfT[ei:ei + 1, lo:lo + 512], start=True, stop=True),
                        inc=s_bcm)
                    bcv = s_bcm.val
                    p.wait("vector", s_bcm, bcv)
                    p.emit("vector", lambda v, jb=jb: v.tensor_copy(
                        out=cfb[jb][:], in_=ps_cfb[:]), inc=s_cfbe)
                    nbc[0] += 1
                for hc in range(4):
                    jj = nop[0] % 2
                    if wst_free[jj] > 0:
                        p.wait("sync", s_pmm, wst_free[jj])
                    for c in range(DC):
                        if ei < 8:
                            srcap = g.stg_rW1[ei * 8 + c]
                        else:
                            srcap = g.stg_sW1[c]
                        p.emit("sync", lambda s, jj=jj, c=c, hc=hc, srcap=srcap:
                               s.dma_start(
                                   out=wst[jj][:, c * 128:(c + 1) * 128],
                                   in_=srcap[:, hc * 128:(hc + 1) * 128]),
                               inc=s_wld[0], incn=16)
                    wldv = s_wld[0].val
                    bu = nop[0] % 2
                    if upb_free[bu] > 0:
                        p.wait("tensor", s_sil, upb_free[bu])
                    p.wait("tensor", s_wld[0], wldv)
                    for c in range(DC):
                        p.emit("tensor", lambda t, c=c, bu=bu, jj=jj, lo=lo: t.matmul(
                            out=ps_up[bu][:], lhsT=wst[jj][:, c * 128:(c + 1) * 128],
                            rhs=g.xln[c][:, lo:lo + 512],
                            start=(c == 0), stop=(c == DC - 1)),
                            inc=(s_pmm if c == DC - 1 else None))
                    wst_free[jj] = s_pmm.val
                    mmv = s_pmm.val
                    p.wait("scalar", s_pmm, mmv)
                    nsl = nop[0]
                    if nsl >= 2:
                        p.wait("scalar", s_hm, hm_rec[nsl - 2])
                    bias = (g.bias_rb1[:, ei * 4 + hc:ei * 4 + hc + 1] if ei < 8
                            else g.bias_sb1[:, hc:hc + 1])
                    js = nsl % 2
                    p.emit("scalar", lambda sc, bu=bu, js=js, bias=bias: sc.activation(
                        out=hsil[js][:], in_=ps_up[bu][:], func=AF.Silu,
                        bias=bias, scale=1.0), inc=s_sil)
                    upb_free[bu] = s_sil.val
                    silv = s_sil.val
                    hix = ei * 4 + hc
                    p.wait("vector", s_sil, silv)
                    if ei < 8:
                        jb_use = (nbc[0] - 1) % 2
                        p.wait("vector", s_cfbe, s_cfbe.val)
                        p.emit("vector", lambda v, hix=hix, js=js, jb_use=jb_use:
                               v.tensor_tensor(
                                   out=h_all[hix][:], in0=hsil[js][:],
                                   in1=cfb[jb_use][:], op=OP.mult),
                               inc=s_hm)
                    else:
                        p.emit("vector", lambda v, hix=hix, js=js: v.tensor_copy(
                            out=h_all[hix][:], in_=hsil[js][:]), inc=s_hm)
                    hm_rec.append(s_hm.val)
                    nop[0] += 1
            hm_all = s_hm.val
            p.wait("tensor", s_hm, hm_all)
            for dh in range(2):
                for dc4 in range(4):
                    dcg = dh * 4 + dc4
                    if dnb_free[dc4] > 0:
                        p.wait("tensor", s_dnev, dnb_free[dc4])
                    jj = dcg % 2
                    if wdd_free[jj] > 0:
                        p.wait("sync", s_dn, wdd_free[jj])
                    for ei in range(9):
                        for hc in range(4):
                            if ei < 8:
                                srcap = g.stg_rW2[ei * 4 + hc]
                            else:
                                srcap = g.stg_sW2[hc]
                            nst = ei * 4 + hc
                            p.emit("sync", lambda s, jj=jj, srcap=srcap, dcg=dcg,
                                   nst=nst: s.dma_start(
                                       out=wdd[jj][:, nst * 128:(nst + 1) * 128],
                                       in_=srcap[:, dcg * 128:(dcg + 1) * 128]),
                                   inc=s_wld[1], incn=16)
                    p.wait("tensor", s_wld[1], s_wld[1].val)
                    for nst in range(36):
                        p.emit("tensor", lambda t, jj=jj, dc4=dc4, nst=nst:
                               t.matmul(
                                   out=ps_dn[dc4][:],
                                   lhsT=wdd[jj][:, nst * 128:(nst + 1) * 128],
                                   rhs=h_all[nst][:],
                                   start=(nst == 0), stop=False),
                               inc=(s_dn if nst == 35 else None))
                    wdd_free[jj] = s_dn.val
                    p.emit("tensor", lambda t, dc4=dc4, dcg=dcg, lo=lo: t.matmul(
                        out=ps_dn[dc4][:], lhsT=g.rb2aug_sb[0:9, dcg * 128:(dcg + 1) * 128],
                        rhs=cfT[0:9, lo:lo + 512], start=False, stop=True),
                        inc=s_dn)
                    dnv = s_dn.val
                    p.wait("vector", s_dn, dnv)
                    p.emit("vector", lambda v, dc4=dc4, dcg=dcg, lo=lo: v.tensor_tensor(
                        out=g.stream[dcg][:, lo:lo + 512],
                        in0=g.stream[dcg][:, lo:lo + 512], in1=ps_dn[dc4][:],
                        op=OP.add), inc=s_dnev)
                    dnb_free[dc4] = s_dnev.val
        p.barrier()



def kernel(**inputs):
    xfull = np.ascontiguousarray(inputs["L0_attn_x"])

    def bias_t(v):
        return np.ascontiguousarray(
            np.asarray(v).reshape(DC, 128).T.astype(np.float32))

    consts = dict(
        ident_f=np.eye(128, dtype=np.float32),
        ident_b=np.eye(128).astype(ml_dtypes.bfloat16),
        ones_col_f=np.ones((128, 1), np.float32),
        ones_col_b=np.ones((128, 1), ml_dtypes.bfloat16),
        ones_row_f=np.ones((1, 128), np.float32),
        ones_row_b=np.ones((1, 128), ml_dtypes.bfloat16),
        maskL=np.tril(np.ones((128, 128))).astype(ml_dtypes.bfloat16),
        maskR=np.triu(np.ones((128, 128))).astype(ml_dtypes.bfloat16),
        wq=np.ascontiguousarray(np.asarray(inputs["L0_attn_wq"], np.float32)),
        wk=np.ascontiguousarray(np.asarray(inputs["L0_attn_wk"], np.float32)),
        wv=np.ascontiguousarray(np.asarray(inputs["L0_attn_wv"], np.float32)),
        wo=np.ascontiguousarray(np.asarray(inputs["L0_attn_wo"], np.float32)),
        bq_t=bias_t(inputs["L0_attn_bq"]),
        bk_t=bias_t(inputs["L0_attn_bk"]),
        bv_t=bias_t(inputs["L0_attn_bv"]),
        bo_t=bias_t(inputs["L0_attn_bo"]),
        bv_row=np.ascontiguousarray(
            np.asarray(inputs["L0_attn_bv"], np.float32).reshape(1, D)),
        wg_i=np.ascontiguousarray(np.asarray(inputs["L0_ffn_w_gate"], np.int32)),
        wu_i=np.ascontiguousarray(np.asarray(inputs["L0_ffn_w_up"], np.int32)),
        wd_i=np.ascontiguousarray(np.asarray(inputs["L0_ffn_w_down"], np.int32)),
        win_f=np.ascontiguousarray(np.asarray(inputs["L1_attn_Win"], np.float32)),
        wout_f=np.ascontiguousarray(np.asarray(inputs["L1_attn_Wout"], np.float32)),
        A_t=np.ascontiguousarray(
            np.asarray(inputs["L1_attn_A"], np.float32)
            .reshape(16, 128, 16, 16).transpose(1, 0, 2, 3).reshape(128, 4096)),
        Bm_t=np.ascontiguousarray(
            np.asarray(inputs["L1_attn_Bm"], np.float32)[:, :, 0]
            .reshape(16, 128, 16).transpose(1, 0, 2).reshape(128, 256)),
        Cm_t=np.ascontiguousarray(
            np.asarray(inputs["L1_attn_Cm"], np.float32)[:, 0, :]
            .reshape(16, 128, 16).transpose(1, 0, 2).reshape(128, 256)),
        c4_t=np.ascontiguousarray(
            np.asarray(inputs["L1_attn_conv_w"], np.float32)
            .reshape(2048, 4)[:, ::-1]
            .reshape(16, 128, 4).transpose(1, 0, 2).reshape(128, 64)),
        bin_t=np.ascontiguousarray(
            np.asarray(inputs["L1_attn_b_in"], np.float32).reshape(16, 128).T),
        cb_t=np.ascontiguousarray(
            np.asarray(inputs["L1_attn_conv_b"], np.float32).reshape(16, 128).T),
        bout_t=bias_t(inputs["L1_attn_b_out"]),
        nw_t=bias_t(inputs["L1_attn_norm_w"]),
        sW1_f=np.ascontiguousarray(np.asarray(inputs["L1_ffn_sW1"], np.float32)),
        sW2_f=np.ascontiguousarray(np.asarray(inputs["L1_ffn_sW2"], np.float32)),
        rW1_f=np.ascontiguousarray(
            np.asarray(inputs["L1_ffn_rW1"], np.float32).reshape(8 * D, 512)),
        rW2_f=np.ascontiguousarray(
            np.asarray(inputs["L1_ffn_rW2"], np.float32).reshape(8 * 512, D)),
        rw_t=np.ascontiguousarray(
            np.asarray(inputs["L1_ffn_routing_W"], np.float32).T
            .reshape(8, 128, 8).transpose(1, 0, 2).reshape(128, 64)),
        ebias_t=np.ascontiguousarray(
            np.asarray(inputs["L1_ffn_expert_bias"], np.float32).reshape(8, 1)),
        sb1_t=np.ascontiguousarray(
            np.asarray(inputs["L1_ffn_sb1"], np.float32).reshape(4, 128).T),
        rb1_t=np.ascontiguousarray(
            np.asarray(inputs["L1_ffn_rb1"], np.float32).reshape(8, 4, 128)
            .transpose(2, 0, 1).reshape(128, 32)),
        rb2aug=np.ascontiguousarray(np.concatenate(
            [np.asarray(inputs["L1_ffn_rb2"], np.float32),
             np.asarray(inputs["L1_ffn_sb2"], np.float32).reshape(1, D)], axis=0)),
    )

    nc = build_program()
    in_maps = []
    for core in range(8):
        m = dict(consts)
        m["x"] = np.ascontiguousarray(xfull[core * NSEQ:(core + 1) * NSEQ])
        in_maps.append(m)
    res = run_bass_kernel_spmd(nc, in_maps, core_ids=list(range(8)))
    global LAST_DEBUG
    LAST_DEBUG = [{k: np.asarray(v) for k, v in r.items() if k.startswith("dbg_")}
                  for r in res.results]
    outs = [np.asarray(r["out"]) for r in res.results]
    return np.concatenate(outs, axis=0).reshape(16, 2048, 1024)


LAST_DEBUG = None


# version 14
# speedup vs baseline: 1.5466x; 1.0009x over previous
import numpy as np
import ml_dtypes
from contextlib import ExitStack
import concourse.bass as bass
import concourse.mybir as mybir
from concourse.bass_utils import run_bass_kernel_spmd
from concourse.alu_op_type import AluOpType as OP

F32 = mybir.dt.float32
BF16 = mybir.dt.bfloat16
I32 = mybir.dt.int32
AF = mybir.ActivationFunctionType
AX = mybir.AxisListType

D = 1024
T = 2048
NSEQ = 2
DC = D // 128
TT = T // 128
NH = 16
HD = 64
WIN = 128
SLAB = 512
NSLAB = T // SLAB

ENGINES = ("sync", "tensor", "vector", "scalar", "gpsimd")

PHASES = dict(l0a=True, l0f=True, l1=True, moe=True)
DBG = []



class Sem:
    def __init__(self, h, name):
        self.h = h
        self.name = name
        self.val = 0


class Prog:
    def __init__(self, nc, ctx):
        self.nc = nc
        self.ctx = ctx
        self.q = {e: [] for e in ENGINES}
        self.bar_sem = None
        self.marks = []

    def mark(self, name):
        self.marks.append((name, {e: sum(1 for it in self.q[e] if it[0] == "emit")
                                  for e in ENGINES}))

    def sem(self, name):
        return Sem(self.ctx.enter_context(self.nc.semaphore(name)), name)

    def S(self, role):
        if not hasattr(self, "_pool"):
            self._pool = {}
        if role not in self._pool:
            self._pool[role] = self.sem(f"sp_{role}")
        return self._pool[role]

    def wait(self, eng, sem, val=None):
        v = sem.val if val is None else val
        if v > 0:
            self.q[eng].append(("wait", sem.h, v))

    def emit(self, eng, fn, inc=None, incn=1):
        if inc is not None:
            inc.val += incn
        self.q[eng].append(("emit", fn, inc, incn))

    def barrier(self):
        if self.bar_sem is None:
            self.bar_sem = self.sem("s_bar")
        s = self.bar_sem
        for e in ENGINES:
            self.emit(e, lambda eng: eng.drain(), inc=s, incn=1)
        tgt = s.val
        for e in ENGINES:
            self.wait(e, s, tgt)
        if getattr(self, "mark_tile", None) is not None:
            mt = self.mark_tile
            self.emit("scalar", lambda sc: sc.activation(
                out=mt[0:1, 0:1], in_=mt[0:1, 0:1], func=AF.Sign))

    def replay(self, block):
        def mk(engname):
            def body(eng):
                for item in self.q[engname]:
                    if item[0] == "wait":
                        _, h, v = item
                        eng.wait_ge(h, v)
                    else:
                        _, fn, inc, incn = item
                        instr = fn(eng)
                        if inc is not None:
                            if instr is None:
                                instr = eng.drain()
                            instr.then_inc(inc.h, incn)
            return body

        for engname in ENGINES:
            getattr(block, engname)(mk(engname))


def chunk_list(e, nc, name, cols, dtype, n=DC):
    return [e(nc.sbuf_tensor(f"{name}{c}", [128, cols], dtype)) for c in range(n)]


class Ctx:
    pass


def dbg_dump_chunks(g, name, tiles, cols, dtype=F32):
    if name not in DBG:
        return
    nc, p = g.nc, g.p
    dram = nc.dram_tensor(f"dbg_{name}", [len(tiles), 128, cols], dtype,
                          kind="ExternalOutput").ap()
    p.barrier()
    s = p.S("dbg")
    for i, t in enumerate(tiles):
        p.emit("gpsimd", lambda gp, i=i, t=t: gp.dma_start(
            out=dram[i], in_=t[:, 0:cols]), inc=s, incn=16)
    p.wait("gpsimd", s)
    p.barrier()



def build_program():
    nc = bass.Bass("TRN2", target_bir_lowering=False)
    g = Ctx()
    g.nc = nc

    g.x_in = nc.dram_tensor("x", [NSEQ, T, D], F32, kind="ExternalInput").ap()
    g.out_hbm = nc.dram_tensor("out", [NSEQ, T, D], F32, kind="ExternalOutput").ap()

    def inp(name, shape, dt=F32):
        return nc.dram_tensor(name, shape, dt, kind="ExternalInput").ap()

    g.wq = inp("wq", [D, D]); g.wk = inp("wk", [D, D])
    g.wv = inp("wv", [D, D]); g.wo = inp("wo", [D, D])
    g.bq_t = inp("bq_t", [128, DC]); g.bk_t = inp("bk_t", [128, DC])
    g.bv_t = inp("bv_t", [128, DC]); g.bo_t = inp("bo_t", [128, DC])
    g.bv_row = inp("bv_row", [1, D])
    g.ident_f = inp("ident_f", [128, 128])
    g.ident_b = inp("ident_b", [128, 128], BF16)
    g.ones_col_f = inp("ones_col_f", [128, 1])
    g.ones_col_b = inp("ones_col_b", [128, 1], BF16)
    g.ones_row_f = inp("ones_row_f", [1, 128])
    g.ones_row_b = inp("ones_row_b", [1, 128], BF16)
    g.maskL_in = inp("maskL", [128, 128], BF16)
    g.maskR_in = inp("maskR", [128, 128], BF16)

    g.stg_wq = nc.dram_tensor("stg_wq", [DC, 128, D], BF16, kind="Internal").ap()
    g.stg_wk = nc.dram_tensor("stg_wk", [DC, 128, D], BF16, kind="Internal").ap()
    g.stg_wv = nc.dram_tensor("stg_wv", [DC, 128, D], BF16, kind="Internal").ap()
    g.stg_wo = nc.dram_tensor("stg_wo", [DC, 128, D], BF16, kind="Internal").ap()
    g.wg_i = inp("wg_i", [3 * D, D], I32)
    g.wu_i = inp("wu_i", [3 * D, D], I32)
    g.wd_i = inp("wd_i", [D, 3 * D], I32)
    g.stg_wgT = nc.dram_tensor("stg_wgT", [DC, 128, 3 * D], BF16, kind="Internal").ap()
    g.stg_wuT = nc.dram_tensor("stg_wuT", [DC, 128, 3 * D], BF16, kind="Internal").ap()
    g.stg_wdT = nc.dram_tensor("stg_wdT", [3 * DC, 128, D], BF16, kind="Internal").ap()
    g.win_f = inp("win_f", [D, 2 * D])
    g.wout_f = inp("wout_f", [2 * D, D])
    g.A_t = inp("A_t", [128, 16 * 256])
    g.Bm_t = inp("Bm_t", [128, 256])
    g.Cm_t = inp("Cm_t", [128, 256])
    g.c4_t = inp("c4_t", [128, 64])
    g.bin_t = inp("bin_t", [128, 16])
    g.cb_t = inp("cb_t", [128, 16])
    g.bout_t = inp("bout_t", [128, DC])
    g.nw_t = inp("nw_t", [128, DC])
    g.stg_win = nc.dram_tensor("stg_win", [DC, 128, 2 * D], BF16, kind="Internal").ap()
    g.stg_wout = nc.dram_tensor("stg_wout", [16, 128, D], BF16, kind="Internal").ap()
    g.stg_Vt = nc.dram_tensor("stg_Vt", [128, 16 * 256], BF16, kind="Internal").ap()
    g.stg_Wt = nc.dram_tensor("stg_Wt", [128, 16 * 256], BF16, kind="Internal").ap()
    g.sW1_f = inp("sW1_f", [D, 512])
    g.sW2_f = inp("sW2_f", [512, D])
    g.rW1_f = inp("rW1_f", [8 * D, 512])
    g.rW2_f = inp("rW2_f", [8 * 512, D])
    g.rw_t = inp("rw_t", [128, 64])
    g.ebias_t = inp("ebias_t", [8, 1])
    g.sb1_t = inp("sb1_t", [128, 4])
    g.rb1_t = inp("rb1_t", [128, 32])
    g.rb2aug = inp("rb2aug", [9, D])
    g.stg_sW1 = nc.dram_tensor("stg_sW1", [DC, 128, 512], BF16, kind="Internal").ap()
    g.stg_sW2 = nc.dram_tensor("stg_sW2", [4, 128, D], BF16, kind="Internal").ap()
    g.stg_rW1 = nc.dram_tensor("stg_rW1", [64, 128, 512], BF16, kind="Internal").ap()
    g.stg_rW2 = nc.dram_tensor("stg_rW2", [32, 128, D], BF16, kind="Internal").ap()

    with ExitStack() as ctx:
        e = ctx.enter_context
        p = Prog(nc, ctx)
        g.p = p
        g.e = e

        g.idf = e(nc.sbuf_tensor("idf", [128, 128], F32))
        p.mark_tile = e(nc.sbuf_tensor("marker", [1, 1], F32))
        g.idb = e(nc.sbuf_tensor("idb", [128, 128], BF16))
        g.ocf = e(nc.sbuf_tensor("ocf", [128, 1], F32))
        g.ocb = e(nc.sbuf_tensor("ocb", [128, 1], BF16))
        g.orf = e(nc.sbuf_tensor("orf", [1, 128], F32))
        g.orb = e(nc.sbuf_tensor("orb", [1, 128], BF16))
        g.maskL = e(nc.sbuf_tensor("sb_maskL", [128, 128], BF16))
        g.maskR = e(nc.sbuf_tensor("sb_maskR", [128, 128], BF16))
        g.bias_q = e(nc.sbuf_tensor("bias_q", [128, DC], F32))
        g.bias_k = e(nc.sbuf_tensor("bias_k", [128, DC], F32))
        g.bias_v = e(nc.sbuf_tensor("bias_v", [128, DC], F32))
        g.bias_o = e(nc.sbuf_tensor("bias_o", [128, DC], F32))
        g.bv_row_b = e(nc.sbuf_tensor("bv_row_b", [1, D], BF16))
        g.kv = e(nc.sbuf_tensor("tkv", [128, 16 * 16], F32))
        g.AQT = e(nc.sbuf_tensor("tAQT", [128, 16 * 256], BF16))
        g.ceff = e(nc.sbuf_tensor("tceff", [128, 16], F32))
        g.corr3 = e(nc.sbuf_tensor("tcorr3", [128, 48], F32))
        g.cl = e(nc.sbuf_tensor("tcl", [128, 64], F32))
        g.bias_bout = e(nc.sbuf_tensor("bias_bout", [128, DC], F32))
        g.nw_sb = e(nc.sbuf_tensor("nw_sb", [128, DC], F32))
        g.rw_sb = e(nc.sbuf_tensor("rw_sb", [128, 64], BF16))
        g.ebias_sb = e(nc.sbuf_tensor("ebias_sb", [8, 1], F32))
        g.bias_sb1 = e(nc.sbuf_tensor("bias_sb1", [128, 4], F32))
        g.bias_rb1 = e(nc.sbuf_tensor("bias_rb1", [128, 32], F32))
        g.rb2aug_sb = e(nc.sbuf_tensor("rb2aug_sb", [9, D], BF16))
        g.czero = e(nc.sbuf_tensor("czero", [128, 1], F32))
        g.cn4 = e(nc.sbuf_tensor("cn4", [128, 1], F32))
        nc.const_aps.register(F32, 0.0, g.czero[:, 0:1])
        nc.const_aps.register(F32, -4.0, g.cn4[:, 0:1])

        s_c = p.sem("s_const")
        for dst, src in [(g.idf, g.ident_f), (g.idb, g.ident_b),
                         (g.ocf, g.ones_col_f), (g.ocb, g.ones_col_b),
                         (g.orf, g.ones_row_f), (g.orb, g.ones_row_b),
                         (g.maskL, g.maskL_in), (g.maskR, g.maskR_in),
                         (g.bias_q, g.bq_t), (g.bias_k, g.bk_t),
                         (g.bias_v, g.bv_t), (g.bias_o, g.bo_t)]:
            p.emit("sync", lambda s, d=dst, x=src: s.dma_start(out=d[:], in_=x),
                   inc=s_c, incn=16)
        bv_row_f = e(nc.sbuf_tensor("bv_row_f", [1, D], F32))
        p.emit("sync", lambda s: s.dma_start(out=bv_row_f[:], in_=g.bv_row),
               inc=s_c, incn=16)
        p.emit("sync", lambda s: s.dma_start(out=g.nw_sb[:], in_=g.nw_t),
               inc=s_c, incn=16)
        for eng in ("tensor", "vector", "scalar"):
            p.wait(eng, s_c)
        p.emit("vector", lambda v: v.memset(g.czero[:], 0.0))
        p.emit("vector", lambda v: v.memset(g.cn4[:], -4.0))
        p.emit("vector", lambda v: v.tensor_copy(out=g.bv_row_b[:], in_=bv_row_f[:]))
        p.emit("vector", lambda v: v.tensor_scalar(
            out=g.bias_q[:], in0=g.bias_q[:], scalar1=0.125, scalar2=None,
            op0=OP.mult))

        p.mark("start")
        with ExitStack() as pctx:
            pe = pctx.enter_context
            g.pwi = [pe(nc.sbuf_tensor(f"prep_i{j}", [128, 3 * D], I32)) for j in range(4)]
            g.pwb = [pe(nc.sbuf_tensor(f"prep_o{j}", [128, 3 * D], BF16)) for j in range(4)]
            g.ptb = [pe(nc.sbuf_tensor(f"prep_t{j}", [128, 128], BF16)) for j in range(4)]
            g.pps = [pe(nc.psum_tensor(f"prep_ps{j}", [128, 128], BF16)) for j in range(2)]
            if PHASES["l1"]:
                build_prep_l1_tables(g, pctx.enter_context)
            build_prep_staging(g)
            if PHASES["l1"]:
                build_prep_l1_ship(g)
            p.barrier()
        g.stream = chunk_list(e, nc, "stream", T, F32)
        g.xln = chunk_list(e, nc, "xln", T, BF16)
        g.rstd_bc = e(nc.sbuf_tensor("rstd_bc", [128, T], BF16))
        g.mrstd_bc = e(nc.sbuf_tensor("mrstd_bc", [128, T], BF16))

        g.s_out = p.sem("s_out")
        for seq in range(NSEQ):
            p.mark(f"prep_done{seq}")
            build_load_stream(g, seq)
            p.mark(f"load{seq}")
            p.barrier()
            if PHASES["l0a"]:
                build_ln(g, seq, tag=f"ln0a{seq}")
                p.mark(f"ln0a{seq}")
                p.barrier()
                if seq == 0:
                    dbg_dump_chunks(g, "xln0", g.xln, T, BF16)
                build_l0a(g, seq)
                p.mark(f"l0a{seq}")
                p.barrier()
            if PHASES["l0f"]:
                build_ln(g, seq, tag=f"ln0f{seq}")
                p.mark(f"ln0f{seq}")
                p.barrier()
                if seq == 0:
                    dbg_dump_chunks(g, "xln1", g.xln, T, BF16)
                build_l0f(g, seq)
                p.mark(f"l0f{seq}")
                p.barrier()
                if seq == 0:
                    dbg_dump_chunks(g, "post_ffn", g.stream, T, F32)
            build_store_stream(g, seq)
            p.barrier()

        p.wait("gpsimd", p.S("so0"), p.S("so0").val)
        p.wait("gpsimd", p.S("so1"), p.S("so1").val)

        block = e(nc.Block())
        p.replay(block)

    global LAST_MARKS
    LAST_MARKS = p.marks
    return nc


LAST_MARKS = None


def build_l0f(g, seq):
    nc, p, e = g.nc, g.p, g.e
    HHALF = 1024
    NHC = HHALF // 128
    NTT = 4
    with ExitStack() as ctx:
        ee = ctx.enter_context
        wgT = chunk_list(ee, nc, f"f{seq}wg", HHALF, BF16)
        wuT = chunk_list(ee, nc, f"f{seq}wu", HHALF, BF16)
        wdT = [ee(nc.sbuf_tensor(f"f{seq}wd{i}", [128, D], BF16)) for i in range(NHC)]
        hsb = [ee(nc.sbuf_tensor(f"f{seq}h{i}", [128, 512], BF16)) for i in range(NHC)]
        sil = [ee(nc.sbuf_tensor(f"f{seq}sl{j}", [128, 512], BF16)) for j in range(2)]
        ps_g = ee(nc.psum_tensor(f"f{seq}psg", [128, 512], F32))
        ps_u = ee(nc.psum_tensor(f"f{seq}psu", [128, 512], F32))
        ps_dn = [ee(nc.psum_tensor(f"f{seq}psd{i}", [128, 512], F32)) for i in range(4)]
        s_wld = p.S("wld0")
        s_gu = p.S("gu")
        s_sil = p.S("sil")
        s_hm = p.S("hm")
        s_dn = p.S("dn")
        s_dnev = p.S("dnev")

        hm_rec = []
        dn_bank_free = [0, 0, 0, 0]
        for half in range(3):
            if half >= 1:
                p.wait("sync", s_dn, s_dn.val)
            wld_base = s_wld.val
            for dc in range(DC):
                p.emit("sync", lambda s, dc=dc, half=half: s.dma_start(
                    out=wgT[dc][:], in_=g.stg_wgT[dc][:, half * HHALF:(half + 1) * HHALF]),
                    inc=s_wld, incn=16)
                p.emit("sync", lambda s, dc=dc, half=half: s.dma_start(
                    out=wuT[dc][:], in_=g.stg_wuT[dc][:, half * HHALF:(half + 1) * HHALF]),
                    inc=s_wld, incn=16)
            for i in range(NHC):
                p.emit("sync", lambda s, i=i, half=half: s.dma_start(
                    out=wdT[i][:], in_=g.stg_wdT[half * NHC + i]),
                    inc=s_wld, incn=16)
            wld_val = s_wld.val
            p.wait("tensor", s_wld, wld_val)
            if seq == 0 and half == 0:
                for eng in ("gpsimd",):
                    p.wait(eng, s_wld, wld_val)
                dbg_dump_chunks(g, "wgT0", wgT, HHALF, BF16)
                dbg_dump_chunks(g, "wdT0", wdT, D, BF16)

            for tt in range(NTT):
                lo = tt * 512
                hm_base = s_hm.val
                for hc in range(NHC):
                    if not (half == 0 and tt == 0 and hc == 0):
                        p.wait("tensor", s_sil, s_sil.val)
                        p.wait("tensor", s_hm, s_hm.val)
                    for dc in range(DC):
                        p.emit("tensor", lambda t, dc=dc, hc=hc, lo=lo: t.matmul(
                            out=ps_g[:], lhsT=wgT[dc][:, hc * 128:(hc + 1) * 128],
                            rhs=g.xln[dc][:, lo:lo + 512],
                            start=(dc == 0), stop=(dc == DC - 1)),
                            inc=(s_gu if dc == DC - 1 else None))
                    guv_g = s_gu.val
                    for dc in range(DC):
                        p.emit("tensor", lambda t, dc=dc, hc=hc, lo=lo: t.matmul(
                            out=ps_u[:], lhsT=wuT[dc][:, hc * 128:(hc + 1) * 128],
                            rhs=g.xln[dc][:, lo:lo + 512],
                            start=(dc == 0), stop=(dc == DC - 1)),
                            inc=(s_gu if dc == DC - 1 else None))
                    guv_u = s_gu.val
                    p.wait("scalar", s_gu, guv_g)
                    nop_ = half * NTT * NHC + tt * NHC + hc
                    sj = nop_ % 2
                    if nop_ >= 2:
                        p.wait("scalar", s_hm, hm_rec[nop_ - 2])
                    p.emit("scalar", lambda sc, sj=sj: sc.activation(
                        out=sil[sj][:], in_=ps_g[:], func=AF.Silu), inc=s_sil)
                    p.wait("vector", s_gu, guv_u)
                    p.wait("vector", s_sil, s_sil.val)
                    if half > 0 or tt > 0:
                        p.wait("vector", s_dn, dn_prev_tt)
                    p.emit("vector", lambda v, sj=sj, hc=hc: v.tensor_tensor(
                        out=hsb[hc][:], in0=sil[sj][:], in1=ps_u[:], op=OP.mult),
                        inc=s_hm)
                    hm_rec.append(s_hm.val)
                hm_val = s_hm.val
                p.wait("tensor", s_hm, hm_val)
                for dh in range(2):
                    for dc4 in range(4):
                        dcg = dh * 4 + dc4
                        if dn_bank_free[dc4] > 0:
                            p.wait("tensor", s_dnev, dn_bank_free[dc4])
                        for hc in range(NHC):
                            p.emit("tensor",
                                   lambda t, hc=hc, dcg=dcg, dc4=dc4: t.matmul(
                                       out=ps_dn[dc4][:], lhsT=wdT[hc][:, dcg * 128:(dcg + 1) * 128],
                                       rhs=hsb[hc][:],
                                       start=(hc == 0), stop=(hc == NHC - 1)),
                                   inc=(s_dn if hc == NHC - 1 else None))
                        dnv = s_dn.val
                        p.wait("vector", s_dn, dnv)
                        p.emit("vector", lambda v, dc4=dc4, dcg=dcg, lo=lo: v.tensor_tensor(
                            out=g.stream[dcg][:, lo:lo + 512], in0=ps_dn[dc4][:],
                            in1=g.stream[dcg][:, lo:lo + 512], op=OP.add), inc=s_dnev)
                        dn_bank_free[dc4] = s_dnev.val
                dn_prev_tt = s_dn.val
                if seq == 0 and half == 0 and tt == 0:
                    dbg_dump_chunks(g, "hsb0", hsb, 512, BF16)
        p.barrier()


def build_prep_l1_tables(g, pe):
    nc, p = g.nc, g.p
    s_tl = p.S("tls")
    A1 = pe(nc.sbuf_tensor("plA1", [128, 4096], BF16))
    A1T = pe(nc.sbuf_tensor("plA1T", [128, 4096], BF16))
    A2 = pe(nc.sbuf_tensor("plA2", [128, 4096], BF16))
    A2T = pe(nc.sbuf_tensor("plA2T", [128, 4096], BF16))
    PRD = pe(nc.sbuf_tensor("plPRD", [128, 4096], BF16))
    Vrev = pe(nc.sbuf_tensor("plVrev", [128, 16 * 256], BF16))
    VtS = pe(nc.sbuf_tensor("plVtS", [128, 16 * 256], BF16))
    WtS = pe(nc.sbuf_tensor("plWtS", [128, 16 * 256], BF16))
    Bmb = pe(nc.sbuf_tensor("plBmb", [128, 256], BF16))
    Cmb = pe(nc.sbuf_tensor("plCmb", [128, 256], BF16))
    smallf = pe(nc.sbuf_tensor("plsm", [128, 544], F32))
    s_afl = p.S("afl")
    s_tl2 = p.S("tl2")
    b_afl = s_afl.val
    b_tl2 = s_tl2.val
    with nc.sbuf_tensor("plAf", [128, 1024], F32) as Af:
        for qt in range(4):
            if qt > 0:
                p.wait("sync", s_tl2, s_tl2.val)
            p.emit("sync", lambda s, qt=qt, Af=Af: s.dma_start(
                out=Af[:], in_=g.A_t[:, qt * 1024:(qt + 1) * 1024]),
                inc=s_afl, incn=16)
            p.wait("vector", s_afl, b_afl + (qt + 1) * 16)
            p.emit("vector", lambda v, qt=qt, Af=Af: v.tensor_copy(
                out=A1[:, qt * 1024:(qt + 1) * 1024], in_=Af[:]),
                inc=s_tl2)
    b_tl = s_tl.val
    p.emit("sync", lambda s: s.dma_start(out=smallf[:, 0:256], in_=g.Bm_t),
           inc=s_tl, incn=16)
    p.emit("sync", lambda s: s.dma_start(out=smallf[:, 256:512], in_=g.Cm_t),
           inc=s_tl, incn=16)
    p.emit("sync", lambda s: s.dma_start(out=g.cl[:, 0:64], in_=g.c4_t),
           inc=s_tl, incn=16)
    p.emit("sync", lambda s: s.dma_start(out=smallf[:, 512:528], in_=g.bin_t),
           inc=s_tl, incn=16)
    p.emit("sync", lambda s: s.dma_start(out=smallf[:, 528:544], in_=g.cb_t),
           inc=s_tl, incn=16)
    p.emit("sync", lambda s: s.dma_start(out=g.bias_bout[:], in_=g.bout_t),
           inc=s_tl, incn=16)
    p.wait("vector", s_tl, b_tl + 16 * 6)

    def ve(fn):
        p.emit("vector", fn)

    ve(lambda v: v.tensor_copy(out=Bmb[:], in_=smallf[:, 0:256]))
    ve(lambda v: v.tensor_copy(out=Cmb[:], in_=smallf[:, 256:512]))
    cl4 = g.cl[:].rearrange("p (d l) -> p d l", l=4)
    ve(lambda v: v.tensor_reduce(
        out=g.ceff[:].rearrange("p (d o) -> p d o", o=1), in_=cl4,
        axis=AX.X, op=OP.add))
    ve(lambda v: v.tensor_tensor(
        out=g.ceff[:], in0=g.ceff[:], in1=smallf[:, 512:528], op=OP.mult))
    ve(lambda v: v.tensor_tensor(
        out=g.ceff[:], in0=g.ceff[:], in1=smallf[:, 528:544], op=OP.add))
    c3 = g.corr3[:].rearrange("p (d t) -> p d t", t=3)
    for t in range(3):
        ve(lambda v, t=t: v.tensor_reduce(
            out=c3[:, :, t:t + 1], in_=cl4[:, :, t + 1:4],
            axis=AX.X, op=OP.add))
    ve(lambda v: v.tensor_tensor(
        out=c3, in0=c3,
        in1=smallf[:, 512:528].rearrange("p (d o) -> p d o", o=1)
        .broadcast_to([128, 16, 3]), op=OP.mult))

    def a_rs(buf):
        return buf[:].rearrange("p (d r s) -> p d r s", r=16, s=16)

    def tree4(pv):
        for lev in (8, 4, 2, 1):
            ve(lambda v, lev=lev, pv=pv: v.tensor_tensor(
                out=pv[:, :, :, 0:lev], in0=pv[:, :, :, 0:lev],
                in1=pv[:, :, :, lev:2 * lev], op=OP.add))

    def transpose_A(dst, srcb):
        ve(lambda v: v.tensor_copy(
            out=a_rs(dst), in_=a_rs(srcb).transpose([0, 1, 3, 2])))

    def square_A(dst, srcN, srcT):
        for dc in range(16):
            pv = PRD[:].rearrange("p (r s gt) -> p r s gt", r=16, s=16)
            srn = a_rs(srcN)[:, dc].unsqueeze(2).broadcast_to([128, 16, 16, 16])
            srt = a_rs(srcT)[:, dc].unsqueeze(1).broadcast_to([128, 16, 16, 16])
            ve(lambda v, srn=srn, srt=srt, pv=pv: v.tensor_tensor(
                out=pv, in0=srn, in1=srt, op=OP.mult))
            tree4(pv)
            ve(lambda v, dc=dc, pv=pv: v.tensor_copy(
                out=a_rs(dst)[:, dc].unsqueeze(3), in_=pv[:, :, :, 0:1]))

    vr = Vrev[:].rearrange("p (d j s) -> p d j s", j=16, s=16)
    wt = WtS[:].rearrange("p (d i s) -> p d i s", i=16, s=16)
    ve(lambda v: v.tensor_copy(
        out=vr[:, :, 0, :], in_=Bmb[:].rearrange("p (d s) -> p d s", s=16)))
    pv0 = PRD[:].rearrange("p (d r st) -> p d r st", r=16, st=16)[:, 0:16]
    cb = Cmb[:].rearrange("p (d s) -> p d s", s=16).unsqueeze(2)\
        .broadcast_to([128, 16, 16, 16])
    ve(lambda v: v.tensor_tensor(out=pv0, in0=a_rs(A1), in1=cb, op=OP.mult))
    tree4(pv0)
    ve(lambda v: v.tensor_copy(
        out=wt[:, :, 0, :].unsqueeze(3), in_=pv0[:, :, :, 0:1]))

    transpose_A(A1T, A1)
    curN, curT, nxtN, nxtT = A1, A1T, A2, A2T
    m = 1
    while True:
        for dc in range(16):
            pj = PRD[:].rearrange("p (j s rt) -> p j s rt", s=16, rt=16)[:, 0:m]
            i0 = vr[:, dc, 0:m, :].unsqueeze(2).broadcast_to([128, m, 16, 16])
            i1 = a_rs(curT)[:, dc].unsqueeze(1).broadcast_to([128, m, 16, 16])
            ve(lambda v, i0=i0, i1=i1, pj=pj: v.tensor_tensor(
                out=pj, in0=i0, in1=i1, op=OP.mult))
            tree4(pj)
            ve(lambda v, dc=dc, m=m, pj=pj: v.tensor_copy(
                out=vr[:, dc, m:2 * m, :].unsqueeze(3), in_=pj[:, :, :, 0:1]))
            pj2 = PRD[:].rearrange("p (j r st) -> p j r st", r=16, st=16)[:, 0:m]
            k0 = wt[:, dc, 0:m, :].unsqueeze(2).broadcast_to([128, m, 16, 16])
            k1 = a_rs(curN)[:, dc].unsqueeze(1).broadcast_to([128, m, 16, 16])
            ve(lambda v, k0=k0, k1=k1, pj2=pj2: v.tensor_tensor(
                out=pj2, in0=k0, in1=k1, op=OP.mult))
            tree4(pj2)
            ve(lambda v, dc=dc, m=m, pj2=pj2: v.tensor_copy(
                out=wt[:, dc, m:2 * m, :].unsqueeze(3), in_=pj2[:, :, :, 0:1]))
        if m == 8:
            break
        square_A(nxtN, curN, curT)
        transpose_A(nxtT, nxtN)
        curN, curT, nxtN, nxtT = nxtN, nxtT, curN, curT
        m *= 2
    square_A(nxtN, curN, curT)
    ve(lambda v: v.tensor_copy(
        out=g.AQT[:].rearrange("p (d s r) -> p d s r", s=16, r=16),
        in_=a_rs(nxtN).transpose([0, 1, 3, 2])))
    kvv = g.kv[:].rearrange("p (d t) -> p d t", t=16)
    for dc in range(16):
        pk = PRD[:].rearrange("p (t st) -> p t st", st=16)[:, 0:16]
        i1 = Cmb[:].rearrange("p (d s) -> p d s", s=16)[:, dc].unsqueeze(1)\
            .broadcast_to([128, 16, 16])
        ve(lambda v, dc=dc, pk=pk, i1=i1: v.tensor_tensor(
            out=pk, in0=vr[:, dc], in1=i1, op=OP.mult))
        for lev in (8, 4, 2, 1):
            ve(lambda v, lev=lev, pk=pk: v.tensor_tensor(
                out=pk[:, :, 0:lev], in0=pk[:, :, 0:lev],
                in1=pk[:, :, lev:2 * lev], op=OP.add))
        ve(lambda v, dc=dc, pk=pk: v.tensor_copy(
            out=kvv[:, dc].unsqueeze(2), in_=pk[:, :, 0:1]))
    vt = VtS[:].rearrange("p (d s i) -> p d s i", s=16, i=16)
    vr_svi = vr.transpose([0, 1, 3, 2])
    for i in range(16):
        ve(lambda v, i=i: v.tensor_copy(
            out=vt[:, :, :, i:i + 1], in_=vr_svi[:, :, :, 15 - i:16 - i]))
    g._VtS = VtS
    g._WtS = WtS


def build_prep_staging(g):
    nc, p = g.nc, g.p
    s_ld = [p.S("ld0"), p.S("ld1"), p.S("ld2"), p.S("ld3")]
    s_cast = p.S("cast")
    s_tr = p.S("tr")
    s_ev = p.S("ev")
    s_st = [p.S("st0"), p.S("st1"), p.S("st2"), p.S("st3")]
    s_tb = [p.S("tb0"), p.S("tb1"), p.S("tb2"), p.S("tb3")]
    s_tl = p.S("tl")
    b_ld = [s.val for s in s_ld]
    b_cast = s_cast.val
    b_tr = s_tr.val
    b_ev = s_ev.val
    b_st = [s.val for s in s_st]
    b_tb = [s.val for s in s_tb]
    b_tl = s_tl.val

    with ExitStack() as c2:
        smf = c2.enter_context(nc.sbuf_tensor("pm_sm", [128, 64], F32))
        r9f = c2.enter_context(nc.sbuf_tensor("pm_r9", [9, D], F32))
        p.wait("sync", p.S("tl2"), p.S("tl2").val)
        p.emit("sync", lambda s: s.dma_start(out=smf[:], in_=g.rw_t), inc=s_tl, incn=16)
        p.emit("sync", lambda s: s.dma_start(out=r9f[:], in_=g.rb2aug), inc=s_tl, incn=16)
        p.emit("sync", lambda s: s.dma_start(out=g.ebias_sb[:], in_=g.ebias_t), inc=s_tl, incn=16)
        p.emit("sync", lambda s: s.dma_start(out=g.bias_sb1[:], in_=g.sb1_t), inc=s_tl, incn=16)
        p.emit("sync", lambda s: s.dma_start(out=g.bias_rb1[:], in_=g.rb1_t), inc=s_tl, incn=16)
        p.wait("scalar", s_tl, b_tl + 16 * 5)
        p.emit("scalar", lambda sc: sc.copy(out=g.rw_sb[:], in_=smf[:]))
        p.emit("scalar", lambda sc: sc.copy(out=g.rb2aug_sb[:], in_=r9f[:]))

        jobs = []
        for (w_in, w_stg, scl) in ((g.wq, g.stg_wq, 0.125), (g.wk, g.stg_wk, None),
                                   (g.wv, g.stg_wv, None), (g.wo, g.stg_wo, None)):
            for c in range(DC):
                jobs.append((w_in[c * 128:(c + 1) * 128, :], D, False, scl,
                             "direct", (w_stg, c)))
        for c in range(DC):
            jobs.append((g.win_f[c * 128:(c + 1) * 128, :], 2 * D, False, ("nw", c),
                         "direct", (g.stg_win, c)))
        for c in range(16):
            jobs.append((g.wout_f[c * 128:(c + 1) * 128, :], D, False, None,
                         "direct", (g.stg_wout, c)))
        for c in range(DC):
            jobs.append((g.sW1_f[c * 128:(c + 1) * 128, :], 512, False, None,
                         "direct", (g.stg_sW1, c)))
        for c in range(4):
            jobs.append((g.sW2_f[c * 128:(c + 1) * 128, :], D, False, None,
                         "direct", (g.stg_sW2, c)))
        for i in range(64):
            jobs.append((g.rW1_f[i * 128:(i + 1) * 128, :], 512, False, None,
                         "direct", (g.stg_rW1, i)))
        for i in range(32):
            jobs.append((g.rW2_f[i * 128:(i + 1) * 128, :], D, False, None,
                         "direct", (g.stg_rW2, i)))
        for (w_in, w_stg) in ((g.wg_i, g.stg_wgT), (g.wu_i, g.stg_wuT)):
            for hc in range(3 * DC):
                jobs.append((w_in[hc * 128:(hc + 1) * 128, :], D, True, 1.0 / 127.0,
                             "transpose", (w_stg, DC, hc)))
        for dc in range(DC):
            jobs.append((g.wd_i[dc * 128:(dc + 1) * 128, :], 3 * D, True, 1.0 / 127.0,
                         "transpose", (g.stg_wdT, 3 * DC, dc)))

        cast_after = []
        tr_after = []
        st_after = []
        nt = 0
        nst = 0
        for jn, (src_ap, ncols, is_int, scl, kind, args) in enumerate(jobs):
            j = jn % 4
            if jn >= 4:
                p.wait("sync", s_cast, b_cast + cast_after[jn - 4] + 1)
            inbuf = g.pwi[j][:, 0:ncols] if is_int else \
                g.pwi[j][:].bitcast(F32)[:, 0:ncols]
            p.emit("sync", lambda s, inbuf=inbuf, src_ap=src_ap: s.dma_start(
                out=inbuf, in_=src_ap), inc=s_ld[j], incn=16)
            p.wait("scalar", s_ld[j], b_ld[j] + (jn // 4 + 1) * 16)
            if jn >= 4:
                p.wait("scalar", s_st[j], s_st[j].val)
                if tr_after[jn - 4] > 0:
                    p.wait("scalar", s_tr, b_tr + tr_after[jn - 4])
            ob = g.pwb[j][:, 0:ncols]
            if isinstance(scl, tuple):
                cslice = g.nw_sb[:, scl[1]:scl[1] + 1]
                p.emit("scalar", lambda sc, inbuf=inbuf, ob=ob, cs=cslice:
                       sc.activation(out=ob, in_=inbuf, func=AF.Identity,
                                     scale=cs), inc=s_cast)
            elif scl is not None:
                p.emit("scalar", lambda sc, inbuf=inbuf, ob=ob, scl=scl: sc.mul(
                    out=ob, in_=inbuf, mul=scl), inc=s_cast)
            else:
                p.emit("scalar", lambda sc, inbuf=inbuf, ob=ob: sc.copy(
                    out=ob, in_=inbuf), inc=s_cast)
            castv = s_cast.val - b_cast - 1
            if kind == "direct":
                w_stg, oc = args
                p.wait("gpsimd", s_cast, b_cast + castv + 1)
                p.emit("gpsimd", lambda gp, w_stg=w_stg, oc=oc, j=j, ncols=ncols:
                       gp.dma_start(out=w_stg[oc], in_=g.pwb[j][:, 0:ncols]),
                       inc=s_st[j], incn=16)
                nst += 1
            else:
                w_stg, n_inner, rc = args
                for ic in range(n_inner):
                    bank = nt % 2
                    tbuf = nt % 4
                    p.wait("tensor", s_cast, b_cast + castv + 1)
                    if nt >= 2:
                        p.wait("tensor", s_ev, b_ev + nt - 1)
                    p.emit("tensor", lambda t, j=j, ic=ic, bank=bank: t.transpose(
                        out=g.pps[bank][:], in_=g.pwb[j][:, ic * 128:(ic + 1) * 128],
                        identity=g.idb[:]), inc=s_tr)
                    p.wait("scalar", s_tr, s_tr.val)
                    p.wait("scalar", s_tb[tbuf], s_tb[tbuf].val)
                    p.emit("scalar", lambda sc, bank=bank, tbuf=tbuf: sc.copy(
                        out=g.ptb[tbuf][:], in_=g.pps[bank][:]), inc=s_ev)
                    p.wait("gpsimd", s_ev, b_ev + nt + 1)
                    p.emit("gpsimd",
                           lambda gp, w_stg=w_stg, ic=ic, rc=rc, tbuf=tbuf:
                           gp.dma_start(out=w_stg[ic][:, rc * 128:(rc + 1) * 128],
                                        in_=g.ptb[tbuf][:]), inc=s_tb[tbuf], incn=16)
                    nt += 1
                    nst += 1
            cast_after.append(castv)
            tr_after.append(nt)
            st_after.append(nst)
    g.prep_done = None


def build_prep_l1_ship(g):
    nc, p = g.nc, g.p
    s_tf = p.S("tl2")
    p.emit("vector", lambda v: v.drain(), inc=s_tf)
    p.wait("gpsimd", s_tf, s_tf.val)
    p.emit("gpsimd", lambda gp: gp.dma_start(out=g.stg_Vt, in_=g._VtS[:]),
           inc=p.S("st"), incn=16)
    p.emit("gpsimd", lambda gp: gp.dma_start(out=g.stg_Wt, in_=g._WtS[:]),
           inc=p.S("st"), incn=16)
    p.wait("gpsimd", p.S("st"), p.S("st").val)


def build_load_stream(g, seq):
    nc, p, e = g.nc, g.p, g.e
    with ExitStack() as ctx:
        ee = ctx.enter_context
        tok = [ee(nc.sbuf_tensor(f"ld_tok{seq}_{j}", [128, D], F32)) for j in range(2)]
        ps = [ee(nc.psum_tensor(f"ld_ps{seq}_{j}", [128, 128], F32)) for j in range(2)]
        s_ld = [p.S("ld0"), p.S("ld1")]
        s_tr = p.S("tr")
        s_ev = p.S("ev")
        b_ld = [s_ld[0].val, s_ld[1].val]
        b_tr = s_tr.val
        b_ev = s_ev.val

        for i in range(TT):
            j = i % 2
            if i >= 2:
                p.wait("sync", s_tr, b_tr + (i - 1) * DC)
            p.emit("sync", lambda s, j=j, i=i: s.dma_start(
                out=tok[j][:], in_=g.x_in[seq, i * 128:(i + 1) * 128, :]),
                inc=s_ld[j], incn=16)
            for c in range(DC):
                n = i * DC + c
                bank = n % 2
                p.wait("tensor", s_ld[j], b_ld[j] + (i // 2 + 1) * 16)
                if n >= 2:
                    p.wait("tensor", s_ev, b_ev + n - 1)
                p.emit("tensor", lambda t, j=j, c=c, bank=bank: t.transpose(
                    out=ps[bank][:], in_=tok[j][:, c * 128:(c + 1) * 128],
                    identity=g.idf[:]), inc=s_tr)
                p.wait("vector", s_tr, b_tr + n + 1)
                p.emit("vector", lambda v, c=c, i=i, bank=bank: v.tensor_copy(
                    out=g.stream[c][:, i * 128:(i + 1) * 128], in_=ps[bank][:]),
                    inc=s_ev)
        p.barrier()


def build_store_stream(g, seq):
    nc, p, e = g.nc, g.p, g.e
    with ExitStack() as ctx:
        ee = ctx.enter_context
        otok = [ee(nc.sbuf_tensor(f"st_tok{seq}_{j}", [128, D], F32)) for j in range(2)]
        ps = [ee(nc.psum_tensor(f"st_ps{seq}_{j}", [128, 128], F32)) for j in range(2)]
        s_tr = p.S("tr")
        s_ev = p.S("ev")
        s_so = [p.S("so0"), p.S("so1")]
        b_tr = s_tr.val
        b_ev = s_ev.val

        for i in range(TT):
            j = i % 2
            for c in range(DC):
                n = i * DC + c
                bank = n % 2
                if n >= 2:
                    p.wait("tensor", s_ev, b_ev + n - 1)
                p.emit("tensor", lambda t, c=c, i=i, bank=bank: t.transpose(
                    out=ps[bank][:], in_=g.stream[c][:, i * 128:(i + 1) * 128],
                    identity=g.idf[:]), inc=s_tr)
                p.wait("vector", s_tr, b_tr + n + 1)
                if i >= 2 and c == 0:
                    p.wait("vector", s_so[j], s_so[j].val)
                p.emit("vector", lambda v, c=c, bank=bank, j=j: v.tensor_copy(
                    out=otok[j][:, c * 128:(c + 1) * 128], in_=ps[bank][:]),
                    inc=s_ev)
            p.wait("gpsimd", s_ev, b_ev + (i + 1) * DC)
            p.emit("gpsimd", lambda gp, i=i, j=j: gp.dma_start(
                out=g.out_hbm[seq, i * 128:(i + 1) * 128, :], in_=otok[j][:]),
                inc=s_so[j], incn=16)
            g.s_out.val += 16
        p.barrier()


def build_ln(g, seq, tag):
    nc, p, e = g.nc, g.p, g.e
    NP = 4
    with ExitStack() as ctx:
        ee = ctx.enter_context
        xsq = [ee(nc.sbuf_tensor(f"{tag}_xsq{j}", [128, 512], BF16)) for j in range(2)]
        mean = [ee(nc.sbuf_tensor(f"{tag}_mean{j}", [1, 512], F32)) for j in range(2)]
        vtmp = [ee(nc.sbuf_tensor(f"{tag}_vtmp{j}", [1, 512], F32)) for j in range(2)]
        m2 = ee(nc.sbuf_tensor(f"{tag}_m2", [1, 512], F32))
        rstd_pc = [ee(nc.sbuf_tensor(f"{tag}_rstd{j}", [1, 512], F32)) for j in range(2)]
        mrstd_pc = [ee(nc.sbuf_tensor(f"{tag}_mrstd{j}", [1, 512], F32)) for j in range(2)]
        ps_m = [ee(nc.psum_tensor(f"{tag}_psm{j}", [1, 512], F32)) for j in range(2)]
        ps_s = [ee(nc.psum_tensor(f"{tag}_pss{j}", [1, 512], F32)) for j in range(2)]
        ps_b1 = [ee(nc.psum_tensor(f"{tag}_psb1{j}", [128, 512], F32)) for j in range(2)]
        ps_b2 = [ee(nc.psum_tensor(f"{tag}_psb2{j}", [128, 512], F32)) for j in range(2)]
        s_sq = p.S("ln_sq")
        s_x2 = p.S("ln_x2")
        s_mm = p.S("ln_mm")
        s_cmb = p.S("ln_cmb")
        s_sqrt = p.S("ln_sqrt")
        s_fin = p.S("ln_fin")
        s_bc = p.S("ln_bc")
        s_ev = p.S("ln_ev")
        b_sq = s_sq.val; b_x2 = s_x2.val; b_mm = s_mm.val; b_cmb = s_cmb.val
        b_sqrt = s_sqrt.val; b_fin = s_fin.val; b_bc = s_bc.val; b_ev = s_ev.val

        def head(piece):
            lo = piece * 512
            j = piece % 2
            for c in range(DC):
                n = piece * DC + c
                if n >= 2:
                    p.wait("scalar", s_x2, b_x2 + n - 1)
                if c == 0 and piece >= 2:
                    p.wait("scalar", s_cmb, b_cmb + piece - 1)
                p.emit("scalar", lambda sc, c=c, lo=lo, jx=n % 2: sc.activation(
                    out=xsq[jx][:], in_=g.stream[c][:, lo:lo + 512], func=AF.Square),
                    inc=s_sq)
                p.wait("tensor", s_sq, b_sq + n + 1)
                if c == 0 and piece >= 2:
                    p.wait("tensor", s_cmb, b_cmb + piece - 1)
                p.emit("tensor", lambda t, c=c, lo=lo, j=j, first=(c == 0): t.matmul(
                    out=ps_m[j][0:1, :], lhsT=g.ocf[:, 0:1],
                    rhs=g.stream[c][:, lo:lo + 512],
                    start=first, stop=(c == DC - 1)))
                p.emit("tensor", lambda t, jx=n % 2, j=j, first=(c == 0),
                       last=(c == DC - 1): t.matmul(
                           out=ps_s[j][0:1, :], lhsT=g.ocb[:, 0:1], rhs=xsq[jx][:],
                           start=first, stop=last), inc=s_x2)
                if c == DC - 1:
                    p.emit("tensor", lambda t: None, inc=s_mm)

        def tail(piece):
            lo = piece * 512
            j = piece % 2
            p.wait("vector", s_mm, b_mm + piece + 1)
            p.emit("vector", lambda v, j=j: v.tensor_scalar(
                out=mean[j][:], in0=ps_m[j][0:1, :], scalar1=1.0 / D, scalar2=None,
                op0=OP.mult))
            p.emit("vector", lambda v, j=j: v.tensor_scalar(
                out=vtmp[j][:], in0=ps_s[j][0:1, :], scalar1=1.0 / D, scalar2=1e-5,
                op0=OP.mult, op1=OP.add))
            p.emit("vector", lambda v, j=j: v.tensor_tensor(
                out=m2[:], in0=mean[j][:], in1=mean[j][:], op=OP.mult))
            p.emit("vector", lambda v, j=j: v.tensor_tensor(
                out=vtmp[j][:], in0=vtmp[j][:], in1=m2[:], op=OP.subtract),
                inc=s_cmb)
            p.wait("scalar", s_cmb, b_cmb + piece + 1)
            p.emit("scalar", lambda sc, j=j: sc.activation(
                out=rstd_pc[j][:], in_=vtmp[j][:], func=AF.Ln))
            p.emit("scalar", lambda sc, j=j: sc.activation(
                out=rstd_pc[j][:], in_=rstd_pc[j][:], func=AF.Exp, scale=-0.5),
                inc=s_sqrt)
            p.wait("vector", s_sqrt, b_sqrt + piece + 1)
            p.emit("vector", lambda v, j=j: v.scalar_tensor_tensor(
                out=mrstd_pc[j][:], in0=mean[j][:], scalar=-1.0, in1=rstd_pc[j][:],
                op0=OP.mult, op1=OP.mult), inc=s_fin)
            p.wait("tensor", s_fin, b_fin + piece + 1)
            if piece >= 2:
                p.wait("tensor", s_ev, b_ev + (piece - 1) * 2)
            p.emit("tensor", lambda t, j=j: t.matmul(
                out=ps_b1[j][:], lhsT=g.orf[0:1, :], rhs=rstd_pc[j][:],
                start=True, stop=True))
            p.emit("tensor", lambda t, j=j: t.matmul(
                out=ps_b2[j][:], lhsT=g.orf[0:1, :], rhs=mrstd_pc[j][:],
                start=True, stop=True), inc=s_bc)
            p.wait("vector", s_bc, b_bc + piece + 1)
            p.emit("vector", lambda v, lo=lo, j=j: v.tensor_copy(
                out=g.rstd_bc[:, lo:lo + 512], in_=ps_b1[j][:]), inc=s_ev)
            p.emit("vector", lambda v, lo=lo, j=j: v.tensor_copy(
                out=g.mrstd_bc[:, lo:lo + 512], in_=ps_b2[j][:]), inc=s_ev)

        head(0)
        head(1)
        for piece in range(NP):
            if piece + 2 < NP:
                head(piece + 2)
            tail(piece)
        for c in range(DC):
            p.emit("vector", lambda v, c=c: v.tensor_tensor(
                out=g.xln[c][:], in0=g.stream[c][:], in1=g.rstd_bc[:],
                op=OP.mult))
            p.emit("vector", lambda v, c=c: v.tensor_tensor(
                out=g.xln[c][:], in0=g.xln[c][:], in1=g.mrstd_bc[:],
                op=OP.add))
        p.barrier()


def build_l0a(g, seq):
    nc, p, e = g.nc, g.p, g.e
    with ExitStack() as ctx:
        ee = ctx.enter_context
        q_T = chunk_list(ee, nc, f"a{seq}q", SLAB, BF16)
        k_T = chunk_list(ee, nc, f"a{seq}k", SLAB + 256, BF16)
        v_tok = [ee(nc.sbuf_tensor(f"a{seq}v{t}", [128, D], BF16)) for t in range(6)]
        attn_T = chunk_list(ee, nc, f"a{seq}o", SLAB, BF16)
        probs = [ee(nc.sbuf_tensor(f"a{seq}pr{j}", [128, 768], BF16)) for j in range(4)]
        rz = [ee(nc.sbuf_tensor(f"a{seq}rz{j}", [1, 256], F32)) for j in range(2)]
        zsb = [ee(nc.sbuf_tensor(f"a{seq}zs{j}", [1, 256], F32)) for j in range(2)]
        rzb16 = [ee(nc.sbuf_tensor(f"a{seq}rzb16{j}", [1, 256], BF16)) for j in range(2)]
        rzb = [ee(nc.sbuf_tensor(f"a{seq}rzb{j}", [128, 128], F32)) for j in range(2)]
        wslc = [ee(nc.sbuf_tensor(f"a{seq}ws{j}", [128, D], BF16)) for j in range(2)]
        wvf = chunk_list(ee, nc, f"a{seq}wv", D, BF16)
        ps_proj = [ee(nc.psum_tensor(f"a{seq}pp{j}", [128, 512], F32)) for j in range(2)]
        ps_sc2 = [ee(nc.psum_tensor(f"a{seq}psc{j}", [128, 384], F32)) for j in range(2)]
        ps_zb = [ee(nc.psum_tensor(f"a{seq}pz{j}", [128, 256], F32)) for j in range(2)]
        ps_av = [ee(nc.psum_tensor(f"a{seq}pa{j}", [128, 128], F32)) for j in range(2)]
        dbg_pr = ee(nc.sbuf_tensor(f"a{seq}dbgpr", [128, 768], BF16))
        dbg_rzb = ee(nc.sbuf_tensor(f"a{seq}dbgrzb", [128, 128], F32))
        dbg_av = ee(nc.sbuf_tensor(f"a{seq}dbgav", [128, 128], F32))
        dbg_z = ee(nc.sbuf_tensor(f"a{seq}dbgz", [128, 256], F32))

        ps_sc = [ps_sc2[0], ps_sc2[1], None, None]
        s_wld = [p.S("wld0"), p.S("wld1")]
        s_pmm = p.S("pmm")
        s_pev = [p.S("pev0"), p.S("pev1")]
        s_qk = p.S("qk")
        s_exp = p.S("exp")
        s_msk = p.S("msk")
        s_msz = p.S("msz")
        s_zmm = p.S("zmm")
        s_rz = p.S("rz")
        s_bcm = p.S("bcm")
        s_rzb = p.S("rzb")
        s_av = p.S("av")
        s_fin = p.S("afin")

        wseq = {0: g.stg_wq, 1: g.stg_wk, 2: g.stg_wv, 3: g.stg_wo}
        wslc_free = [0, 0]
        wvf_free = [0]
        nslc = [0]

        def load_slices(widx, m):
            j = nslc[0] % 2
            nslc[0] += 1
            if wslc_free[j] > 0:
                p.wait("sync", s_pmm, wslc_free[j])
            for c in range(DC):
                p.emit("sync", lambda s, j=j, c=c, w=wseq[widx], m=m: s.dma_start(
                    out=wslc[j][:, c * 128:(c + 1) * 128],
                    in_=w[c][:, m * 128:(m + 1) * 128]), inc=s_wld[0], incn=16)
            return j, s_wld[0].val

        pfill = [0]
        bank_ev = [s_pev[0].val, s_pev[1].val]

        def proj_guard():
            b = pfill[0] % 2
            if bank_ev[b] > 0:
                p.wait("tensor", s_pev[b], bank_ev[b])
            return b

        def proj(widx, m, mov_cols, evac):
            wj, wldv = load_slices(widx, m)
            b = proj_guard()
            p.wait("tensor", s_wld[0], wldv)
            lo, n = mov_cols
            for kc in range(DC):
                p.emit("tensor",
                       lambda t, kc=kc, b=b, lo=lo, n=n, wj=wj: t.matmul(
                           out=ps_proj[b][:, 0:n],
                           lhsT=wslc[wj][:, kc * 128:(kc + 1) * 128],
                           rhs=g.xln[kc][:, lo:lo + n],
                           start=(kc == 0), stop=(kc == DC - 1)),
                       inc=(s_pmm if kc == DC - 1 else None))
            wslc_free[wj] = s_pmm.val
            evac(b, s_pmm.val)
            bank_ev[b] = s_pev[b].val
            pfill[0] += 1

        def mk_evac_s(dst, dcols, bias, m):
            def ev(b, mmv):
                p.wait("scalar", s_pmm, mmv)
                lo, n = dcols
                p.emit("scalar", lambda sc, b=b, lo=lo, n=n, m=m: sc.activation(
                    out=dst[:, lo:lo + n], in_=ps_proj[b][:, 0:n], func=AF.Identity,
                    bias=bias[:, m:m + 1], scale=1.0), inc=s_pev[b])
            return ev

        def proj_tok_t6(tile_tok_lo, tt_ix, wv_ldv):
            for b in (0, 1):
                if bank_ev[b] > 0:
                    p.wait("tensor", s_pev[b], bank_ev[b])
            p.wait("tensor", s_wld[1], wv_ldv)
            for kc in range(DC):
                for half in (0, 1):
                    p.emit("tensor",
                           lambda t, kc=kc, half=half, lo=tile_tok_lo: t.matmul(
                               out=ps_proj[half][:, 0:512],
                               lhsT=g.xln[kc][:, lo:lo + 128],
                               rhs=wvf[kc][:, half * 512:(half + 1) * 512],
                               start=(kc == 0), stop=False),
                           inc=(s_pmm if half == 1 else None))
            for half in (0, 1):
                p.emit("tensor", lambda t, half=half: t.matmul(
                    out=ps_proj[half][:, 0:512], lhsT=g.orb[0:1, 0:128],
                    rhs=g.bv_row_b[0:1, half * 512:half * 512 + 512],
                    start=False, stop=True), inc=s_pmm)
                mmv = s_pmm.val
                p.wait("vector", s_pmm, mmv)
                p.emit("vector", lambda v, half=half, tt=tt_ix: v.tensor_copy(
                    out=v_tok[tt][:, half * 512:(half + 1) * 512],
                    in_=ps_proj[half][:, 0:512]), inc=s_pev[half])
                bank_ev[half] = s_pev[half].val
            pfill[0] += 2

        ps_sc[2] = ps_proj[0][:, 0:384] if False else None
        pair_rec = {}
        gp = 0

        def sc_tile(u):
            j = u % 4
            if j < 2:
                return ps_sc[j][:]
            return ps_proj[j - 2][:, 0:384]

        for s in range(NSLAB):
            body_lo = s * SLAB
            wlo = body_lo - 128
            if wvf_free[0] > 0:
                p.wait("sync", s_pmm, wvf_free[0])
            for kc in range(DC):
                p.emit("sync", lambda sy, kc=kc: sy.dma_start(
                    out=wvf[kc][:], in_=g.stg_wv[kc]), inc=s_wld[1], incn=16)
            wv_ldv = s_wld[1].val
            for m in range(DC):
                proj(0, m, (body_lo, SLAB), mk_evac_s(q_T[m], (0, SLAB), g.bias_q, m))
            vlo = max(0, -wlo)
            vhi = 768 - max(0, (wlo + 768) - T)
            nmsz = 0
            if vlo > 0:
                for m in range(DC):
                    p.emit("vector", lambda v, m=m, vlo=vlo: v.memset(
                        k_T[m][:, 0:vlo], 0.0), inc=s_msz)
                for t6 in range(vlo // 128):
                    p.emit("vector", lambda v, t6=t6: v.memset(v_tok[t6][:], 0.0),
                           inc=s_msz)
            if vhi < 768:
                for m in range(DC):
                    p.emit("vector", lambda v, m=m, vhi=vhi: v.memset(
                        k_T[m][:, vhi:768], 0.0), inc=s_msz)
                for t6 in range(vhi // 128, 6):
                    p.emit("vector", lambda v, t6=t6: v.memset(v_tok[t6][:], 0.0),
                           inc=s_msz)
            msz_val = s_msz.val
            for m in range(DC):
                wj, wldv = load_slices(1, m)
                p.wait("tensor", s_wld[0], wldv)
                for ti in range(2):
                    blo = max(vlo, ti * 384)
                    bhi = min(vhi, (ti + 1) * 384)
                    if bhi <= blo:
                        continue
                    b = proj_guard()
                    n = bhi - blo
                    lo = wlo + blo
                    for kc in range(DC):
                        p.emit("tensor",
                               lambda t, kc=kc, b=b, lo=lo, n=n, wj=wj: t.matmul(
                                   out=ps_proj[b][:, 0:n],
                                   lhsT=wslc[wj][:, kc * 128:(kc + 1) * 128],
                                   rhs=g.xln[kc][:, lo:lo + n],
                                   start=(kc == 0), stop=(kc == DC - 1)),
                               inc=(s_pmm if kc == DC - 1 else None))
                    mk_evac_s(k_T[m], (blo, n), g.bias_k, m)(b, s_pmm.val)
                    bank_ev[b] = s_pev[b].val
                    pfill[0] += 1
                wslc_free[wj] = s_pmm.val
            for t6 in range(6):
                tok_lo = wlo + t6 * 128
                if tok_lo < 0 or tok_lo >= T:
                    continue
                proj_tok_t6(tok_lo, t6, wv_ldv)
            wvf_free[0] = s_pmm.val

            kqv_ev0, kqv_ev1 = s_pev[0].val, s_pev[1].val
            if seq == 0 and s == 0:
                dbg_dump_chunks(g, "q0", q_T, SLAB, BF16)
                dbg_dump_chunks(g, "k0", k_T, SLAB + 256, BF16)
                dbg_dump_chunks(g, "v0", v_tok, D, BF16)

            pairs = [(bb, hp) for bb in range(4) for hp in range(DC)]
            npairs = len(pairs)
            base_u = gp

            def emit_qk(u):
                bb, hp = pairs[u - base_u]
                if u >= 4:
                    r = pair_rec[u - 4]
                    p.wait("tensor", s_exp, r["exp"])
                if u - base_u <= 1:
                    p.wait("tensor", s_pev[0], kqv_ev0)
                    p.wait("tensor", s_pev[1], kqv_ev1)
                    p.wait("tensor", s_msz, msz_val)
                rec = pair_rec.setdefault(u, {})
                sct = sc_tile(u)
                for half in range(2):
                    for kc in range(3):
                        p.emit("tensor",
                               lambda t, half=half, kc=kc, bb=bb, hp=hp, sct=sct:
                               t.matmul(
                                   out=sct[:, kc * 128:(kc + 1) * 128],
                                   lhsT=k_T[hp][half * 64:(half + 1) * 64,
                                                (bb + kc) * 128:(bb + kc + 1) * 128],
                                   rhs=q_T[hp][half * 64:(half + 1) * 64,
                                               bb * 128:(bb + 1) * 128],
                                   start=(kc == 0), stop=(kc == 2),
                                   skip_group_check=True),
                               inc=(s_qk if kc == 2 else None))
                    rec[f"qk{half}"] = s_qk.val
                    if half == 0:
                        p.wait("tensor", s_exp, s_exp.val + 1)

            def emit_expmask(u):
                pp = u % 4
                rec = pair_rec[u]
                sct = sc_tile(u)
                for half in range(2):
                    if u >= 4:
                        p.wait("scalar", s_av, pair_rec[u - 4]["av"])
                    p.wait("scalar", s_qk, rec[f"qk{half}"])
                    p.emit("scalar", lambda sc, pp=pp, half=half, sct=sct:
                           sc.activation(
                               out=probs[pp][:, half * 384:(half + 1) * 384],
                               in_=sct, func=AF.Exp, bias=g.cn4[:, 0:1],
                               scale=1.0), inc=s_exp)
                rec["exp"] = s_exp.val
                p.wait("vector", s_exp, s_exp.val)
                for half in range(2):
                    p.emit("vector", lambda v, pp=pp, half=half: v.tensor_tensor(
                        out=probs[pp][:, half * 384:half * 384 + 128],
                        in0=probs[pp][:, half * 384:half * 384 + 128],
                        in1=g.maskL[:], op=OP.mult))
                    p.emit("vector", lambda v, pp=pp, half=half: v.tensor_tensor(
                        out=probs[pp][:, half * 384 + 256:half * 384 + 384],
                        in0=probs[pp][:, half * 384 + 256:half * 384 + 384],
                        in1=g.maskR[:], op=OP.mult),
                        inc=(s_msk if half == 1 else None))
                rec["msk"] = s_msk.val

            def emit_pe_z(u):
                pp = u % 2
                rec = pair_rec[u]
                p.wait("tensor", s_msk, rec["msk"])
                if u >= 2:
                    p.wait("tensor", s_rzb, pair_rec[u - 2]["rzb"])
                pr2 = probs[u % 4][:].rearrange("p (h x) -> p h x", h=2)
                for kc in range(3):
                    p.emit("tensor", lambda t, pp=pp, kc=kc, pr2=pr2: t.matmul(
                        out=ps_zb[pp][0:1, :], lhsT=g.ocb[:, 0:1],
                        rhs=pr2[:, :, kc * 128:(kc + 1) * 128],
                        start=(kc == 0), stop=(kc == 2), skip_group_check=True),
                        inc=(s_zmm if kc == 2 else None))
                rec["zmm"] = s_zmm.val

            def emit_pe_av(u):
                pp = u % 2
                p4 = u % 4
                rec = pair_rec[u]
                if u >= 2:
                    p.wait("tensor", s_fin, pair_rec[u - 2]["fin"])
                bb, hp = pairs[u - base_u]
                for half in range(2):
                    for kc in range(3):
                        p.emit("tensor",
                               lambda t, pp=pp, p4=p4, half=half, kc=kc, bb=bb, hp=hp:
                               t.matmul(
                                   out=ps_av[pp][half * 64:(half + 1) * 64, :],
                                   lhsT=v_tok[bb + kc][:, hp * 128 + half * 64:
                                                       hp * 128 + (half + 1) * 64],
                                   rhs=probs[p4][:, half * 384 + kc * 128:
                                                 half * 384 + (kc + 1) * 128],
                                   start=(kc == 0), stop=(kc == 2),
                                   tile_position=(0, half * 64),
                                   skip_group_check=True),
                               inc=(s_av if (half == 1 and kc == 2) else None))
                rec["av"] = s_av.val

            def emit_pe_bc(u):
                pp = u % 2
                rec = pair_rec[u]
                p.wait("tensor", s_rz, rec["rz"])
                p.emit("tensor", lambda t, pp=pp: t.matmul(
                    out=ps_zb[pp][64:128, 0:128], lhsT=g.orb[0:1, 0:64],
                    rhs=rzb16[pp][0:1, 128:256], start=True, stop=True,
                    tile_position=(0, 64), skip_group_check=True))
                p.emit("tensor", lambda t, pp=pp: t.matmul(
                    out=ps_zb[pp][0:64, 0:128], lhsT=g.orb[0:1, 0:64],
                    rhs=rzb16[pp][0:1, 0:128], start=True, stop=True,
                    skip_group_check=True), inc=s_bcm)
                rec["bcm"] = s_bcm.val

            def emit_v_rz(u):
                pp = u % 2
                rec = pair_rec[u]
                p.wait("scalar", s_zmm, rec["zmm"])
                if u >= 2:
                    p.wait("scalar", s_bcm, pair_rec[u - 2]["bcm"])
                p.emit("scalar", lambda sc, pp=pp: sc.activation(
                    out=rz[pp][:], in_=ps_zb[pp][0:1, :], func=AF.Ln))
                p.emit("scalar", lambda sc, pp=pp: sc.activation(
                    out=rzb16[pp][:], in_=rz[pp][:], func=AF.Exp, scale=-1.0),
                    inc=s_rz)
                rec["rz"] = s_rz.val

            def emit_v_fin(u):
                pp = u % 2
                rec = pair_rec[u]
                bb, hp = pairs[u - base_u]
                p.wait("vector", s_bcm, rec["bcm"])
                p.emit("vector", lambda v, pp=pp: v.tensor_copy(
                    out=rzb[pp][:], in_=ps_zb[pp][:, 0:128]), inc=s_rzb)
                rec["rzb"] = s_rzb.val
                p.wait("vector", s_av, rec["av"])
                p.emit("vector", lambda v, pp=pp, hp=hp, bb=bb: v.tensor_tensor(
                    out=attn_T[hp][:, bb * 128:(bb + 1) * 128],
                    in0=ps_av[pp][:], in1=rzb[pp][:], op=OP.mult), inc=s_fin)
                rec["fin"] = s_fin.val

            emit_qk(base_u)
            emit_expmask(base_u)
            if npairs > 1:
                emit_qk(base_u + 1)
                emit_expmask(base_u + 1)
            for u in range(base_u, base_u + npairs):
                emit_pe_z(u)
                emit_v_rz(u)
                if u + 2 < base_u + npairs:
                    emit_qk(u + 2)
                emit_pe_av(u)
                emit_pe_bc(u)
                emit_v_fin(u)
                if u + 2 < base_u + npairs:
                    emit_expmask(u + 2)
            gp += npairs
            p.wait("tensor", s_exp, s_exp.val)

            attn_done = s_fin.val
            for m in range(DC):
                wj, wldv = load_slices(3, m)
                b_ = proj_guard()
                p.wait("tensor", s_wld[0], wldv)
                if m == 0:
                    p.wait("tensor", s_fin, attn_done)
                for kc in range(DC):
                    p.emit("tensor", lambda t, kc=kc, b_=b_, wj=wj: t.matmul(
                        out=ps_proj[b_][:, 0:SLAB],
                        lhsT=wslc[wj][:, kc * 128:(kc + 1) * 128],
                        rhs=attn_T[kc][:, 0:SLAB],
                        start=(kc == 0), stop=(kc == DC - 1)),
                        inc=(s_pmm if kc == DC - 1 else None))
                wslc_free[wj] = s_pmm.val
                mmv = s_pmm.val
                p.wait("vector", s_pmm, mmv)
                p.emit("vector", lambda v, b_=b_, m=m, lo=body_lo: v.scalar_tensor_tensor(
                    out=g.stream[m][:, lo:lo + SLAB], in0=ps_proj[b_][:, 0:SLAB],
                    scalar=g.bias_o[:, m:m + 1], in1=g.stream[m][:, lo:lo + SLAB],
                    op0=OP.add, op1=OP.add), inc=s_pev[b_])
                bank_ev[b_] = s_pev[b_].val
                pfill[0] += 1
        p.barrier()




def build_l1(g, seq):
    nc, p, e = g.nc, g.p, g.e
    Q = 16
    TQ = 512
    NCH = TQ // Q
    NQ = T // TQ
    NDC = 16
    with ExitStack() as ctx:
        ee = ctx.enter_context
        vbuf = ee(nc.sbuf_tensor(f"l1v{seq}", [128, 520], BF16))
        vtails = ee(nc.sbuf_tensor(f"l1vt{seq}", [128, NDC * 4], BF16))
        uc = ee(nc.sbuf_tensor(f"l1u{seq}", [128, TQ], BF16))
        y_all = ee(nc.sbuf_tensor(f"l1y{seq}", [128, NDC * TQ], BF16))
        g_all = ee(nc.sbuf_tensor(f"l1g{seq}", [128, NDC * NCH * 16], BF16))
        h_st = ee(nc.sbuf_tensor(f"l1h{seq}", [128, (NCH + 1) * 256], BF16))
        gp = ee(nc.sbuf_tensor(f"l1p{seq}", [128, 4096], BF16))
        wstat = [ee(nc.sbuf_tensor(f"l1w{seq}_{j}", [128, 2048], BF16)) for j in range(2)]
        vtb = [ee(nc.sbuf_tensor(f"l1vtb{seq}_{j}", [128, 256], BF16)) for j in range(2)]
        wtb = [ee(nc.sbuf_tensor(f"l1wtb{seq}_{j}", [128, 256], BF16)) for j in range(2)]
        xsq = [ee(nc.sbuf_tensor(f"l1xq{seq}_{j}", [128, 512], BF16)) for j in range(2)]
        rpc = ee(nc.sbuf_tensor(f"l1rp{seq}", [1, 512], F32))
        otmp = ee(nc.sbuf_tensor(f"l1ot{seq}", [128, 512], F32))
        ps_w = [ee(nc.psum_tensor(f"l1pw{seq}_{j}", [128, 512], F32)) for j in range(2)]
        ps_ss = ee(nc.psum_tensor(f"l1ps{seq}", [1, 512], F32))
        ps_bc = ee(nc.psum_tensor(f"l1pb{seq}", [128, 512], F32))

        s_pmm = p.S("pmm")
        s_pev = [p.S("pev0"), p.S("pev1")]
        s_wld = [p.S("wld0"), p.S("wld1")]
        s_sq = p.S("ln_sq")
        s_mm = p.S("ln_mm")
        s_cmb = p.S("ln_cmb")
        s_sqrt = p.S("ln_sqrt")
        s_bc = p.S("ln_bc")
        s_ev = p.S("ln_ev")
        s_scy = p.S("scy")
        s_scv = p.S("scv")

        s_x2 = p.S("ln_x2")
        b_sq = s_sq.val; b_mm = s_mm.val; b_cmb = s_cmb.val
        b_sqrt = s_sqrt.val; b_bc = s_bc.val; b_ev = s_ev.val
        b_x2 = s_x2.val
        for piece in range(4):
            lo = piece * 512
            for c in range(DC):
                n = piece * DC + c
                if n >= 2:
                    p.wait("scalar", s_x2, b_x2 + n - 1)
                p.emit("scalar", lambda sc, c=c, lo=lo, j=n % 2: sc.activation(
                    out=xsq[j][:], in_=g.xln[c][:, lo:lo + 512], func=AF.Square),
                    inc=s_sq)
                p.wait("tensor", s_sq, b_sq + n + 1)
                if piece > 0 and c == 0:
                    p.wait("tensor", s_cmb, b_cmb + piece)
                p.emit("tensor", lambda t, j=n % 2, first=(c == 0), last=(c == DC - 1):
                       t.matmul(out=ps_ss[0:1, :], lhsT=g.ocb[:, 0:1], rhs=xsq[j][:],
                                start=first, stop=last),
                       inc=s_x2)
                if c == DC - 1:
                    p.emit("tensor", lambda t: None, inc=s_mm)
            p.wait("vector", s_mm, b_mm + piece + 1)
            p.emit("vector", lambda v: v.tensor_scalar(
                out=rpc[:], in0=ps_ss[0:1, :], scalar1=1.0 / (2 * D), scalar2=1e-6,
                op0=OP.mult, op1=OP.add), inc=s_cmb)
            p.wait("scalar", s_cmb, b_cmb + piece + 1)
            p.emit("scalar", lambda sc: sc.activation(
                out=rpc[:], in_=rpc[:], func=AF.Ln))
            p.emit("scalar", lambda sc: sc.activation(
                out=rpc[:], in_=rpc[:], func=AF.Exp, scale=-0.5), inc=s_sqrt)
            p.wait("tensor", s_sqrt, b_sqrt + piece + 1)
            if piece > 0:
                p.wait("tensor", s_ev, b_ev + piece)
            p.emit("tensor", lambda t: t.matmul(
                out=ps_bc[:], lhsT=g.orf[0:1, :], rhs=rpc[:],
                start=True, stop=True), inc=s_bc)
            p.wait("vector", s_bc, b_bc + piece + 1)
            p.emit("vector", lambda v, lo=lo: v.tensor_copy(
                out=g.rstd_bc[:, lo:lo + 512], in_=ps_bc[:]), inc=s_ev)
        p.barrier()

        kvv = g.kv[:].rearrange("p (d t) -> p d t", t=16)
        hsv = h_st[:].rearrange("p (c d s) -> p c d s", d=NDC, s=16)
        gav = g_all[:].rearrange("p (d c s) -> p d c s", c=NCH, s=16)
        yav = y_all[:].rearrange("p (d t) -> p d t", t=TQ)
        aqv = g.AQT[:].rearrange("p (d s r) -> p d s r", s=16, r=16)

        wst_free = [0, 0]
        vt_free = [0, 0]
        wt_free = [0, 0]
        bank_free = [0, 0]

        p.emit("vector", lambda v: v.memset(h_st[:, 0:256], 0.0))
        p.emit("vector", lambda v: v.memset(vtails[:], 0.0))

        for q in range(NQ):
            lo = q * TQ
            for dc in range(NDC):
                jj = dc % 2
                if wst_free[jj] > 0:
                    p.wait("sync", s_pmm, wst_free[jj])
                for c in range(DC):
                    p.emit("sync", lambda s, jj=jj, c=c, dc=dc: s.dma_start(
                        out=wstat[jj][:, c * 128:(c + 1) * 128],
                        in_=g.stg_win[c][:, dc * 128:(dc + 1) * 128]),
                        inc=s_wld[0], incn=16)
                wldv = s_wld[0].val
                if vt_free[jj] > 0:
                    p.wait("sync", s_scv, vt_free[jj])
                p.emit("sync", lambda s, jj=jj, dc=dc: s.dma_start(
                    out=vtb[jj][:], in_=g.stg_Vt[:, dc * 256:(dc + 1) * 256]),
                    inc=s_wld[1], incn=16)
                vtldv = s_wld[1].val
                b = dc % 2
                if bank_free[b] > 0:
                    p.wait("tensor", s_pev[b], bank_free[b])
                p.wait("tensor", s_wld[0], wldv)
                for c in range(DC):
                    p.emit("tensor", lambda t, c=c, b=b, jj=jj, lo=lo: t.matmul(
                        out=ps_w[b][:], lhsT=wstat[jj][:, c * 128:(c + 1) * 128],
                        rhs=g.xln[c][:, lo:lo + 512],
                        start=(c == 0), stop=(c == DC - 1)),
                        inc=(s_pmm if c == DC - 1 else None))
                wst_free[jj] = s_pmm.val
                mmv = s_pmm.val
                p.wait("vector", s_pmm, mmv)
                if q == 0:
                    p.emit("vector", lambda v: v.memset(vbuf[:, 0:4], 0.0))
                else:
                    p.emit("vector", lambda v, dc=dc: v.tensor_copy(
                        out=vbuf[:, 0:4], in_=vtails[:, dc * 4:(dc + 1) * 4]))
                p.emit("vector", lambda v, b=b, lo=lo: v.tensor_tensor(
                    out=vbuf[:, 4:516], in0=ps_w[b][:],
                    in1=g.rstd_bc[:, lo:lo + 512], op=OP.mult), inc=s_pev[b])
                bank_free[b] = s_pev[b].val
                p.emit("vector", lambda v, dc=dc: v.tensor_copy(
                    out=vtails[:, dc * 4:(dc + 1) * 4], in_=vbuf[:, 512:516]))
                p.emit("vector", lambda v, dc=dc: v.tensor_scalar(
                    out=uc[:], in0=vbuf[:, 1:513],
                    scalar1=g.cl[:, dc * 4 + 3:dc * 4 + 4],
                    scalar2=g.ceff[:, dc:dc + 1], op0=OP.mult, op1=OP.add))
                for lam in (2, 1, 0):
                    p.emit("vector", lambda v, dc=dc, lam=lam: v.scalar_tensor_tensor(
                        out=uc[:], in0=vbuf[:, 4 - lam:516 - lam],
                        scalar=g.cl[:, dc * 4 + lam:dc * 4 + lam + 1], in1=uc[:],
                        op0=OP.mult, op1=OP.add))
                if q == 0:
                    p.emit("vector", lambda v, dc=dc: v.tensor_tensor(
                        out=uc[:, 0:3], in0=uc[:, 0:3],
                        in1=g.corr3[:, dc * 3:dc * 3 + 3], op=OP.subtract))
                ucv = uc[:].rearrange("p (c i) -> p c i", i=Q)
                p.emit("vector", lambda v, dc=dc, ucv=ucv: v.tensor_scalar(
                    out=yav[:, dc].rearrange("p (c i) -> p c i", i=Q),
                    in0=ucv, scalar1=kvv[:, dc, 0:1], scalar2=None, op0=OP.mult))
                for tau in range(1, Q):
                    p.emit("vector", lambda v, dc=dc, tau=tau, ucv=ucv:
                           v.scalar_tensor_tensor(
                               out=yav[:, dc].rearrange("p (c i) -> p c i", i=Q)[:, :, tau:Q],
                               in0=ucv[:, :, 0:Q - tau],
                               scalar=kvv[:, dc, tau:tau + 1],
                               in1=yav[:, dc].rearrange("p (c i) -> p c i", i=Q)[:, :, tau:Q],
                               op0=OP.mult, op1=OP.add))
                p.wait("vector", s_wld[1], vtldv)
                vtv = vtb[jj][:].rearrange("p (s i) -> p s i", i=Q)
                for cg in range(2):
                    pg = gp[:].rearrange("p (c s i) -> p c s i", c=16, s=16)
                    i0 = ucv[:, cg * 16:(cg + 1) * 16].unsqueeze(2)\
                        .broadcast_to([128, 16, 16, Q])
                    i1 = vtv.unsqueeze(1).broadcast_to([128, 16, 16, Q])
                    p.emit("vector", lambda v, pg=pg, i0=i0, i1=i1: v.tensor_tensor(
                        out=pg, in0=i0, in1=i1, op=OP.mult))
                    for lev in (8, 4, 2):
                        p.emit("vector", lambda v, pg=pg, lev=lev: v.tensor_tensor(
                            out=pg[:, :, :, 0:lev], in0=pg[:, :, :, 0:lev],
                            in1=pg[:, :, :, lev:2 * lev], op=OP.add))
                    p.emit("vector", lambda v, pg=pg, dc=dc, cg=cg: v.tensor_tensor(
                        out=gav[:, dc, cg * 16:(cg + 1) * 16].unsqueeze(3),
                        in0=pg[:, :, :, 0:1], in1=pg[:, :, :, 1:2], op=OP.add))
                p.emit("vector", lambda v: v.drain(), inc=s_scv)
                vt_free[jj] = s_scv.val

            for c in range(NCH):
                pg = gp[:].rearrange("p (d s r) -> p d s r", d=NDC, s=16)[:, :, :, 0:16]
                i0 = hsv[:, c].unsqueeze(2).broadcast_to([128, NDC, 16, 16])
                p.emit("vector", lambda v, pg=pg, i0=i0: v.tensor_tensor(
                    out=pg, in0=i0, in1=aqv.unsqueeze(0)[0] if False else aqv,
                    op=OP.mult))
                for lev in (8, 4, 2):
                    p.emit("vector", lambda v, pg=pg, lev=lev: v.tensor_tensor(
                        out=pg[:, :, :, 0:lev], in0=pg[:, :, :, 0:lev],
                        in1=pg[:, :, :, lev:2 * lev], op=OP.add))
                p.emit("vector", lambda v, pg=pg: v.tensor_tensor(
                    out=pg[:, :, :, 0:1], in0=pg[:, :, :, 0:1],
                    in1=pg[:, :, :, 1:2], op=OP.add))
                p.emit("vector", lambda v, c=c, pg=pg: v.tensor_tensor(
                    out=hsv[:, c + 1].unsqueeze(3), in0=pg[:, :, :, 0:1],
                    in1=gav[:, :, c, :].unsqueeze(3), op=OP.add))

            for dc in range(NDC):
                jj = dc % 2
                if wt_free[jj] > 0:
                    p.wait("sync", s_scv, wt_free[jj])
                p.emit("sync", lambda s, jj=jj, dc=dc: s.dma_start(
                    out=wtb[jj][:], in_=g.stg_Wt[:, dc * 256:(dc + 1) * 256]),
                    inc=s_wld[1], incn=16)
                p.wait("vector", s_wld[1], s_wld[1].val)
                wtv = wtb[jj][:].rearrange("p (i s) -> p i s", s=16)
                for cg in range(2):
                    pg = gp[:].rearrange("p (c i s) -> p c i s", c=16, i=Q)
                    i0 = hsv[:, cg * 16:(cg + 1) * 16, dc, :].unsqueeze(2)\
                        .broadcast_to([128, 16, Q, 16])
                    i1 = wtv.unsqueeze(1).broadcast_to([128, 16, Q, 16])
                    p.emit("vector", lambda v, pg=pg, i0=i0, i1=i1: v.tensor_tensor(
                        out=pg, in0=i0, in1=i1, op=OP.mult))
                    for lev in (8, 4, 2):
                        p.emit("vector", lambda v, pg=pg, lev=lev: v.tensor_tensor(
                            out=pg[:, :, :, 0:lev], in0=pg[:, :, :, 0:lev],
                            in1=pg[:, :, :, lev:2 * lev], op=OP.add))
                    p.emit("vector", lambda v, pg=pg: v.tensor_tensor(
                        out=pg[:, :, :, 0:1], in0=pg[:, :, :, 0:1],
                        in1=pg[:, :, :, 1:2], op=OP.add))
                    yslice = yav[:, dc].rearrange("p (c i) -> p c i", i=Q)
                    p.emit("vector", lambda v, pg=pg, ys=yslice, cg=cg: v.tensor_tensor(
                        out=ys[:, cg * 16:(cg + 1) * 16].unsqueeze(3),
                        in0=ys[:, cg * 16:(cg + 1) * 16].unsqueeze(3),
                        in1=pg[:, :, :, 0:1], op=OP.add))
                p.emit("vector", lambda v: v.drain(), inc=s_scv)
                wt_free[jj] = s_scv.val
            p.emit("vector", lambda v: v.tensor_copy(
                out=h_st[:, 0:256], in_=h_st[:, NCH * 256:(NCH + 1) * 256]))
            p.emit("vector", lambda v: v.drain(), inc=s_scy)
            y_ready = s_scy.val

            for dout in range(DC):
                jj = dout % 2
                if wst_free[jj] > 0:
                    p.wait("sync", s_pmm, wst_free[jj])
                for ci in range(16):
                    p.emit("sync", lambda s, jj=jj, ci=ci, dout=dout: s.dma_start(
                        out=wstat[jj][:, ci * 128:(ci + 1) * 128],
                        in_=g.stg_wout[ci][:, dout * 128:(dout + 1) * 128]),
                        inc=s_wld[0], incn=16)
                wldv = s_wld[0].val
                b = dout % 2
                if bank_free[b] > 0:
                    p.wait("tensor", s_pev[b], bank_free[b])
                p.wait("tensor", s_wld[0], wldv)
                p.wait("tensor", s_scy, y_ready)
                for ci in range(16):
                    p.emit("tensor", lambda t, ci=ci, b=b, jj=jj: t.matmul(
                        out=ps_w[b][:], lhsT=wstat[jj][:, ci * 128:(ci + 1) * 128],
                        rhs=y_all[:, ci * 512:(ci + 1) * 512],
                        start=(ci == 0), stop=(ci == 15)),
                        inc=(s_pmm if ci == 15 else None))
                wst_free[jj] = s_pmm.val
                mmv = s_pmm.val
                p.wait("vector", s_pmm, mmv)
                p.emit("vector", lambda v, b=b, dout=dout, lo=lo: v.scalar_tensor_tensor(
                    out=otmp[:], in0=ps_w[b][:], scalar=g.bias_bout[:, dout:dout + 1],
                    in1=g.xln[dout][:, lo:lo + 512], op0=OP.add, op1=OP.add))
                p.emit("vector", lambda v, dout=dout, lo=lo: v.tensor_tensor(
                    out=g.stream[dout][:, lo:lo + 512],
                    in0=g.stream[dout][:, lo:lo + 512], in1=otmp[:], op=OP.add),
                    inc=s_pev[b])
                bank_free[b] = s_pev[b].val
        p.barrier()


def build_moe(g, seq):
    nc, p, e = g.nc, g.p, g.e
    NTT = 4
    with ExitStack() as ctx:
        ee = ctx.enter_context
        logits = ee(nc.sbuf_tensor(f"m{seq}lg", [8, T], F32))
        cfT = ee(nc.sbuf_tensor(f"m{seq}cf", [9, T], BF16))
        cftok = ee(nc.sbuf_tensor(f"m{seq}ct", [128, 8], F32))
        ttmp = ee(nc.sbuf_tensor(f"m{seq}tt", [128, 8], F32))
        m1 = ee(nc.sbuf_tensor(f"m{seq}m1", [128, 1], F32))
        m2 = ee(nc.sbuf_tensor(f"m{seq}m2", [128, 1], F32))
        cfb = [ee(nc.sbuf_tensor(f"m{seq}cb{j}", [128, 512], BF16)) for j in range(2)]
        hsil = [ee(nc.sbuf_tensor(f"m{seq}hs{j}", [128, 512], BF16)) for j in range(2)]
        h_all = [ee(nc.sbuf_tensor(f"m{seq}h{i}", [128, 512], BF16)) for i in range(36)]
        wst = [ee(nc.sbuf_tensor(f"m{seq}ws{j}", [128, 1024], BF16)) for j in range(2)]
        wdd = [ee(nc.sbuf_tensor(f"m{seq}wd{j}", [128, 36 * 128], BF16)) for j in range(2)]
        ps_lg = ee(nc.psum_tensor(f"m{seq}plg", [8, 512], F32))
        ps_tr = ee(nc.psum_tensor(f"m{seq}ptr", [128, 256], F32))
        ps_cfb = ee(nc.psum_tensor(f"m{seq}pcb", [128, 512], F32))
        ps_up = [ee(nc.psum_tensor(f"m{seq}pu{j}", [128, 512], F32)) for j in range(2)]
        ps_dn = [ee(nc.psum_tensor(f"m{seq}pd{i}", [128, 512], F32)) for i in range(4)]

        s_pmm = p.S("pmm")
        s_pev = [p.S("pev0"), p.S("pev1")]
        s_wld = [p.S("wld0"), p.S("wld1")]
        s_lg = p.S("qk")
        s_lge = p.S("exp")
        s_act = p.S("msk")
        s_tr = p.S("tr")
        s_cf = p.S("rz")
        s_cfT = p.S("rzb")
        s_bcm = p.S("bcm")
        s_cfbe = p.S("msz")
        s_sil = p.S("sil")
        s_hm = p.S("hm")
        s_dn = p.S("dn")
        s_dnev = p.S("dnev")

        b_lg = s_lg.val; b_lge = s_lge.val; b_act = s_act.val; b_tr = s_tr.val
        b_cf = s_cf.val; b_cfT = s_cfT.val
        for tile in range(NTT):
            lo = tile * 512
            if tile > 0:
                p.wait("tensor", s_lge, b_lge + tile)
            for c in range(DC):
                p.emit("tensor", lambda t, c=c, lo=lo: t.matmul(
                    out=ps_lg[0:8, :], lhsT=g.rw_sb[:, c * 8:(c + 1) * 8],
                    rhs=g.xln[c][:, lo:lo + 512],
                    start=(c == 0), stop=(c == DC - 1)),
                    inc=(s_lg if c == DC - 1 else None))
            p.wait("scalar", s_lg, b_lg + tile + 1)
            p.emit("scalar", lambda sc, lo=lo: sc.activation(
                out=logits[:, lo:lo + 512], in_=ps_lg[0:8, :], func=AF.Identity,
                bias=g.ebias_sb[0:8, 0:1], scale=1.0), inc=s_lge)
        p.wait("scalar", s_lge, b_lge + NTT)
        p.emit("scalar", lambda sc: sc.activation(
            out=logits[:], in_=logits[:], func=AF.Softplus))
        p.emit("scalar", lambda sc: sc.activation(
            out=logits[:], in_=logits[:], func=AF.Sqrt), inc=s_act)
        p.emit("vector", lambda v: v.memset(cfT[8:9, :], 1.0))
        p.wait("tensor", s_act, b_act + 1)
        for i in range(TT):
            p.emit("tensor", lambda t, i=i: t.transpose(
                out=ps_tr[:, 0:8], in_=logits[0:8, i * 128:(i + 1) * 128],
                identity=g.idf[:]), inc=s_tr)
            p.wait("vector", s_tr, b_tr + 2 * i + 1)
            p.emit("vector", lambda v: v.tensor_copy(out=cftok[:], in_=ps_tr[:, 0:8]))
            p.emit("vector", lambda v: v.tensor_reduce(
                out=m1[:], in_=cftok[:], axis=AX.X, op=OP.max))
            p.emit("vector", lambda v: v.tensor_scalar(
                out=ttmp[:], in0=cftok[:], scalar1=m1[:, 0:1], scalar2=None,
                op0=OP.is_equal))
            p.emit("vector", lambda v: v.scalar_tensor_tensor(
                out=ttmp[:], in0=ttmp[:], scalar=-1e30, in1=cftok[:],
                op0=OP.mult, op1=OP.add))
            p.emit("vector", lambda v: v.tensor_reduce(
                out=m2[:], in_=ttmp[:], axis=AX.X, op=OP.max))
            p.emit("vector", lambda v: v.tensor_scalar(
                out=ttmp[:], in0=cftok[:], scalar1=m2[:, 0:1], scalar2=None,
                op0=OP.is_ge))
            p.emit("vector", lambda v: v.tensor_tensor(
                out=cftok[:], in0=cftok[:], in1=ttmp[:], op=OP.mult),
                inc=s_cf)
            p.wait("tensor", s_cf, b_cf + i + 1)
            if i > 0:
                p.wait("tensor", s_cfT, b_cfT + i)
            p.emit("tensor", lambda t, i=i: t.transpose(
                out=ps_tr[0:8, 128:256], in_=cftok[:], identity=g.idf[:]),
                inc=s_tr)
            p.wait("vector", s_tr, b_tr + 2 * i + 2)
            p.emit("vector", lambda v, i=i: v.tensor_copy(
                out=cfT[0:8, i * 128:(i + 1) * 128], in_=ps_tr[0:8, 128:256]),
                inc=s_cfT)
        p.barrier()

        wst_free = [0, 0]
        wdd_free = [0, 0]
        upb_free = [0, 0]
        dnb_free = [0, 0, 0, 0]
        b_bcm = s_bcm.val
        b_cfbe = s_cfbe.val
        nbc = [0]
        hm_rec = []
        sil_rec = []
        nop = [0]

        for tile in range(NTT):
            lo = tile * 512
            for ei in range(9):
                if ei < 8:
                    jb = nbc[0] % 2
                    if nbc[0] >= 2:
                        p.wait("tensor", s_cfbe, b_cfbe + nbc[0] - 1)
                    p.emit("tensor", lambda t, ei=ei, lo=lo: t.matmul(
                        out=ps_cfb[:], lhsT=g.orb[0:1, :],
                        rhs=cfT[ei:ei + 1, lo:lo + 512], start=True, stop=True),
                        inc=s_bcm)
                    bcv = s_bcm.val
                    p.wait("vector", s_bcm, bcv)
                    p.emit("vector", lambda v, jb=jb: v.tensor_copy(
                        out=cfb[jb][:], in_=ps_cfb[:]), inc=s_cfbe)
                    nbc[0] += 1
                for hc in range(4):
                    jj = nop[0] % 2
                    if wst_free[jj] > 0:
                        p.wait("sync", s_pmm, wst_free[jj])
                    for c in range(DC):
                        if ei < 8:
                            srcap = g.stg_rW1[ei * 8 + c]
                        else:
                            srcap = g.stg_sW1[c]
                        p.emit("sync", lambda s, jj=jj, c=c, hc=hc, srcap=srcap:
                               s.dma_start(
                                   out=wst[jj][:, c * 128:(c + 1) * 128],
                                   in_=srcap[:, hc * 128:(hc + 1) * 128]),
                               inc=s_wld[0], incn=16)
                    wldv = s_wld[0].val
                    bu = nop[0] % 2
                    if upb_free[bu] > 0:
                        p.wait("tensor", s_sil, upb_free[bu])
                    p.wait("tensor", s_wld[0], wldv)
                    for c in range(DC):
                        p.emit("tensor", lambda t, c=c, bu=bu, jj=jj, lo=lo: t.matmul(
                            out=ps_up[bu][:], lhsT=wst[jj][:, c * 128:(c + 1) * 128],
                            rhs=g.xln[c][:, lo:lo + 512],
                            start=(c == 0), stop=(c == DC - 1)),
                            inc=(s_pmm if c == DC - 1 else None))
                    wst_free[jj] = s_pmm.val
                    mmv = s_pmm.val
                    p.wait("scalar", s_pmm, mmv)
                    nsl = nop[0]
                    if nsl >= 2:
                        p.wait("scalar", s_hm, hm_rec[nsl - 2])
                    bias = (g.bias_rb1[:, ei * 4 + hc:ei * 4 + hc + 1] if ei < 8
                            else g.bias_sb1[:, hc:hc + 1])
                    js = nsl % 2
                    p.emit("scalar", lambda sc, bu=bu, js=js, bias=bias: sc.activation(
                        out=hsil[js][:], in_=ps_up[bu][:], func=AF.Silu,
                        bias=bias, scale=1.0), inc=s_sil)
                    upb_free[bu] = s_sil.val
                    silv = s_sil.val
                    hix = ei * 4 + hc
                    p.wait("vector", s_sil, silv)
                    if ei < 8:
                        jb_use = (nbc[0] - 1) % 2
                        p.wait("vector", s_cfbe, s_cfbe.val)
                        p.emit("vector", lambda v, hix=hix, js=js, jb_use=jb_use:
                               v.tensor_tensor(
                                   out=h_all[hix][:], in0=hsil[js][:],
                                   in1=cfb[jb_use][:], op=OP.mult),
                               inc=s_hm)
                    else:
                        p.emit("vector", lambda v, hix=hix, js=js: v.tensor_copy(
                            out=h_all[hix][:], in_=hsil[js][:]), inc=s_hm)
                    hm_rec.append(s_hm.val)
                    nop[0] += 1
            hm_all = s_hm.val
            p.wait("tensor", s_hm, hm_all)
            for dh in range(2):
                for dc4 in range(4):
                    dcg = dh * 4 + dc4
                    if dnb_free[dc4] > 0:
                        p.wait("tensor", s_dnev, dnb_free[dc4])
                    jj = dcg % 2
                    if wdd_free[jj] > 0:
                        p.wait("sync", s_dn, wdd_free[jj])
                    for ei in range(9):
                        for hc in range(4):
                            if ei < 8:
                                srcap = g.stg_rW2[ei * 4 + hc]
                            else:
                                srcap = g.stg_sW2[hc]
                            nst = ei * 4 + hc
                            p.emit("sync", lambda s, jj=jj, srcap=srcap, dcg=dcg,
                                   nst=nst: s.dma_start(
                                       out=wdd[jj][:, nst * 128:(nst + 1) * 128],
                                       in_=srcap[:, dcg * 128:(dcg + 1) * 128]),
                                   inc=s_wld[1], incn=16)
                    p.wait("tensor", s_wld[1], s_wld[1].val)
                    for nst in range(36):
                        p.emit("tensor", lambda t, jj=jj, dc4=dc4, nst=nst:
                               t.matmul(
                                   out=ps_dn[dc4][:],
                                   lhsT=wdd[jj][:, nst * 128:(nst + 1) * 128],
                                   rhs=h_all[nst][:],
                                   start=(nst == 0), stop=False),
                               inc=(s_dn if nst == 35 else None))
                    wdd_free[jj] = s_dn.val
                    p.emit("tensor", lambda t, dc4=dc4, dcg=dcg, lo=lo: t.matmul(
                        out=ps_dn[dc4][:], lhsT=g.rb2aug_sb[0:9, dcg * 128:(dcg + 1) * 128],
                        rhs=cfT[0:9, lo:lo + 512], start=False, stop=True),
                        inc=s_dn)
                    dnv = s_dn.val
                    p.wait("vector", s_dn, dnv)
                    p.emit("vector", lambda v, dc4=dc4, dcg=dcg, lo=lo: v.tensor_tensor(
                        out=g.stream[dcg][:, lo:lo + 512],
                        in0=g.stream[dcg][:, lo:lo + 512], in1=ps_dn[dc4][:],
                        op=OP.add), inc=s_dnev)
                    dnb_free[dc4] = s_dnev.val
        p.barrier()



def kernel(**inputs):
    xfull = np.ascontiguousarray(inputs["L0_attn_x"])

    def bias_t(v):
        return np.ascontiguousarray(
            np.asarray(v).reshape(DC, 128).T.astype(np.float32))

    consts = dict(
        ident_f=np.eye(128, dtype=np.float32),
        ident_b=np.eye(128).astype(ml_dtypes.bfloat16),
        ones_col_f=np.ones((128, 1), np.float32),
        ones_col_b=np.ones((128, 1), ml_dtypes.bfloat16),
        ones_row_f=np.ones((1, 128), np.float32),
        ones_row_b=np.ones((1, 128), ml_dtypes.bfloat16),
        maskL=np.tril(np.ones((128, 128))).astype(ml_dtypes.bfloat16),
        maskR=np.triu(np.ones((128, 128))).astype(ml_dtypes.bfloat16),
        wq=np.ascontiguousarray(np.asarray(inputs["L0_attn_wq"], np.float32)),
        wk=np.ascontiguousarray(np.asarray(inputs["L0_attn_wk"], np.float32)),
        wv=np.ascontiguousarray(np.asarray(inputs["L0_attn_wv"], np.float32)),
        wo=np.ascontiguousarray(np.asarray(inputs["L0_attn_wo"], np.float32)),
        bq_t=bias_t(inputs["L0_attn_bq"]),
        bk_t=bias_t(inputs["L0_attn_bk"]),
        bv_t=bias_t(inputs["L0_attn_bv"]),
        bo_t=bias_t(inputs["L0_attn_bo"]),
        bv_row=np.ascontiguousarray(
            np.asarray(inputs["L0_attn_bv"], np.float32).reshape(1, D)),
        wg_i=np.ascontiguousarray(np.asarray(inputs["L0_ffn_w_gate"], np.int32)),
        wu_i=np.ascontiguousarray(np.asarray(inputs["L0_ffn_w_up"], np.int32)),
        wd_i=np.ascontiguousarray(np.asarray(inputs["L0_ffn_w_down"], np.int32)),
        win_f=np.ascontiguousarray(np.asarray(inputs["L1_attn_Win"], np.float32)),
        wout_f=np.ascontiguousarray(np.asarray(inputs["L1_attn_Wout"], np.float32)),
        A_t=np.ascontiguousarray(
            np.asarray(inputs["L1_attn_A"], np.float32)
            .reshape(16, 128, 16, 16).transpose(1, 0, 2, 3).reshape(128, 4096)),
        Bm_t=np.ascontiguousarray(
            np.asarray(inputs["L1_attn_Bm"], np.float32)[:, :, 0]
            .reshape(16, 128, 16).transpose(1, 0, 2).reshape(128, 256)),
        Cm_t=np.ascontiguousarray(
            np.asarray(inputs["L1_attn_Cm"], np.float32)[:, 0, :]
            .reshape(16, 128, 16).transpose(1, 0, 2).reshape(128, 256)),
        c4_t=np.ascontiguousarray(
            np.asarray(inputs["L1_attn_conv_w"], np.float32)
            .reshape(2048, 4)[:, ::-1]
            .reshape(16, 128, 4).transpose(1, 0, 2).reshape(128, 64)),
        bin_t=np.ascontiguousarray(
            np.asarray(inputs["L1_attn_b_in"], np.float32).reshape(16, 128).T),
        cb_t=np.ascontiguousarray(
            np.asarray(inputs["L1_attn_conv_b"], np.float32).reshape(16, 128).T),
        bout_t=bias_t(inputs["L1_attn_b_out"]),
        nw_t=bias_t(inputs["L1_attn_norm_w"]),
        sW1_f=np.ascontiguousarray(np.asarray(inputs["L1_ffn_sW1"], np.float32)),
        sW2_f=np.ascontiguousarray(np.asarray(inputs["L1_ffn_sW2"], np.float32)),
        rW1_f=np.ascontiguousarray(
            np.asarray(inputs["L1_ffn_rW1"], np.float32).reshape(8 * D, 512)),
        rW2_f=np.ascontiguousarray(
            np.asarray(inputs["L1_ffn_rW2"], np.float32).reshape(8 * 512, D)),
        rw_t=np.ascontiguousarray(
            np.asarray(inputs["L1_ffn_routing_W"], np.float32).T
            .reshape(8, 128, 8).transpose(1, 0, 2).reshape(128, 64)),
        ebias_t=np.ascontiguousarray(
            np.asarray(inputs["L1_ffn_expert_bias"], np.float32).reshape(8, 1)),
        sb1_t=np.ascontiguousarray(
            np.asarray(inputs["L1_ffn_sb1"], np.float32).reshape(4, 128).T),
        rb1_t=np.ascontiguousarray(
            np.asarray(inputs["L1_ffn_rb1"], np.float32).reshape(8, 4, 128)
            .transpose(2, 0, 1).reshape(128, 32)),
        rb2aug=np.ascontiguousarray(np.concatenate(
            [np.asarray(inputs["L1_ffn_rb2"], np.float32),
             np.asarray(inputs["L1_ffn_sb2"], np.float32).reshape(1, D)], axis=0)),
    )

    nc = build_program()
    in_maps = []
    for core in range(8):
        m = dict(consts)
        m["x"] = np.ascontiguousarray(xfull[core * NSEQ:(core + 1) * NSEQ])
        in_maps.append(m)
    res = run_bass_kernel_spmd(nc, in_maps, core_ids=list(range(8)))
    global LAST_DEBUG
    LAST_DEBUG = [{k: np.asarray(v) for k, v in r.items() if k.startswith("dbg_")}
                  for r in res.results]
    outs = [np.asarray(r["out"]) for r in res.results]
    return np.concatenate(outs, axis=0).reshape(16, 2048, 1024)


LAST_DEBUG = None
